# version 33
# baseline (speedup 1.0000x reference)
import numpy as np
import concourse.bass as bass
import concourse.mybir as mybir
from concourse.bass_utils import run_bass_kernel_spmd
from contextlib import ExitStack

F32 = mybir.dt.float32
BF16 = mybir.dt.bfloat16
ALU = mybir.AluOpType
ACT = mybir.ActivationFunctionType
AX = mybir.AxisListType.X

MAGIC = 12582912.0
LN_GAMMA = -0.10536051565782628
INV2PI = 0.15915494309189535
C1_2PI = 6.28125
C2_2PI = 0.0019353071795864769
HALF_PI = 1.5707963267948966
LNF = 0.017988946039015984
QMAX = 32767.0
WQMAX = 7.0
EPS_LN = 1e-5

B, T, H, I = 2, 2048, 1024, 3072
NT = B * T

ENGINES = ("sync", "vector", "scalar", "tensor", "gpsimd")
DEBUG_TAPS = set()
PHASE_LIMIT = 7


class P:

    def __init__(self, nc, es):
        self.nc = nc
        self.es = es
        self.q = {e: [] for e in ENGINES}
        self.cnt = {}
        self._sems = {}
        self.guards = {}
        self.dma_streams = {}

    def sem(self, name):
        if name not in self._sems:
            self._sems[name] = self.es.enter_context(self.nc.semaphore(name))
        return self._sems[name]

    def inc(self, name, n=1):
        self.cnt[name] = self.cnt.get(name, 0) + n
        return self.cnt[name]

    def val(self, name):
        return self.cnt.get(name, 0)

    def emit(self, eng, fn, kind="op"):
        self.q[eng].append((fn, kind))

    def wait(self, eng, sem_name, value):
        if value <= 0:
            return
        s = self.sem(sem_name)
        self.emit(eng, lambda e, s=s, v=value: e.wait_ge(s, v), kind="wait")

    def sig(self, eng, sem_name):
        s = self.sem(sem_name)
        v = self.inc(sem_name)
        q = self.q[eng]
        i = len(q) - 1
        while i >= 0 and q[i][1] == "wait":
            i -= 1
        if i >= 0 and q[i][1] == "op":
            fn = q[i][0]
            q[i] = (lambda e, fn=fn, s=s: fn(e).then_inc(s, 1), "pre")
            return v
        if eng == "vector":
            self.emit(eng, lambda e: e.drain(), kind="drain")
        self.emit(eng, lambda e, s=s: e.nop().then_inc(s, 1), kind="pre")
        return v

    def sig_nop(self, eng, sem_name):
        s = self.sem(sem_name)
        v = self.inc(sem_name)
        if eng == "vector":
            self.emit(eng, lambda e: e.drain(), kind="drain")
        self.emit(eng, lambda e, s=s: e.nop().then_inc(s, 1), kind="pre")
        return v

    def dma(self, eng, out, in_, sem_name):
        s = self.sem(sem_name)
        v = self.inc(sem_name, 16)
        self.dma_streams[sem_name] = eng
        self.emit(eng, lambda e, o=out, i=in_, s=s: e.dma_start(out=o, in_=i).then_inc(s, 16),
                  kind="dma")
        return v

    def flush_sig(self, eng):
        return self.sig(eng, "fl_" + eng)

    def dma_finalize(self):
        for name, eng in sorted(self.dma_streams.items()):
            self.wait(eng, name, self.val(name))

    def gwait(self, eng, key):
        g = self.guards.get(key)
        if g is not None:
            self.wait(eng, g[0], g[1])

    def gset(self, key, sem_name, value):
        self.guards[key] = (sem_name, value)

    def barrier(self, engines=ENGINES, skip=()):
        for name, eng in sorted(self.dma_streams.items()):
            if name in skip:
                continue
            self.wait(eng, name, self.val(name))
        s = self.sem("bar")
        tgt = self.inc("bar", len(engines))
        for eng in engines:
            if eng == "vector":
                self.emit(eng, lambda e: e.drain(), kind="drain")
            self.emit(eng, lambda e, s=s: e.nop().then_inc(s, 1), kind="pre")
            self.emit(eng, lambda e, t=tgt, s=s: e.wait_ge(s, t), kind="wait")

    def V(self, fn, drain=False):
        self.emit("vector", fn)
        if drain:
            self.emit("vector", lambda e: e.drain(), kind="drain")

    def S(self, fn, drain=False):
        self.emit("scalar", fn)
        if drain:
            self.emit("scalar", lambda e: e.drain(), kind="drain")

    def TE(self, fn):
        self.emit("tensor", fn)

    def G(self, fn):
        self.emit("gpsimd", fn)


def build_nc():
    nc = bass.Bass("TRN2", target_bir_lowering=False)

    d = {}
    d["x_in"] = nc.dram_tensor("x", [B, T, H], F32, kind="ExternalInput")
    d["wq_in"] = nc.dram_tensor("wq", [H, H], F32, kind="ExternalInput")
    d["wk_in"] = nc.dram_tensor("wk", [H, H], F32, kind="ExternalInput")
    d["wv_in"] = nc.dram_tensor("wv", [H, H], F32, kind="ExternalInput")
    d["wk16"] = nc.dram_tensor("wk16", [H, H], BF16)
    d["wu_in"] = nc.dram_tensor("wu", [I, H], F32, kind="ExternalInput")
    d["wg_in"] = nc.dram_tensor("wg", [I, H], F32, kind="ExternalInput")
    d["wd_in"] = nc.dram_tensor("wd", [H, I], F32, kind="ExternalInput")
    d["ident_in"] = nc.dram_tensor("ident", [128, 128], BF16, kind="ExternalInput")
    d["out_t"] = nc.dram_tensor("out", [B, T, H], F32, kind="ExternalOutput")

    d["tabs"] = nc.dram_tensor("tabs", [4, 512, T], BF16)
    d["attnT_d"] = nc.dram_tensor("attnT_d", [B, H, T], BF16)
    d["sums_d"] = nc.dram_tensor("sums_d", [B, T], F32)
    d["sp_up"] = nc.dram_tensor("sp_up", [I, NT], BF16)
    d["sp_gate"] = nc.dram_tensor("sp_gate", [I, NT], BF16)
    d["sp_g2"] = nc.dram_tensor("sp_g2", [I, NT], BF16)
    d["sp_m"] = nc.dram_tensor("sp_m", [I, NT], BF16)
    d["cc_in"] = [nc.dram_tensor(f"cc_in{i}", [1, 8], F32) for i in range(4)]
    d["cc_out"] = [nc.dram_tensor(f"cc_out{i}", [1, 8], F32) for i in range(4)]
    d["misc_d"] = nc.dram_tensor("misc_d", [8, 256], F32)

    d["taps"] = {}
    for name, shape, dt in [
        ("tap_xnT", [128, 8 * NT], BF16),
        ("tap_xqT", [128, 8 * NT], BF16),
        ("tap_scales", [1, 64], F32),
        ("tap_dtile", [128, 512], BF16),
        ("tap_tabs", [4 * 512, T], BF16),
        ("tap_kT", [128, 8 * T], BF16),
        ("tap_vtok", [128, 16 * H], BF16),
        ("tap_qT", [128, 8 * 512], BF16),
        ("tap_sums", [B, T], F32),
        ("tap_wuT", [128, 8 * I], BF16),
        ("tap_wdT", [128, 24 * H], BF16),
    ]:
        if name in DEBUG_TAPS:
            d["taps"][name] = nc.dram_tensor(name, shape, dt, kind="ExternalOutput")

    with ExitStack() as es0:
        p = P(nc, es0)
        build_program(p, d)
        with nc.Block() as block:
            @block.sync
            def _(e):
                for fn, _k in p.q["sync"]:
                    fn(e)

            @block.vector
            def _(e):
                for fn, _k in p.q["vector"]:
                    fn(e)

            @block.scalar
            def _(e):
                for fn, _k in p.q["scalar"]:
                    fn(e)

            @block.tensor
            def _(e):
                for fn, _k in p.q["tensor"]:
                    fn(e)

            @block.gpsimd
            def _(e):
                for fn, _k in p.q["gpsimd"]:
                    fn(e)
    return nc


def build_program(p, d):
    nc = p.nc
    stP, stXN, stA, stB, stC, stD, stDw, stD2 = (ExitStack() for _ in range(8))

    def sb(st, name, shape, dt, side=None):
        return st.enter_context(nc.sbuf_tensor(name, shape, dt, side=side))

    def psal(st, name, shape, dt=F32):
        return st.enter_context(nc.psum_tensor(name, shape, dt))

    for s in ("tt", "vv", "sc", "gg", "cc"):
        p.sem(s)

    ident = sb(stP, "identity", [128, 128], BF16)
    colbuf = sb(stP, "colbuf", [128, 16], F32)
    rowbuf = sb(stP, "rowbuf", [1, 392], F32)
    mxb = sb(stP, "mxb", [128, 12], F32)
    onesf = sb(stP, "onesf", [1, 128], F32)
    onesb = sb(stP, "onesb", [128, 1], BF16)
    pcol = psal(stP, "pcol", [128, 1], F32)

    idv = p.dma("sync", ident[:], d["ident_in"].ap(), "ld_i")
    cast_v = p.dma("gpsimd", d["wk16"].ap(), d["wk_in"].ap(), "gl_x")
    p.G(lambda e: e.memset(onesf[0:1, :], 1.0))
    p.G(lambda e: e.memset(onesb[:, 0:1], 1.0))
    p.G(lambda e: e.memset(rowbuf[0:1, :], 0.0))
    p.G(lambda e: e.memset(colbuf[:, 12:13], MAGIC))
    gset = p.sig("gpsimd", "gg")

    def bcast_col(src_ap, dst_col):
        p.gwait("tensor", "pcol")
        tv = p.inc("tt")
        p.emit("tensor", lambda e, s=src_ap: e.matmul(pcol[:, 0:1], onesf[0:1, :], s,
                                          start=True, stop=True).then_inc(p.sem("tt"), 1),
               kind="pre")
        p.wait("scalar", "tt", tv)
        p.S(lambda e, dc=dst_col: e.copy(dc, pcol[:, 0:1]))
        sv = p.sig("scalar", "sc")
        p.gset("pcol", "sc", sv)
        return sv

    p.dma_streams["gl_c"] = "gpsimd"
    p.dma_streams["gl_x"] = "gpsimd"
    p.dma_streams["gl_p"] = "gpsimd"
    p.dma_streams["gl_v"] = "gpsimd"

    def collective_max(idx, src_ap, n_vals):
        p.wait("gpsimd", "gg", p.val("gg"))
        p.dma("gpsimd", d["cc_in"][idx].ap()[0:1, n_vals:8],
              rowbuf[0:1, 144:144 + (8 - n_vals)], "gl_c")
        v = p.inc("gl_c", 16)
        p.emit("gpsimd", lambda e, idx=idx, s=src_ap, n=n_vals: e.dma_start(
            out=d["cc_in"][idx].ap()[0:1, 0:n], in_=s).then_inc(p.sem("gl_c"), 16), kind="dma")
        p.wait("gpsimd", "gl_c", v)
        ccv = p.inc("cc")
        p.emit("gpsimd", lambda e, idx=idx: e.collective_compute(
            "AllReduce", ALU.max, replica_groups=[list(range(8))],
            ins=[d["cc_in"][idx].ap()], outs=[d["cc_out"][idx].ap()]).then_inc(p.sem("cc"), 1),
            kind="dma")
        p.wait("gpsimd", "cc", ccv)
        rv = p.inc("gl_c", 16)
        self_sem = p.sem("gl_c")
        p.emit("gpsimd", lambda e, idx=idx, s=self_sem: e.dma_start(
            out=rowbuf[0:1, 160 + 8 * idx:168 + 8 * idx],
            in_=d["cc_out"][idx].ap()).then_inc(s, 16), kind="dma")
        return rv

    def pcollapse(cols_ap, n, dst_off):
        gv = p.inc("gl_p", 16)
        sp_ = p.sem("gl_p")
        p.emit("gpsimd", lambda e, c=cols_ap, s=sp_: e.dma_start(
            out=d["misc_d"].ap()[0:1, 0:128 * n].rearrange("a (p c) -> (a p) c", c=n),
            in_=c).then_inc(s, 16), kind="dma")
        p.wait("gpsimd", "gl_p", gv)
        gv2 = p.inc("gl_p", 16)
        p.emit("gpsimd", lambda e, n=n, s=sp_: e.dma_start(
            out=rowbuf[0:1, 256:256 + 128 * n],
            in_=d["misc_d"].ap()[0:1, 0:128 * n]).then_inc(s, 16), kind="dma")
        p.wait("vector", "gl_p", gv2)
        for j in range(n):
            p.V(lambda e, j=j, n=n, o=dst_off: e.tensor_reduce(
                rowbuf[0:1, o + j:o + j + 1],
                rowbuf[0:1, 256:256 + 128 * n].rearrange("a (c n) -> a c n", n=n)[:, :, j],
                AX, ALU.max), drain=True)
        return p.sig("vector", "vv")

    xnT = sb(stXN, "xnT", [128, 8 * NT], BF16, side="right")

    xbuf = sb(stA, "xbuf", [128, 4 * H], F32)
    xnml = sb(stA, "xnml", [128, 4 * H], BF16)
    lncols_ = sb(stA, "lncols_", [128, 32], F32)
    bn6_ = sb(stA, "bn6_", [128, 48], F32)
    ptr = psal(stA, "ptr", [128, 4 * 1024], BF16)

    tabio = sb(stA, "tabio", [128, T], F32)
    tabc1 = sb(stA, "tabc1", [128, 8], F32)
    tabth = sb(stA, "tabth", [128, T], F32)
    tabk = sb(stA, "tabk", [128, T], F32)
    tabt1 = sb(stA, "tabt1", [128, T], F32)
    tabsin = sb(stA, "tabsin", [128, T], F32)
    tabcos = sb(stA, "tabcos", [128, T], F32)
    tabs1 = sb(stA, "tabs1", [128, T], F32)
    tabs2 = sb(stA, "tabs2", [128, T], F32)
    tabo = sb(stA, "tabo", [128, T], BF16)

    def emit_tab(ic):
        i0 = ic * 128
        p.gwait("gpsimd", "tabgen_v")
        p.gwait("gpsimd", "tabgen_s")
        p.G(lambda e: e.iota(tabio[:], [[1, T]], channel_multiplier=0,
                             allow_small_or_imprecise_dtypes=True))
        p.G(lambda e, i0=i0: e.iota(tabc1[:, 0:1], [[1, 1]], base=i0, channel_multiplier=1,
                                    allow_small_or_imprecise_dtypes=True))
        gi = p.sig("gpsimd", "gg")
        p.wait("scalar", "gg", gi)
        p.S(lambda e: e.activation(tabc1[:, 1:2], tabc1[:, 0:1], ACT.Exp, scale=-LNF))
        ssc = p.sig("scalar", "sc")
        p.wait("vector", "sc", ssc)
        p.wait("vector", "gg", gi)
        p.V(lambda e: e.tensor_scalar(tabc1[:, 4:5], tabc1[:, 0:1], 2.0 / 1433.6,
                                      409.6 / 1433.6, ALU.mult, ALU.add), drain=True)
        v0l = p.sig("vector", "vv")
        p.wait("scalar", "vv", v0l)
        p.S(lambda e: e.activation(tabc1[:, 2:3], tabc1[:, 4:5], ACT.Ln))
        s0l = p.sig("scalar", "sc")
        p.wait("vector", "sc", s0l)
        p.V(lambda e: e.tensor_scalar(tabc1[:, 3:4], tabc1[:, 2:3], 1.0 / 512, None, ALU.mult))
        p.V(lambda e: e.tensor_scalar(tabth[:], tabio[:], tabc1[:, 1:2], None, ALU.mult), drain=True)
        p.V(lambda e: e.tensor_scalar(tabk[:], tabth[:], INV2PI, MAGIC, ALU.mult, ALU.add), drain=True)
        p.V(lambda e: e.tensor_scalar(tabk[:], tabk[:], MAGIC, None, ALU.subtract), drain=True)
        p.V(lambda e: e.scalar_tensor_tensor(tabt1[:], tabk[:], -C1_2PI, tabth[:], ALU.mult, ALU.add), drain=True)
        p.V(lambda e: e.scalar_tensor_tensor(tabt1[:], tabk[:], -C2_2PI, tabt1[:], ALU.mult, ALU.add), drain=True)
        p.V(lambda e: e.tensor_scalar(tabt1[:], tabt1[:], 3.141592, -3.141592, ALU.min, ALU.max))
        v1 = p.sig("vector", "vv")
        p.wait("scalar", "vv", v1)
        p.S(lambda e: e.activation(tabsin[:], tabt1[:], ACT.Sin))
        s1 = p.sig("scalar", "sc")
        p.wait("vector", "sc", s1)
        p.V(lambda e: e.tensor_scalar(tabk[:], tabth[:], INV2PI, 0.25, ALU.mult, ALU.add), drain=True)
        p.V(lambda e: e.tensor_scalar(tabk[:], tabk[:], MAGIC, MAGIC, ALU.add, ALU.subtract), drain=True)
        p.V(lambda e: e.scalar_tensor_tensor(tabt1[:], tabk[:], -C1_2PI, tabth[:], ALU.mult, ALU.add), drain=True)
        p.V(lambda e: e.scalar_tensor_tensor(tabt1[:], tabk[:], -C2_2PI, tabt1[:], ALU.mult, ALU.add), drain=True)
        p.V(lambda e: e.tensor_scalar(tabt1[:], tabt1[:], HALF_PI, None, ALU.add), drain=True)
        p.V(lambda e: e.tensor_scalar(tabt1[:], tabt1[:], 3.141592, -3.141592, ALU.min, ALU.max))
        p.V(lambda e: e.tensor_scalar(tabs1[:], tabio[:], tabc1[:, 3:4], None, ALU.mult))
        v2 = p.sig("vector", "vv")
        p.wait("scalar", "vv", v2)
        p.S(lambda e: e.activation(tabcos[:], tabt1[:], ACT.Sin))
        p.S(lambda e: e.activation(tabs2[:], tabs1[:], ACT.Exp, scale=-1.0), drain=True)
        p.S(lambda e: e.activation(tabs1[:], tabs1[:], ACT.Exp))
        s2 = p.sig("scalar", "sc")
        p.gset("tabgen_s", "sc", s2)
        p.wait("vector", "sc", s2)
        for j, (trig, scl) in enumerate([(tabcos, tabs1), (tabsin, tabs1),
                                         (tabcos, tabs2), (tabsin, tabs2)]):
            p.V(lambda e, a=trig, bb=scl: e.tensor_tensor(tabo[:], a[:], bb[:], ALU.mult))
            vj = p.sig("vector", "vv")
            p.gset("tabgen_v", "vv", vj)
            p.wait("sync", "vv", vj)
            lv = p.dma("sync", d["tabs"].ap()[j, i0:i0 + 128, :], tabo[:], "ld_tb")
            p.wait("vector", "ld_tb", lv)


    p.wait("tensor", "ld_i", idv)
    for i in range(NT // 128):
        if i > 0 and i % 8 == 0:
            emit_tab(i // 8 - 1)
        b, tr = divmod(i, T // 128)
        bf = i % 4
        xt = xbuf[:, bf * H:(bf + 1) * H]
        xn = xnml[:, bf * H:(bf + 1) * H]
        lncols = lncols_[:, bf * 8:(bf + 1) * 8]
        bn6 = bn6_[:, bf * 12:(bf + 1) * 12]
        p.gwait("sync", "xbuf%d" % bf)
        ldv = p.dma("sync", xt, d["x_in"].ap()[b, tr * 128:(tr + 1) * 128, :], "ld_x%d" % bf)
        p.wait("vector", "ld_x%d" % bf, ldv)
        p.V(lambda e, xt=xt, bn6=bn6: e.bn_stats(bn6[:, 0:6], xt[:, 0:512]))
        p.V(lambda e, xt=xt, bn6=bn6: e.bn_stats(bn6[:, 6:12], xt[:, 512:1024]), drain=True)
        p.V(lambda e, lncols=lncols, bn6=bn6: e.bn_aggr(lncols[:, 0:2], bn6[:, 0:12]), drain=True)
        p.V(lambda e, lncols=lncols: e.tensor_scalar(lncols[:, 1:2], lncols[:, 1:2], EPS_LN, None, ALU.add))
        v1 = p.sig("vector", "vv")
        p.wait("scalar", "vv", v1)
        p.S(lambda e, lncols=lncols: e.activation(lncols[:, 2:3], lncols[:, 1:2], ACT.Sqrt))
        s1 = p.sig("scalar", "sc")
        p.wait("vector", "sc", s1)
        p.V(lambda e, lncols=lncols: e.reciprocal(lncols[:, 3:4], lncols[:, 2:3]), drain=True)
        p.V(lambda e, lncols=lncols: e.scalar_tensor_tensor(lncols[:, 4:5], lncols[:, 0:1], -1.0,
                                                            lncols[:, 3:4], ALU.mult, ALU.mult))
        v2 = p.sig("vector", "vv")
        p.wait("scalar", "vv", v2)
        p.gwait("scalar", "xnml%d" % bf)
        p.S(lambda e, xt=xt, xn=xn, lncols=lncols: e.activation(xn, xt, ACT.Identity,
                                                 bias=lncols[:, 4:5], scale=lncols[:, 3:4]))
        sv = p.sig("scalar", "sc")
        p.gset("xbuf%d" % bf, "sc", sv)
        p.wait("tensor", "sc", sv)
        p.gwait("tensor", "ptr%d" % bf)
        for k in range(8):
            p.TE(lambda e, xn=xn, k=k, bf=bf: e.transpose(
                ptr[:, bf * 1024 + k * 128: bf * 1024 + (k + 1) * 128],
                xn[:, k * 128:(k + 1) * 128], ident[:]))
        tv = p.sig("tensor", "tt")
        p.gset("xnml%d" % bf, "tt", tv)
        p.wait("scalar", "tt", tv)
        for k in range(8):
            p.S(lambda e, k=k, bf=bf, i=i: e.copy(
                xnT[:, k * NT + i * 128: k * NT + (i + 1) * 128],
                ptr[:, bf * 1024 + k * 128: bf * 1024 + (k + 1) * 128]))
        ev = p.sig("scalar", "sc")
        p.gset("ptr%d" % bf, "sc", ev)

    emit_tab(3)
    if "tap_tabs" in DEBUG_TAPS:
        p.dma("sync", d["taps"]["tap_tabs"].ap(),
              d["tabs"].ap().rearrange("a b c -> (a b) c"), "ld_tp")
    if "tap_xnT" in DEBUG_TAPS:
        p.dma("sync", d["taps"]["tap_xnT"].ap(), xnT[:], "ld_tp")

    xnT_sc = p.val("sc")
    p.barrier()
    stA.close()
    if PHASE_LIMIT < 2:
        return

    wqkv = sb(stB, "wqkv", [128, 8 * H], BF16)
    kT = sb(stB, "kT", [128, 8 * T], BF16)
    vtok = sb(stB, "vtok", [128, 16 * H], BF16)
    qT = sb(stB, "qT", [128, 8 * 512], BF16)
    expb = sb(stB, "expb", [128, 16 * 512], BF16)
    ptilb = sb(stB, "ptilb", [128, 16 * 512], BF16)
    tb0 = sb(stB, "tb0", [128, 2 * 512], BF16)
    tb1 = sb(stB, "tb1", [128, 2 * 512], BF16)
    evq = sb(stB, "evq", [128, 2 * 512], BF16)
    rotbuf = sb(stB, "rotbuf", [128, 2 * 512], F32)
    atst = sb(stB, "atst", [128, 512], BF16)
    sumsb = sb(stB, "sumsb", [1, 512], F32, side="right")
    d2s = sb(stB, "d2s", [128, 3968], BF16)
    pmm = psal(stB, "pmm", [128, 2 * 1024], F32)
    ppv = psal(stB, "ppv", [128, 2 * 512], F32)
    psums = psal(stB, "psums", [1, 512], F32)

    p.G(lambda e: e.iota(d2s[:], [[1, 3968]], base=-1920, channel_multiplier=-1,
                         allow_small_or_imprecise_dtypes=True))
    gd2 = p.sig("gpsimd", "gg")
    p.wait("scalar", "gg", gd2)
    p.S(lambda e: e.activation(d2s[:], d2s[:], ACT.Abs), drain=True)
    p.S(lambda e: e.activation(d2s[:], d2s[:], ACT.Exp, scale=LN_GAMMA))
    s_d2 = p.sig("scalar", "sc")
    p.wait("vector", "sc", s_d2)

    def load_w(name, wait_val=None):
        p.wait("gpsimd", "tt", p.val("tt") if wait_val is None else wait_val)
        v = None
        sw = p.sem("gl_w")
        p.dma_streams["gl_w"] = "gpsimd"
        for hc in range(8):
            v = p.inc("gl_w", 16)
            p.emit("gpsimd", lambda e, hc=hc, name=name, s=sw: e.dma_start(
                out=wqkv[:, hc * H:(hc + 1) * H],
                in_=d[name].ap()[hc * 128:(hc + 1) * 128, :]).then_inc(s, 16), kind="dma")
        return v

    def xpos_pair(b, tab_idx, src_tok0, n512, dst, dst_chunk_w, slot_tag):
        for c in range(4):
            for tcb in range(n512):
                sl = (c * n512 + tcb) % 2
                tok_off = src_tok0 + tcb * 512
                p.gwait("sync", "tb%d" % sl)
                lv1 = p.dma("sync", tb0[:, sl * 512:(sl + 1) * 512],
                            d["tabs"].ap()[tab_idx, c * 128:(c + 1) * 128,
                                           (tok_off - b * T):(tok_off - b * T) + 512], "ld_t%d" % sl)
                lv2 = p.dma("sync", tb1[:, sl * 512:(sl + 1) * 512],
                            d["tabs"].ap()[tab_idx + 1, c * 128:(c + 1) * 128,
                                           (tok_off - b * T):(tok_off - b * T) + 512], "ld_t%d" % sl)
                p.gwait("tensor", "pmm%d" % sl)
                for half, cc_ in enumerate((c, c + 4)):
                    for hc in range(8):
                        p.TE(lambda e, cc_=cc_, hc=hc, half=half, sl=sl, tok_off=tok_off: e.matmul(
                            pmm[:, sl * 1024 + half * 512: sl * 1024 + (half + 1) * 512],
                            wqkv[:, hc * H + cc_ * 128: hc * H + (cc_ + 1) * 128],
                            xnT[:, hc * NT + tok_off: hc * NT + tok_off + 512],
                            start=(hc == 0), stop=(hc == 7)))
                tv = p.sig("tensor", "tt")
                p.wait("scalar", "tt", tv)
                p.gwait("scalar", "evq")
                for half in range(2):
                    p.S(lambda e, half=half, sl=sl: e.copy(
                        evq[:, half * 512:(half + 1) * 512],
                        pmm[:, sl * 1024 + half * 512: sl * 1024 + (half + 1) * 512]))
                sv = p.sig("scalar", "sc")
                p.gset("pmm%d" % sl, "sc", sv)
                p.wait("vector", "sc", sv)
                p.wait("vector", "ld_t%d" % sl, lv2)
                r0, r1 = rotbuf[:, 0:512], rotbuf[:, 512:1024]
                Qe, Qo = evq[:, 0:512], evq[:, 512:1024]
                cs = tb0[:, sl * 512:(sl + 1) * 512]
                ss = tb1[:, sl * 512:(sl + 1) * 512]
                o_e = c * dst_chunk_w + tcb * 512
                o_o = (c + 4) * dst_chunk_w + tcb * 512
                p.V(lambda e, cs=cs: e.tensor_tensor(r0, Qe, cs, ALU.mult))
                p.V(lambda e, ss=ss: e.tensor_tensor(r1, Qo, ss, ALU.mult), drain=True)
                p.V(lambda e, o=o_e: e.tensor_tensor(dst[:, o:o + 512], r0, r1, ALU.subtract), drain=True)
                p.V(lambda e, cs=cs: e.tensor_tensor(r0, Qo, cs, ALU.mult))
                p.V(lambda e, ss=ss: e.tensor_tensor(r1, Qe, ss, ALU.mult), drain=True)
                p.V(lambda e, o=o_o: e.tensor_tensor(dst[:, o:o + 512], r0, r1, ALU.add), drain=True)
                ev = p.sig("vector", "vv")
                p.gset("evq", "vv", ev)
                p.gset("tb%d" % sl, "vv", ev)

    for b in range(B):
        tok0 = b * T
        if b == 0:
            p.wait("gpsimd", "gl_x", cast_v)
        wl = load_w("wk16", wait_val=None if b == 0 else wq_free)
        p.wait("tensor", "gl_w", wl)
        if b == 1:
            p.wait("gpsimd", "vv", vmax)
            vloc = pcollapse(mxb[:, 8:9], 1, 130)
            p.wait("gpsimd", "vv", vloc)
            cm = collective_max(0, rowbuf[0:1, 130:131], 1)
            p.wait("vector", "gl_c", cm)
            p.V(lambda e: e.tensor_scalar(rowbuf[0:1, 1:2], rowbuf[0:1, 160:161], 1.0 / QMAX, None, ALU.mult), drain=True)
            p.V(lambda e: e.reciprocal(rowbuf[0:1, 0:1], rowbuf[0:1, 1:2]), drain=True)
            vsx = p.sig("vector", "vv")
        xpos_pair(b, 2, tok0, 4, kT, T, "k")
        p.wait("tensor", "vv", p.val("vv"))
        p.wait("tensor", "sc", p.val("sc"))
        wl = load_w("wv_in")
        p.wait("tensor", "gl_w", wl)
        for j in range(16):
            for vc in range(2):
                sl = (j * 2 + vc) % 2
                p.gwait("tensor", "pmm%d" % sl)
                for hc in range(8):
                    p.TE(lambda e, j=j, vc=vc, hc=hc, sl=sl, tok0=tok0: e.matmul(
                        pmm[:, sl * 1024: sl * 1024 + 512],
                        xnT[:, hc * NT + tok0 + j * 128: hc * NT + tok0 + (j + 1) * 128],
                        wqkv[:, hc * H + vc * 512: hc * H + (vc + 1) * 512],
                        start=(hc == 0), stop=(hc == 7)))
                tv = p.sig("tensor", "tt")
                p.wait("scalar", "tt", tv)
                p.S(lambda e, j=j, vc=vc, sl=sl: e.copy(
                    vtok[:, j * H + vc * 512: j * H + (vc + 1) * 512],
                    pmm[:, sl * 1024: sl * 1024 + 512]))
                sv = p.sig("scalar", "sc")
                p.gset("pmm%d" % sl, "sc", sv)
        p.wait("tensor", "sc", p.val("sc"))
        wl = load_w("wq_in")
        p.wait("tensor", "gl_w", wl)

        if b == 0 and "tap_kT" in DEBUG_TAPS:
            p.wait("sync", "vv", p.val("vv"))
            p.dma("sync", d["taps"]["tap_kT"].ap(), kT[:], "ld_tp")
        if b == 0 and "tap_vtok" in DEBUG_TAPS:
            p.wait("sync", "sc", p.val("sc"))
            p.dma("sync", d["taps"]["tap_vtok"].ap(), vtok[:], "ld_tp")

        xpos_pair(b, 0, tok0, 1, qT, 512, "q")
        if b == 0 and "tap_qT" in DEBUG_TAPS:
            p.wait("sync", "vv", p.val("vv"))
            p.dma("sync", d["taps"]["tap_qT"].ap(), qT[:], "ld_tp")
        qdone = p.val("vv")
        for qc in range(4):
            q0 = qc * 512
            p.wait("tensor", "vv", qdone)

            for kc in range(16):
                sl = kc % 2
                k0 = kc * 128
                p.gwait("tensor", "pmm%d" % sl)
                for dc in range(8):
                    p.TE(lambda e, dc=dc, kc=kc, sl=sl: e.matmul(
                        pmm[:, sl * 1024: sl * 1024 + 512],
                        kT[:, dc * T + kc * 128: dc * T + (kc + 1) * 128],
                        qT[:, dc * 512:(dc + 1) * 512],
                        start=(dc == 0), stop=(dc == 7)))
                tv = p.sig("tensor", "tt")
                p.wait("scalar", "tt", tv)
                p.S(lambda e, kc=kc, sl=sl: e.activation(
                    expb[:, kc * 512:(kc + 1) * 512],
                    pmm[:, sl * 1024: sl * 1024 + 512], ACT.Exp))
                sv = p.sig("scalar", "sc")
                p.gset("pmm%d" % sl, "sc", sv)
                p.wait("vector", "sc", sv)
                p.V(lambda e, kc=kc, k0=k0, q0=q0: e.tensor_tensor(
                    ptilb[:, kc * 512:(kc + 1) * 512], expb[:, kc * 512:(kc + 1) * 512],
                    d2s[:, 1920 + q0 - k0: 1920 + q0 - k0 + 512], ALU.mult))
                pv = p.sig("vector", "vv")
            p.wait("tensor", "sc", p.val("sc"))
            p.gwait("tensor", "psums")
            for kc in range(16):
                p.TE(lambda e, kc=kc: e.matmul(psums[0:1, :], onesb[:, 0:1],
                                               expb[:, kc * 512:(kc + 1) * 512],
                                               start=(kc == 0), stop=(kc == 15)))
            tsum = p.sig("tensor", "tt")
            p.wait("scalar", "tt", tsum)
            p.gwait("scalar", "psums")
            p.gwait("scalar", "sumsb")
            p.S(lambda e: e.copy(sumsb[0:1, :], psums[0:1, :]))
            sv = p.sig("scalar", "sc")
            p.gset("psums", "sc", sv)
            p.wait("vector", "sc", sv)
            p.V(lambda e: e.reciprocal(sumsb[0:1, :], sumsb[0:1, :]))
            vsum = p.sig("vector", "vv")
            p.wait("scalar", "vv", vsum)
            sdq = p.dma("scalar", d["sums_d"].ap()[b:b + 1, q0:q0 + 512],
                        sumsb[0:1, :], "sd_s")
            p.gset("sumsb", "sd_s", sdq)

            pt_done = pv
            if qc < 3:
                xpos_pair(b, 0, tok0 + q0 + 512, 1, qT, 512, "q")
                qdone = p.val("vv")
                if qc == 2:
                    wq_free = p.val("tt")
            p.wait("tensor", "vv", pt_done)
            for vc in range(8):
                sl = vc % 2
                p.gwait("tensor", "ppv%d" % sl)
                for kc in range(16):
                    p.TE(lambda e, vc=vc, kc=kc, sl=sl: e.matmul(
                        ppv[:, sl * 512:(sl + 1) * 512],
                        vtok[:, kc * H + vc * 128: kc * H + (vc + 1) * 128],
                        ptilb[:, kc * 512:(kc + 1) * 512],
                        start=(kc == 0), stop=(kc == 15)))
                tv = p.sig("tensor", "tt")
                p.wait("scalar", "tt", tv)
                p.gwait("scalar", "atst")
                p.S(lambda e, sl=sl: e.copy(atst[:, 0:512],
                                            ppv[:, sl * 512:(sl + 1) * 512]))
                sv = p.sig("scalar", "sc")
                p.gset("ppv%d" % sl, "sc", sv)
                p.wait("scalar", "sc", sv)
                dv = p.dma("scalar", d["attnT_d"].ap()[b, vc * 128:(vc + 1) * 128, q0:q0 + 512],
                           atst[:, 0:512], "sd_a0")
                p.gset("atst", "sd_a0", dv)


        if b == 0:
            p.wait("vector", "sc", xnT_sc)
            p.V(lambda e: e.drain())
            for k in range(8):
                p.V(lambda e, k=k: e.tensor_reduce(mxb[:, k:k + 1], xnT[:, k * NT:(k + 1) * NT],
                                                   AX, ALU.max, apply_absolute_value=True))
            p.V(lambda e: e.drain())
            p.V(lambda e: e.tensor_reduce(mxb[:, 8:9], mxb[:, 0:8], AX, ALU.max))
            vmax = p.sig("vector", "vv")

    p.wait("tensor", "vv", vsx)
    p.wait("tensor", "gg", gset)
    bcast_col(rowbuf[0:1, 0:1], colbuf[:, 0:1])
    bcx = bcast_col(rowbuf[0:1, 1:2], colbuf[:, 1:2])
    p.wait("gpsimd", "sc", bcx)
    if "tap_scales" in DEBUG_TAPS and PHASE_LIMIT < 4:
        p.wait("sync", "sc", p.val("sc"))
        p.dma("sync", d["taps"]["tap_scales"].ap()[0:1, 0:64], rowbuf[0:1, 0:64], "ld_tp")

    p.barrier()
    stB.close()
    if PHASE_LIMIT < 3:
        return

    xqT = sb(stC, "xqT", [128, 8 * NT], BF16)
    stPB = ExitStack()
    rtmp = sb(stPB, "rtmp", [128, NT], F32)
    for k in range(8):
        p.V(lambda e, k=k: e.tensor_scalar(rtmp[:], xnT[:, k * NT:(k + 1) * NT],
                                           colbuf[:, 0:1], MAGIC, ALU.mult, ALU.add), drain=True)
        p.V(lambda e, k=k: e.tensor_scalar(xqT[:, k * NT:(k + 1) * NT], rtmp[:],
                                           MAGIC, colbuf[:, 1:2], ALU.subtract, ALU.mult), drain=True)
    vxq = p.sig("vector", "vv")
    if "tap_xqT" in DEBUG_TAPS:
        p.wait("sync", "vv", vxq)
        p.dma("sync", d["taps"]["tap_xqT"].ap(), xqT[:], "ld_tp")

    p.barrier()
    stXN.close()
    stPB.close()
    if PHASE_LIMIT < 4:
        return

    wst = sb(stC, "wst", [128, 2 * H], BF16)
    wnat = sb(stC, "wnat", [128, 2 * H], F32)
    wsc = sb(stC, "wsc", [128, 256], F32)
    wq_b = sb(stC, "wq_b", [128, 2 * H], F32)
    wqb16 = sb(stC, "wqb16", [128, 4 * H], BF16)
    stg = sb(stC, "stg", [128, 4 * 512], BF16)
    mxc = sb(stC, "mxc", [128, 392], F32)
    pwt = psal(stC, "pwt", [128, 2 * 1024], BF16)
    pug = psal(stC, "pug", [128, 2 * 512], F32)

    def deq_chunk(cb):
        src_name = "wu_in" if cb < 24 else "wg_in"
        ic = cb % 24
        bf = cb % 2
        s4 = cb % 4
        nblk = H // 32
        wn = wnat[:, bf * H: bf * H + H]
        ws0 = bf * 128
        wqb = wq_b[:, bf * H: bf * H + H]
        w16 = wqb16[:, s4 * H: s4 * H + H]
        p.gwait("sync", "wnat%d" % bf)
        lv = p.dma("sync", wn, d[src_name].ap()[ic * 128:(ic + 1) * 128, :], "ld_w%d" % bf)
        p.wait("vector", "ld_w%d" % bf, lv)
        p.gwait("vector", "wqb16%d" % s4)
        p.V(lambda e, wn=wn, nblk=nblk, ws0=ws0: e.tensor_reduce(
            wsc[:, ws0:ws0 + nblk], wn.rearrange("p (b j) -> p b j", j=32), AX, ALU.max,
            apply_absolute_value=True), drain=True)
        p.V(lambda e, nblk=nblk, ws0=ws0: e.tensor_scalar(wsc[:, ws0:ws0 + nblk], wsc[:, ws0:ws0 + nblk],
                                                          1.0 / WQMAX, None, ALU.mult), drain=True)
        p.V(lambda e, nblk=nblk, ws0=ws0: e.reciprocal(wsc[:, ws0 + nblk:ws0 + 2 * nblk],
                                                       wsc[:, ws0:ws0 + nblk]), drain=True)
        p.V(lambda e, wn=wn, nblk=nblk, ws0=ws0, wqb=wqb: e.tensor_tensor(
            wqb.rearrange("p (b j) -> p b j", j=32),
            wn.rearrange("p (b j) -> p b j", j=32),
            bass.AP(wsc, ws0 + nblk, [[256, 128], [1, nblk], [0, 32]]),
            ALU.mult), drain=True)
        vwn = p.sig("vector", "vv")
        p.gset("wnat%d" % bf, "vv", vwn)
        p.V(lambda e, wqb=wqb: e.tensor_scalar(wqb, wqb,
                                               MAGIC, MAGIC, ALU.add, ALU.subtract), drain=True)
        p.V(lambda e, nblk=nblk, ws0=ws0, wqb=wqb, w16=w16: e.tensor_tensor(
            w16.rearrange("p (b j) -> p b j", j=32),
            wqb.rearrange("p (b j) -> p b j", j=32),
            bass.AP(wsc, ws0, [[256, 128], [1, nblk], [0, 32]]),
            ALU.mult))
        return p.sig("vector", "vv")

    def deq_transpose(cb, vw):
        bf = cb % 2
        s4 = cb % 4
        w16 = wqb16[:, s4 * H: s4 * H + H]
        p.wait("tensor", "vv", vw)
        p.gwait("tensor", "pwt%d" % bf)
        for k in range(8):
            p.TE(lambda e, k=k, bf=bf, w16=w16: e.transpose(
                pwt[:, bf * 1024 + k * 128: bf * 1024 + k * 128 + 128],
                w16[:, k * 128:(k + 1) * 128], ident[:]))
        tv = p.sig("tensor", "tt")
        p.gset("wqb16%d" % s4, "tt", tv)
        p.wait("scalar", "tt", tv)
        p.gwait("scalar", "wstw%d" % bf)
        for k in range(8):
            p.S(lambda e, k=k, bf=bf: e.copy(
                wst[:, bf * H + k * 128: bf * H + (k + 1) * 128],
                pwt[:, bf * 1024 + k * 128: bf * 1024 + k * 128 + 128]))
        sv = p.sig("scalar", "sc")
        p.gset("pwt%d" % bf, "sc", sv)
        p.gset("wstr%d" % bf, "sc", sv)

    def upgate_mms(cb):
        bf = cb % 2
        ib = cb % 24
        spname = "sp_up" if cb < 24 else "sp_gate"
        p.gwait("tensor", "wstr%d" % bf)
        for tcol in range(8):
            tile_idx = cb * 8 + tcol
            sl = tile_idx % 4
            p.gwait("tensor", "pug%d" % (tile_idx % 2))
            for hc in range(8):
                p.TE(lambda e, bf=bf, tcol=tcol, hc=hc, sl=sl: e.matmul(
                    pug[:, (sl % 2) * 512:((sl % 2) + 1) * 512],
                    wst[:, bf * H + hc * 128: bf * H + (hc + 1) * 128],
                    xqT[:, hc * NT + tcol * 512: hc * NT + (tcol + 1) * 512],
                    start=(hc == 0), stop=(hc == 7)))
            tv = p.sig("tensor", "tt")
            if tcol == 7:
                p.gset("wstw%d" % bf, "tt", tv)
            p.wait("scalar", "tt", tv)
            p.gwait("scalar", "stgv%d" % sl)
            p.gwait("scalar", "stgd%d" % sl)
            p.S(lambda e, sl=sl: e.copy(stg[:, sl * 512:(sl + 1) * 512],
                                        pug[:, (sl % 2) * 512:((sl % 2) + 1) * 512]))
            sv = p.sig("scalar", "sc")
            p.gset("pug%d" % (tile_idx % 2), "sc", sv)
            p.wait("scalar", "sc", sv)
            dv = p.dma("scalar", d[spname].ap()[ib * 128:(ib + 1) * 128,
                                                tcol * 512:(tcol + 1) * 512],
                       stg[:, sl * 512:(sl + 1) * 512], "sd_p%d" % sl)
            p.wait("vector", "sc", sv)
            p.V(lambda e, sl=sl, col=4 + tile_idx: e.tensor_reduce(
                mxc[:, col:col + 1], stg[:, sl * 512:(sl + 1) * 512],
                AX, ALU.max, apply_absolute_value=True))
            mv = p.sig("vector", "vv")
            p.gset("stgv%d" % sl, "vv", mv)
            p.gset("stgd%d" % sl, "sd_p%d" % sl, dv)

    vws = {}
    for cb in range(52):
        if cb < 48:
            vws[cb] = deq_chunk(cb)
        if 3 <= cb < 51:
            deq_transpose(cb - 3, vws[cb - 3])
        if cb >= 4:
            upgate_mms(cb - 4)

    p.V(lambda e: e.drain(), drain=False)
    p.V(lambda e: e.tensor_reduce(mxc[:, 0:1], mxc[:, 4:196], AX, ALU.max), drain=True)
    p.V(lambda e: e.tensor_reduce(mxc[:, 1:2], mxc[:, 196:388], AX, ALU.max), drain=True)
    vmx = p.sig("vector", "vv")
    p.wait("gpsimd", "vv", vmx)
    pcollapse(mxc[:, 0:1], 1, 132)
    vloc = pcollapse(mxc[:, 1:2], 1, 133)
    p.wait("gpsimd", "vv", vloc)
    cm = collective_max(1, rowbuf[0:1, 132:134], 2)
    p.wait("vector", "gl_c", cm)
    p.V(lambda e: e.tensor_scalar(rowbuf[0:1, 3:4], rowbuf[0:1, 168:169], 1.0 / QMAX, None, ALU.mult), drain=True)
    p.V(lambda e: e.reciprocal(rowbuf[0:1, 2:3], rowbuf[0:1, 3:4]), drain=True)
    p.V(lambda e: e.tensor_scalar(rowbuf[0:1, 5:6], rowbuf[0:1, 169:170], 1.0 / QMAX, None, ALU.mult), drain=True)
    p.V(lambda e: e.reciprocal(rowbuf[0:1, 4:5], rowbuf[0:1, 5:6]), drain=True)
    p.V(lambda e: e.tensor_scalar(rowbuf[0:1, 6:7], rowbuf[0:1, 5:6], 1.0 / 65536.0, None, ALU.mult), drain=True)
    p.V(lambda e: e.tensor_scalar(rowbuf[0:1, 7:8], rowbuf[0:1, 5:6], -MAGIC, None, ALU.mult), drain=True)
    vsc2 = p.sig("vector", "vv")
    p.wait("tensor", "vv", vsc2)
    for src, dstc in ((2, 2), (3, 3), (4, 4), (5, 5), (6, 6), (7, 7)):
        bcast_col(rowbuf[0:1, src:src + 1], colbuf[:, dstc:dstc + 1])

    if "tap_scales" in DEBUG_TAPS:
        p.wait("sync", "sc", p.val("sc"))
        p.dma("sync", d["taps"]["tap_scales"].ap()[0:1, 0:64], rowbuf[0:1, 0:64], "ld_tp")

    p.barrier()
    stC.close()
    if PHASE_LIMIT < 5:
        return

    wdT = sb(stD, "wdT", [128, 24 * H], BF16)
    stDs = ExitStack()
    gin = sb(stDs, "gin", [128, 4 * 2048], BF16)
    t1b_ = sb(stDs, "t1b_", [128, 4 * 2048], F32)
    rb__ = sb(stDs, "rb__", [128, 2048], F32)
    sgb_ = sb(stDs, "sgb_", [128, 4 * 2048], F32)
    f3b_ = sb(stDs, "f3b_", [128, 2048], F32)
    g2o = sb(stDs, "g2o", [128, 4 * 2048], BF16)
    mxd = sb(stDs, "mxd", [128, 104], F32)

    n_sweep = 48
    pend_mx = None
    for ti in range(n_sweep):
        ic, th = divmod(ti, 2)
        bf = ti % 4
        g_t = gin[:, bf * 2048:(bf + 1) * 2048]
        o_t = g2o[:, bf * 2048:(bf + 1) * 2048]
        t1b = t1b_[:, bf * 2048:(bf + 1) * 2048]
        rb_ = rb__[:, 0:2048]
        sgb = sgb_[:, bf * 2048:(bf + 1) * 2048]
        f3b = f3b_[:, 0:2048]
        p.gwait("sync", "gin%d" % bf)
        lv = p.dma("sync", g_t, d["sp_gate"].ap()[ic * 128:(ic + 1) * 128,
                                                  th * 2048:(th + 1) * 2048], "ld_g%d" % bf)
        p.wait("scalar", "ld_g%d" % bf, lv)
        p.gwait("scalar", "pdch%d" % bf)
        p.gwait("scalar", "pdcv%d" % bf)
        p.S(lambda e, g_t=g_t, t1b=t1b: e.activation(t1b, g_t, ACT.Identity,
                                                     scale=colbuf[:, 4:5], bias=colbuf[:, 12:13]))
        p.emit("scalar", lambda e: e.drain(), kind="drain")
        p.S(lambda e, sgb=sgb, t1b=t1b: e.activation(sgb, t1b, ACT.Sigmoid,
                                                     scale=colbuf[:, 5:6], bias=colbuf[:, 7:8]))
        s1 = p.sig("scalar", "sc")
        p.gset("gin%d" % bf, "sc", s1)
        p.wait("vector", "sc", s1)
        p.V(lambda e, rb_=rb_, sgb=sgb: e.tensor_scalar(rb_, sgb, 65536.0, MAGIC, ALU.mult, ALU.add))
        if pend_mx is not None:
            po_t, pc_, pbf = pend_mx
            p.emit("vector", lambda e: e.drain(), kind="drain")
            p.V(lambda e, po_t=po_t, pc_=pc_: e.tensor_reduce(
                mxd[:, pc_:pc_ + 1], po_t, AX, ALU.max, apply_absolute_value=True))
            vmxt = p.sig("vector", "vv")
            p.gset("g2g%d" % pbf, "vv", vmxt)
        else:
            p.emit("vector", lambda e: e.drain(), kind="drain")
        p.V(lambda e, f3b=f3b, rb_=rb_: e.tensor_scalar(f3b, rb_, MAGIC, 65535.0, ALU.subtract, ALU.min), drain=True)
        p.gset("pdcv%d" % bf, "vv", p.val("vv") + 1)
        p.gwait("vector", "g2o%d" % bf)
        p.gwait("vector", "g2g%d" % bf)
        p.V(lambda e, o_t=o_t, t1b=t1b, f3b=f3b: e.scalar_tensor_tensor(
            o_t, t1b, MAGIC, f3b, ALU.subtract, ALU.mult))
        v2 = p.sig("vector", "vv")
        p.gset("pdch%d" % bf, "vv", v2)
        pend_mx = (o_t, 4 + ti, bf)
        p.wait("gpsimd", "vv", v2)
        dv = p.dma("gpsimd", d["sp_g2"].ap()[ic * 128:(ic + 1) * 128,
                                             th * 2048:(th + 1) * 2048], o_t, "sd_g%d" % bf)
        p.gset("g2o%d" % bf, "sd_g%d" % bf, dv)

    po_t, pc_, pbf = pend_mx
    p.V(lambda e: e.drain(), drain=False)
    p.V(lambda e, po_t=po_t, pc_=pc_: e.tensor_reduce(
        mxd[:, pc_:pc_ + 1], po_t, AX, ALU.max, apply_absolute_value=True), drain=True)
    p.V(lambda e: e.tensor_reduce(mxd[:, 0:1], mxd[:, 4:52], AX, ALU.max), drain=True)
    vmx = p.sig("vector", "vv")
    if PHASE_LIMIT < 4.4:
        return
    p.wait("gpsimd", "vv", vmx)
    vloc = pcollapse(mxd[:, 0:1], 1, 136)
    p.wait("gpsimd", "vv", vloc)
    cm = collective_max(2, rowbuf[0:1, 136:137], 1)
    if PHASE_LIMIT < 4.7:
        return
    wdnat = sb(stDw, "wdnat", [128, 2 * I], F32)
    wdsc = sb(stDw, "wdsc", [128, 256], F32)
    wdq = sb(stDw, "wdq", [128, I], F32)
    wdq16 = sb(stDw, "wdq16", [128, I], BF16)
    pwd = psal(stDw, "pwd", [128, 2 * 1024], BF16)
    for hc in range(8):
        bf = hc % 2
        wn = wdnat[:, bf * I:(bf + 1) * I]
        p.gwait("sync", "wdnat%d" % bf)
        lv = p.dma("sync", wn, d["wd_in"].ap()[hc * 128:(hc + 1) * 128, :], "ld_w%d" % bf)
        p.wait("vector", "ld_w%d" % bf, lv)
        p.gwait("vector", "wdq16")
        p.V(lambda e, wn=wn: e.tensor_reduce(
            wdsc[:, 0:96], wn.rearrange("p (b j) -> p b j", j=32), AX, ALU.max,
            apply_absolute_value=True), drain=True)
        p.V(lambda e: e.tensor_scalar(wdsc[:, 0:96], wdsc[:, 0:96], 1.0 / WQMAX, None, ALU.mult), drain=True)
        p.V(lambda e: e.reciprocal(wdsc[:, 96:192], wdsc[:, 0:96]), drain=True)
        p.V(lambda e, wn=wn: e.tensor_tensor(
            wdq[:].rearrange("p (b j) -> p b j", j=32),
            wn.rearrange("p (b j) -> p b j", j=32),
            bass.AP(wdsc, 96, [[256, 128], [1, 96], [0, 32]]),
            ALU.mult), drain=True)
        vwn = p.sig("vector", "vv")
        p.gset("wdnat%d" % bf, "vv", vwn)
        p.V(lambda e: e.tensor_scalar(wdq[:], wdq[:], MAGIC, MAGIC, ALU.add, ALU.subtract), drain=True)
        p.V(lambda e: e.tensor_tensor(
            wdq16[:].rearrange("p (b j) -> p b j", j=32),
            wdq[:].rearrange("p (b j) -> p b j", j=32),
            bass.AP(wdsc, 0, [[256, 128], [1, 96], [0, 32]]),
            ALU.mult))
        vw = p.sig("vector", "vv")
        p.wait("tensor", "vv", vw)
        for k in range(24):
            sl2 = k % 8
            if sl2 == 0:
                p.gwait("tensor", "pwd%d" % ((k // 8) % 2))
            p.TE(lambda e, k=k, sl2=sl2, g=(k // 8) % 2: e.transpose(
                pwd[:, g * 1024 + sl2 * 128: g * 1024 + sl2 * 128 + 128],
                wdq16[:, k * 128:(k + 1) * 128], ident[:]))
            if sl2 == 7 or k == 23:
                tv = p.sig("tensor", "tt")
                p.wait("scalar", "tt", tv)
                kk0 = (k // 8) * 8
                for kk in range(kk0, k + 1):
                    p.S(lambda e, kk=kk, hc=hc, g=(k // 8) % 2: e.copy(
                        wdT[:, kk * H + hc * 128: kk * H + (hc + 1) * 128],
                        pwd[:, g * 1024 + (kk % 8) * 128: g * 1024 + (kk % 8) * 128 + 128]))
                sv = p.sig("scalar", "sc")
                p.gset("pwd%d" % ((k // 8) % 2), "sc", sv)
        p.gset("wdq16", "sc", p.val("sc"))

    if "tap_wdT" in DEBUG_TAPS:
        p.wait("sync", "sc", p.val("sc"))
        p.dma("sync", d["taps"]["tap_wdT"].ap(), wdT[:], "ld_tp")

    p.barrier()
    stDw.close()
    if PHASE_LIMIT < 6:
        return

    p.wait("vector", "gl_c", cm)
    p.V(lambda e: e.tensor_tensor(rowbuf[0:1, 176:177], rowbuf[0:1, 176:177],
                                  rowbuf[0:1, 6:7], ALU.mult), drain=True)
    p.V(lambda e: e.tensor_scalar(rowbuf[0:1, 9:10], rowbuf[0:1, 176:177], 1.0 / QMAX, None, ALU.mult), drain=True)
    p.V(lambda e: e.reciprocal(rowbuf[0:1, 8:9], rowbuf[0:1, 9:10]), drain=True)
    p.V(lambda e: e.tensor_tensor(rowbuf[0:1, 13:14], rowbuf[0:1, 8:9],
                                  rowbuf[0:1, 6:7], ALU.mult), drain=True)
    vsc3 = p.sig("vector", "vv")
    p.wait("tensor", "vv", vsc3)
    bcast_col(rowbuf[0:1, 13:14], colbuf[:, 8:9])
    bc3 = bcast_col(rowbuf[0:1, 9:10], colbuf[:, 9:10])
    p.wait("vector", "sc", bc3)

    stD2a = ExitStack()
    uin = sb(stD2a, "uin", [128, 4 * 2048], BF16)
    pend_mx = None
    for ti in range(n_sweep):
        ic, th = divmod(ti, 2)
        bf = ti % 4
        g_t = gin[:, bf * 2048:(bf + 1) * 2048]
        u_t = uin[:, bf * 2048:(bf + 1) * 2048]
        o_t = g2o[:, bf * 2048:(bf + 1) * 2048]
        t1b = t1b_[:, bf * 2048:(bf + 1) * 2048]
        rb_ = rb__[:, 0:2048]
        sgb = sgb_[:, bf * 2048:(bf + 1) * 2048]
        f3b = f3b_[:, 0:2048]
        p.gwait("sync", "gin%d" % bf)
        lv1 = p.dma("sync", g_t, d["sp_g2"].ap()[ic * 128:(ic + 1) * 128,
                                                 th * 2048:(th + 1) * 2048], "ld_g%d" % bf)
        p.gwait("sync", "uin%d" % bf)
        lv2 = p.dma("sync", u_t, d["sp_up"].ap()[ic * 128:(ic + 1) * 128,
                                                 th * 2048:(th + 1) * 2048], "ld_u%d" % bf)
        p.wait("scalar", "ld_g%d" % bf, lv1)
        p.wait("scalar", "ld_u%d" % bf, lv2)
        p.gwait("scalar", "pdch%d" % bf)
        p.gwait("scalar", "pdcv%d" % bf)
        p.S(lambda e, g_t=g_t, t1b=t1b: e.activation(t1b, g_t, ACT.Identity,
                                                     scale=colbuf[:, 8:9], bias=colbuf[:, 12:13]))
        p.emit("scalar", lambda e: e.drain(), kind="drain")
        p.S(lambda e, u_t=u_t, sgb=sgb: e.activation(sgb, u_t, ACT.Identity,
                                                     scale=colbuf[:, 2:3], bias=colbuf[:, 12:13]))
        s1 = p.sig("scalar", "sc")
        p.gset("gin%d" % bf, "sc", s1)
        p.gset("uin%d" % bf, "sc", s1)
        p.wait("vector", "sc", s1)
        p.V(lambda e, rb_=rb_, t1b=t1b: e.tensor_scalar(rb_, t1b, MAGIC, None, ALU.subtract))
        if pend_mx is not None:
            po_t, pc_, pbf = pend_mx
            p.emit("vector", lambda e: e.drain(), kind="drain")
            p.V(lambda e, po_t=po_t, pc_=pc_: e.tensor_reduce(
                mxd[:, pc_:pc_ + 1], po_t, AX, ALU.max, apply_absolute_value=True))
            vmxt = p.sig("vector", "vv")
            p.gset("g2g%d" % pbf, "vv", vmxt)
        else:
            p.emit("vector", lambda e: e.drain(), kind="drain")
        p.V(lambda e, f3b=f3b, sgb=sgb: e.tensor_scalar(f3b, sgb, MAGIC, None, ALU.subtract), drain=True)
        p.gset("pdcv%d" % bf, "vv", p.val("vv") + 1)
        p.gwait("vector", "g2o%d" % bf)
        p.gwait("vector", "g2g%d" % bf)
        p.V(lambda e, o_t=o_t, rb_=rb_, f3b=f3b: e.tensor_tensor(o_t, rb_, f3b, ALU.mult))
        v2 = p.sig("vector", "vv")
        p.gset("pdch%d" % bf, "vv", v2)
        pend_mx = (o_t, 52 + ti, bf)
        p.wait("gpsimd", "vv", v2)
        dv = p.dma("gpsimd", d["sp_m"].ap()[ic * 128:(ic + 1) * 128,
                                            th * 2048:(th + 1) * 2048], o_t, "sd_g%d" % bf)
        p.gset("g2o%d" % bf, "sd_g%d" % bf, dv)

    po_t, pc_, pbf = pend_mx
    p.V(lambda e: e.drain(), drain=False)
    p.V(lambda e, po_t=po_t, pc_=pc_: e.tensor_reduce(
        mxd[:, pc_:pc_ + 1], po_t, AX, ALU.max, apply_absolute_value=True), drain=True)
    p.V(lambda e: e.tensor_reduce(mxd[:, 1:2], mxd[:, 52:100], AX, ALU.max), drain=True)
    vmx = p.sig("vector", "vv")
    p.wait("gpsimd", "vv", vmx)
    vloc = pcollapse(mxd[:, 1:2], 1, 137)
    p.wait("gpsimd", "vv", vloc)
    cm = collective_max(3, rowbuf[0:1, 137:138], 1)
    p.wait("vector", "gl_c", cm)
    p.V(lambda e: e.tensor_scalar(rowbuf[0:1, 20:21], rowbuf[0:1, 184:185], 1.0 / QMAX, None, ALU.mult), drain=True)
    p.V(lambda e: e.tensor_tensor(rowbuf[0:1, 21:22], rowbuf[0:1, 20:21], rowbuf[0:1, 9:10], ALU.mult), drain=True)
    p.V(lambda e: e.tensor_tensor(rowbuf[0:1, 11:12], rowbuf[0:1, 21:22], rowbuf[0:1, 3:4], ALU.mult), drain=True)
    p.V(lambda e: e.tensor_tensor(rowbuf[0:1, 22:23], rowbuf[0:1, 9:10], rowbuf[0:1, 3:4], ALU.mult), drain=True)
    p.V(lambda e: e.reciprocal(rowbuf[0:1, 23:24], rowbuf[0:1, 11:12]), drain=True)
    p.V(lambda e: e.tensor_tensor(rowbuf[0:1, 10:11], rowbuf[0:1, 22:23], rowbuf[0:1, 23:24], ALU.mult), drain=True)
    vsc4 = p.sig("vector", "vv")
    p.wait("tensor", "vv", vsc4)
    bcast_col(rowbuf[0:1, 10:11], colbuf[:, 10:11])
    bc4 = bcast_col(rowbuf[0:1, 11:12], colbuf[:, 11:12])
    p.wait("vector", "sc", bc4)
    p.barrier()
    stD2a.close()
    stDs.close()
    if PHASE_LIMIT < 7:
        p.barrier()
        return

    oqT_ = sb(stD2, "oqT_", [128, 2 * 24 * 512], BF16)
    oqrs = sb(stD2, "oqrs", [128, 1024], F32)
    minb = sb(stD2, "minb", [128, 4 * 1024], BF16)
    I4 = I * 4
    attin = sb(stD2, "attin", [128, 4 * 1024], BF16)
    xres = sb(stD2, "xres", [128, 4 * 1024], F32)
    accb = sb(stD2, "accb", [128, 2 * 1024], F32)
    outb = sb(stD2, "outb", [128, 2 * 1024], F32)
    rrow = sb(stD2, "rrow", [1, T], F32, side="right")
    rcol1 = sb(stD2, "rcol1", [128, 2], F32)
    pdown = psal(stD2, "pdown", [128, 2 * 1024], F32)
    patt = psal(stD2, "patt", [128, 2 * 1024], BF16)

    pend_store = []
    for b in range(B):
        p.gwait("sync", "rrow")
        rv = p.dma("sync", rrow[0:1, :], d["sums_d"].ap()[b:b + 1, :], "ld_rr")
        p.wait("tensor", "ld_rr", rv)
        for st_ in range(4):
            t0 = st_ * 512
            ssl = st_ % 2
            oqT = oqT_[:, ssl * 24 * 512:(ssl + 1) * 24 * 512]
            p.gwait("vector", "oqT%d" % ssl)
            for icp in range(12):
                bf = (st_ * 12 + icp) % 4
                mt = minb[:, bf * 1024: bf * 1024 + 1024]
                p.gwait("gpsimd", "minb%d" % bf)
                lv = p.dma("gpsimd", mt,
                           bass.AP(d["sp_m"], (2 * icp * 128) * NT + b * T + t0,
                                   [[NT, 128], [128 * NT, 2], [1, 512]]), "ld_m%d" % bf)
                p.wait("vector", "ld_m%d" % bf, lv)
                p.V(lambda e, mt=mt: e.tensor_scalar(oqrs[:, 0:1024], mt, colbuf[:, 10:11],
                                                     MAGIC, ALU.mult, ALU.add), drain=True)
                vm = p.sig("vector", "vv")
                p.gset("minb%d" % bf, "vv", vm)
                p.V(lambda e, icp=icp, oqT=oqT: e.tensor_scalar(
                    oqT[:, icp * 1024:(icp + 1) * 1024],
                    oqrs[:, 0:1024], MAGIC, colbuf[:, 11:12],
                    ALU.subtract, ALU.mult), drain=True)
            voq = p.sig("vector", "vv")
            p.wait("tensor", "vv", voq)
            avs = {}
            xvs = {}
            for ts_ in range(4):
                sl4 = ts_ % 4
                tr = (t0 + ts_ * 128) // 128
                p.gwait("sync", "attin%d" % sl4)
                avs[ts_] = p.dma("sync", attin[:, sl4 * 1024:(sl4 + 1) * 1024],
                                 bass.AP(d["attnT_d"], b * H * T + tr * 128,
                                         [[T, 128], [128 * T, 8], [1, 128]]), "ld_a%d" % sl4)
                p.gwait("sync", "xres%d" % sl4)
                xvs[ts_] = p.dma("sync", xres[:, sl4 * 1024:(sl4 + 1) * 1024],
                                 d["x_in"].ap()[b, tr * 128:(tr + 1) * 128, :], "ld_xr%d" % sl4)
            for ts_ in range(4):
                tc_g = (b * T + t0 + ts_ * 128) // 128
                sl = ts_ % 2
                sl4 = ts_ % 4
                tr = (t0 + ts_ * 128) // 128
                av = avs[ts_]
                xv = xvs[ts_]
                p.gwait("tensor", "pdown%d" % sl)
                for dcol in range(2):
                    for ic in range(24):
                        p.TE(lambda e, ic=ic, dcol=dcol, ts_=ts_, sl=sl, oqT=oqT: e.matmul(
                            pdown[:, sl * 1024 + dcol * 512: sl * 1024 + (dcol + 1) * 512],
                            oqT[:, ic * 512 + ts_ * 128: ic * 512 + (ts_ + 1) * 128],
                            wdT[:, ic * H + dcol * 512: ic * H + (dcol + 1) * 512],
                            start=(ic == 0), stop=(ic == 23)))
                p.wait("tensor", "ld_a%d" % sl4, av)
                p.gwait("tensor", "patt%d" % sl)
                for k in range(8):
                    p.TE(lambda e, k=k, sl=sl, sl4=sl4: e.transpose(
                        patt[:, sl * 1024 + k * 128: sl * 1024 + (k + 1) * 128],
                        attin[:, sl4 * 1024 + k * 128: sl4 * 1024 + (k + 1) * 128], ident[:]))
                p.gwait("tensor", "rcol%d" % sl)
                p.TE(lambda e, tr=tr, sl=sl: e.matmul(
                    pcol[:, 0:1], rrow[0:1, tr * 128:(tr + 1) * 128],
                    onesf[0:1, 0:1], start=True, stop=True))
                tv = p.sig("tensor", "tt")
                p.gset("oqT%d" % ssl, "tt", tv)
                p.wait("scalar", "tt", tv)
                p.S(lambda e, sl=sl: e.copy(rcol1[:, sl:sl + 1], pcol[:, 0:1]))
                sv = p.sig("scalar", "sc")
                p.gset("rrow", "sc", sv)
                p.wait("vector", "sc", sv)
                p.wait("vector", "tt", tv)
                p.wait("vector", "ld_xr%d" % sl4, xv)
                p.gwait("vector", "outb%d" % sl)
                p.V(lambda e, sl=sl, sl4=sl4: e.scalar_tensor_tensor(
                    accb[:, sl * 1024:(sl + 1) * 1024], patt[:, sl * 1024:(sl + 1) * 1024],
                    rcol1[:, sl:sl + 1],
                    xres[:, sl4 * 1024:(sl4 + 1) * 1024], ALU.mult, ALU.add), drain=True)
                p.V(lambda e, sl=sl: e.tensor_tensor(
                    outb[:, sl * 1024:(sl + 1) * 1024], accb[:, sl * 1024:(sl + 1) * 1024],
                    pdown[:, sl * 1024:(sl + 1) * 1024], ALU.add))
                vo = p.sig("vector", "vv")
                p.gset("patt%d" % sl, "vv", vo)
                p.gset("pdown%d" % sl, "vv", vo)
                p.gset("attin%d" % sl4, "vv", vo)
                p.gset("xres%d" % sl4, "vv", vo)
                p.gset("rcol%d" % sl, "vv", vo)
                pend_store.append((vo, b, tr, sl))
                if len(pend_store) > 1:
                    q_vo, q_b, q_tr, q_sl = pend_store.pop(0)
                    p.wait("scalar", "vv", q_vo)
                    q_ov = p.dma("scalar", d["out_t"].ap()[q_b, q_tr * 128:(q_tr + 1) * 128, :],
                                 outb[:, q_sl * 1024:(q_sl + 1) * 1024], "ld_o%d" % q_sl)
                    p.gset("outb%d" % q_sl, "ld_o%d" % q_sl, q_ov)

    for q_vo, q_b, q_tr, q_sl in pend_store:
        p.wait("scalar", "vv", q_vo)
        q_ov = p.dma("scalar", d["out_t"].ap()[q_b, q_tr * 128:(q_tr + 1) * 128, :],
                     outb[:, q_sl * 1024:(q_sl + 1) * 1024], "ld_o%d" % q_sl)
        p.gset("outb%d" % q_sl, "ld_o%d" % q_sl, q_ov)
    p.dma_finalize()
    p.barrier()
    stD2.close()
    stD.close()
    stP.close()



def kernel(L0_attn_X, L0_attn_W_Q, L0_attn_W_K, L0_attn_W_V,
           L0_ffn_w_gate, L0_ffn_w_up, L0_ffn_w_down):
    import ml_dtypes
    perm = np.concatenate([np.arange(0, H, 2), np.arange(1, H, 2)])
    wq = np.ascontiguousarray(np.asarray(L0_attn_W_Q, np.float32)[:, perm])
    wk = np.ascontiguousarray(np.asarray(L0_attn_W_K, np.float32)[:, perm])
    wv = np.ascontiguousarray(np.asarray(L0_attn_W_V, np.float32))
    wu = np.ascontiguousarray(np.asarray(L0_ffn_w_up, np.float32))
    wg = np.ascontiguousarray(np.asarray(L0_ffn_w_gate, np.float32))
    wd = np.ascontiguousarray(np.asarray(L0_ffn_w_down, np.float32))
    X = np.asarray(L0_attn_X, np.float32)
    ident = np.eye(128, dtype=ml_dtypes.bfloat16)

    nc = build_nc()
    in_maps = []
    for c in range(8):
        in_maps.append({"x": np.ascontiguousarray(X[2 * c:2 * c + 2]),
                        "wq": wq, "wk": wk, "wv": wv,
                        "wu": wu, "wg": wg, "wd": wd, "ident": ident})
    res = run_bass_kernel_spmd(nc, in_maps, core_ids=list(range(8)))
    out = np.concatenate([r["out"] for r in res.results], axis=0)
    return out.astype(np.float32)


# version 34
# speedup vs baseline: 1.0180x; 1.0180x over previous
import numpy as np
import concourse.bass as bass
import concourse.mybir as mybir
from concourse.bass_utils import run_bass_kernel_spmd
from contextlib import ExitStack

F32 = mybir.dt.float32
BF16 = mybir.dt.bfloat16
ALU = mybir.AluOpType
ACT = mybir.ActivationFunctionType
AX = mybir.AxisListType.X

MAGIC = 12582912.0
LN_GAMMA = -0.10536051565782628
INV2PI = 0.15915494309189535
C1_2PI = 6.28125
C2_2PI = 0.0019353071795864769
HALF_PI = 1.5707963267948966
LNF = 0.017988946039015984
QMAX = 32767.0
WQMAX = 7.0
EPS_LN = 1e-5

B, T, H, I = 2, 2048, 1024, 3072
NT = B * T

ENGINES = ("sync", "vector", "scalar", "tensor", "gpsimd")
DEBUG_TAPS = set()
PHASE_LIMIT = 7


class P:

    def __init__(self, nc, es):
        self.nc = nc
        self.es = es
        self.q = {e: [] for e in ENGINES}
        self.cnt = {}
        self._sems = {}
        self.guards = {}
        self.dma_streams = {}

    def sem(self, name):
        if name not in self._sems:
            self._sems[name] = self.es.enter_context(self.nc.semaphore(name))
        return self._sems[name]

    def inc(self, name, n=1):
        self.cnt[name] = self.cnt.get(name, 0) + n
        return self.cnt[name]

    def val(self, name):
        return self.cnt.get(name, 0)

    def emit(self, eng, fn, kind="op"):
        self.q[eng].append((fn, kind))

    def wait(self, eng, sem_name, value):
        if value <= 0:
            return
        s = self.sem(sem_name)
        self.emit(eng, lambda e, s=s, v=value: e.wait_ge(s, v), kind="wait")

    def sig(self, eng, sem_name):
        s = self.sem(sem_name)
        v = self.inc(sem_name)
        q = self.q[eng]
        i = len(q) - 1
        while i >= 0 and q[i][1] == "wait":
            i -= 1
        if i >= 0 and q[i][1] == "op":
            fn = q[i][0]
            q[i] = (lambda e, fn=fn, s=s: fn(e).then_inc(s, 1), "pre")
            return v
        if eng == "vector":
            self.emit(eng, lambda e: e.drain(), kind="drain")
        self.emit(eng, lambda e, s=s: e.nop().then_inc(s, 1), kind="pre")
        return v

    def sig_nop(self, eng, sem_name):
        s = self.sem(sem_name)
        v = self.inc(sem_name)
        if eng == "vector":
            self.emit(eng, lambda e: e.drain(), kind="drain")
        self.emit(eng, lambda e, s=s: e.nop().then_inc(s, 1), kind="pre")
        return v

    def dma(self, eng, out, in_, sem_name):
        s = self.sem(sem_name)
        v = self.inc(sem_name, 16)
        self.dma_streams[sem_name] = eng
        self.emit(eng, lambda e, o=out, i=in_, s=s: e.dma_start(out=o, in_=i).then_inc(s, 16),
                  kind="dma")
        return v

    def flush_sig(self, eng):
        return self.sig(eng, "fl_" + eng)

    def dma_finalize(self):
        for name, eng in sorted(self.dma_streams.items()):
            self.wait(eng, name, self.val(name))

    def gwait(self, eng, key):
        g = self.guards.get(key)
        if g is not None:
            self.wait(eng, g[0], g[1])

    def gset(self, key, sem_name, value):
        self.guards[key] = (sem_name, value)

    def barrier(self, engines=ENGINES, skip=()):
        for name, eng in sorted(self.dma_streams.items()):
            if name in skip:
                continue
            self.wait(eng, name, self.val(name))
        s = self.sem("bar")
        tgt = self.inc("bar", len(engines))
        for eng in engines:
            if eng == "vector":
                self.emit(eng, lambda e: e.drain(), kind="drain")
            self.emit(eng, lambda e, s=s: e.nop().then_inc(s, 1), kind="pre")
            self.emit(eng, lambda e, t=tgt, s=s: e.wait_ge(s, t), kind="wait")

    def V(self, fn, drain=False):
        self.emit("vector", fn)
        if drain:
            self.emit("vector", lambda e: e.drain(), kind="drain")

    def S(self, fn, drain=False):
        self.emit("scalar", fn)
        if drain:
            self.emit("scalar", lambda e: e.drain(), kind="drain")

    def TE(self, fn):
        self.emit("tensor", fn)

    def G(self, fn):
        self.emit("gpsimd", fn)


def build_nc():
    nc = bass.Bass("TRN2", target_bir_lowering=False)

    d = {}
    d["x_in"] = nc.dram_tensor("x", [B, T, H], F32, kind="ExternalInput")
    d["wq_in"] = nc.dram_tensor("wq", [H, H], F32, kind="ExternalInput")
    d["wk_in"] = nc.dram_tensor("wk", [H, H], F32, kind="ExternalInput")
    d["wv_in"] = nc.dram_tensor("wv", [H, H], F32, kind="ExternalInput")
    d["wk16"] = nc.dram_tensor("wk16", [H, H], BF16)
    d["wu_in"] = nc.dram_tensor("wu", [I, H], F32, kind="ExternalInput")
    d["wg_in"] = nc.dram_tensor("wg", [I, H], F32, kind="ExternalInput")
    d["wd_in"] = nc.dram_tensor("wd", [H, I], F32, kind="ExternalInput")
    d["ident_in"] = nc.dram_tensor("ident", [128, 128], BF16, kind="ExternalInput")
    d["out_t"] = nc.dram_tensor("out", [B, T, H], F32, kind="ExternalOutput")

    d["tabs"] = nc.dram_tensor("tabs", [4, 512, T], BF16)
    d["attnT_d"] = nc.dram_tensor("attnT_d", [B, H, T], BF16)
    d["sums_d"] = nc.dram_tensor("sums_d", [B, T], F32)
    d["sp_up"] = nc.dram_tensor("sp_up", [I, NT], BF16)
    d["sp_gate"] = nc.dram_tensor("sp_gate", [I, NT], BF16)
    d["sp_g2"] = nc.dram_tensor("sp_g2", [I, NT], BF16)
    d["sp_m"] = nc.dram_tensor("sp_m", [I, NT], BF16)
    d["cc_in"] = [nc.dram_tensor(f"cc_in{i}", [1, 8], F32) for i in range(4)]
    d["cc_out"] = [nc.dram_tensor(f"cc_out{i}", [1, 8], F32) for i in range(4)]
    d["misc_d"] = nc.dram_tensor("misc_d", [8, 256], F32)

    d["taps"] = {}
    for name, shape, dt in [
        ("tap_xnT", [128, 8 * NT], BF16),
        ("tap_xqT", [128, 8 * NT], BF16),
        ("tap_scales", [1, 64], F32),
        ("tap_dtile", [128, 512], BF16),
        ("tap_tabs", [4 * 512, T], BF16),
        ("tap_kT", [128, 8 * T], BF16),
        ("tap_vtok", [128, 16 * H], BF16),
        ("tap_qT", [128, 8 * 512], BF16),
        ("tap_sums", [B, T], F32),
        ("tap_wuT", [128, 8 * I], BF16),
        ("tap_wdT", [128, 24 * H], BF16),
    ]:
        if name in DEBUG_TAPS:
            d["taps"][name] = nc.dram_tensor(name, shape, dt, kind="ExternalOutput")

    with ExitStack() as es0:
        p = P(nc, es0)
        build_program(p, d)
        with nc.Block() as block:
            @block.sync
            def _(e):
                for fn, _k in p.q["sync"]:
                    fn(e)

            @block.vector
            def _(e):
                for fn, _k in p.q["vector"]:
                    fn(e)

            @block.scalar
            def _(e):
                for fn, _k in p.q["scalar"]:
                    fn(e)

            @block.tensor
            def _(e):
                for fn, _k in p.q["tensor"]:
                    fn(e)

            @block.gpsimd
            def _(e):
                for fn, _k in p.q["gpsimd"]:
                    fn(e)
    return nc


def build_program(p, d):
    nc = p.nc
    stP, stXN, stA, stB, stC, stD, stDw, stD2 = (ExitStack() for _ in range(8))

    def sb(st, name, shape, dt, side=None):
        return st.enter_context(nc.sbuf_tensor(name, shape, dt, side=side))

    def psal(st, name, shape, dt=F32):
        return st.enter_context(nc.psum_tensor(name, shape, dt))

    for s in ("tt", "vv", "sc", "gg", "cc"):
        p.sem(s)

    ident = sb(stP, "identity", [128, 128], BF16)
    colbuf = sb(stP, "colbuf", [128, 16], F32)
    rowbuf = sb(stP, "rowbuf", [1, 392], F32)
    mxb = sb(stP, "mxb", [128, 12], F32)
    onesf = sb(stP, "onesf", [1, 128], F32)
    onesb = sb(stP, "onesb", [128, 1], BF16)
    pcol = psal(stP, "pcol", [128, 1], F32)

    idv = p.dma("sync", ident[:], d["ident_in"].ap(), "ld_i")
    cast_v = p.dma("gpsimd", d["wk16"].ap(), d["wk_in"].ap(), "gl_x")
    p.G(lambda e: e.memset(onesf[0:1, :], 1.0))
    p.G(lambda e: e.memset(onesb[:, 0:1], 1.0))
    p.G(lambda e: e.memset(rowbuf[0:1, :], 0.0))
    p.G(lambda e: e.memset(colbuf[:, 12:13], MAGIC))
    gset = p.sig("gpsimd", "gg")

    def bcast_col(src_ap, dst_col):
        p.gwait("tensor", "pcol")
        tv = p.inc("tt")
        p.emit("tensor", lambda e, s=src_ap: e.matmul(pcol[:, 0:1], onesf[0:1, :], s,
                                          start=True, stop=True).then_inc(p.sem("tt"), 1),
               kind="pre")
        p.wait("scalar", "tt", tv)
        p.S(lambda e, dc=dst_col: e.copy(dc, pcol[:, 0:1]))
        sv = p.sig("scalar", "sc")
        p.gset("pcol", "sc", sv)
        return sv

    p.dma_streams["gl_c"] = "gpsimd"
    p.dma_streams["gl_x"] = "gpsimd"
    p.dma_streams["gl_p"] = "gpsimd"
    p.dma_streams["gl_v"] = "gpsimd"

    def collective_max(idx, src_ap, n_vals):
        p.wait("gpsimd", "gg", p.val("gg"))
        p.dma("gpsimd", d["cc_in"][idx].ap()[0:1, n_vals:8],
              rowbuf[0:1, 144:144 + (8 - n_vals)], "gl_c")
        v = p.inc("gl_c", 16)
        p.emit("gpsimd", lambda e, idx=idx, s=src_ap, n=n_vals: e.dma_start(
            out=d["cc_in"][idx].ap()[0:1, 0:n], in_=s).then_inc(p.sem("gl_c"), 16), kind="dma")
        p.wait("gpsimd", "gl_c", v)
        ccv = p.inc("cc")
        p.emit("gpsimd", lambda e, idx=idx: e.collective_compute(
            "AllReduce", ALU.max, replica_groups=[list(range(8))],
            ins=[d["cc_in"][idx].ap()], outs=[d["cc_out"][idx].ap()]).then_inc(p.sem("cc"), 1),
            kind="dma")
        p.wait("gpsimd", "cc", ccv)
        rv = p.inc("gl_c", 16)
        self_sem = p.sem("gl_c")
        p.emit("gpsimd", lambda e, idx=idx, s=self_sem: e.dma_start(
            out=rowbuf[0:1, 160 + 8 * idx:168 + 8 * idx],
            in_=d["cc_out"][idx].ap()).then_inc(s, 16), kind="dma")
        return rv

    def pcollapse(cols_ap, n, dst_off):
        gv = p.inc("gl_p", 16)
        sp_ = p.sem("gl_p")
        p.emit("gpsimd", lambda e, c=cols_ap, s=sp_: e.dma_start(
            out=d["misc_d"].ap()[0:1, 0:128 * n].rearrange("a (p c) -> (a p) c", c=n),
            in_=c).then_inc(s, 16), kind="dma")
        p.wait("gpsimd", "gl_p", gv)
        gv2 = p.inc("gl_p", 16)
        p.emit("gpsimd", lambda e, n=n, s=sp_: e.dma_start(
            out=rowbuf[0:1, 256:256 + 128 * n],
            in_=d["misc_d"].ap()[0:1, 0:128 * n]).then_inc(s, 16), kind="dma")
        p.wait("vector", "gl_p", gv2)
        for j in range(n):
            p.V(lambda e, j=j, n=n, o=dst_off: e.tensor_reduce(
                rowbuf[0:1, o + j:o + j + 1],
                rowbuf[0:1, 256:256 + 128 * n].rearrange("a (c n) -> a c n", n=n)[:, :, j],
                AX, ALU.max), drain=True)
        return p.sig("vector", "vv")

    xnT = sb(stXN, "xnT", [128, 8 * NT], BF16, side="right")

    xbuf = sb(stA, "xbuf", [128, 2 * H], F32)
    xnml = sb(stA, "xnml", [128, 2 * H], BF16)
    lncols_ = sb(stA, "lncols_", [128, 16], F32)
    bn6_ = sb(stA, "bn6_", [128, 24], F32)
    ptr = psal(stA, "ptr", [128, 2 * 1024], BF16)

    tabio = sb(stA, "tabio", [128, T], F32)
    tabc1 = sb(stA, "tabc1", [128, 8], F32)
    tabth = sb(stA, "tabth", [128, T], F32)
    tabk = sb(stA, "tabk", [128, T], F32)
    tabt1 = sb(stA, "tabt1", [128, T], F32)
    tabsin = sb(stA, "tabsin", [128, T], F32)
    tabcos = sb(stA, "tabcos", [128, T], F32)
    tabs1 = sb(stA, "tabs1", [128, T], F32)
    tabs2 = sb(stA, "tabs2", [128, T], F32)
    tabo = sb(stA, "tabo", [128, T], BF16)

    def emit_tab(ic):
        i0 = ic * 128
        p.gwait("gpsimd", "tabgen_v")
        p.gwait("gpsimd", "tabgen_s")
        p.G(lambda e: e.iota(tabio[:], [[1, T]], channel_multiplier=0,
                             allow_small_or_imprecise_dtypes=True))
        p.G(lambda e, i0=i0: e.iota(tabc1[:, 0:1], [[1, 1]], base=i0, channel_multiplier=1,
                                    allow_small_or_imprecise_dtypes=True))
        gi = p.sig("gpsimd", "gg")
        p.wait("scalar", "gg", gi)
        p.S(lambda e: e.activation(tabc1[:, 1:2], tabc1[:, 0:1], ACT.Exp, scale=-LNF))
        ssc = p.sig("scalar", "sc")
        p.wait("vector", "sc", ssc)
        p.wait("vector", "gg", gi)
        p.V(lambda e: e.tensor_scalar(tabc1[:, 4:5], tabc1[:, 0:1], 2.0 / 1433.6,
                                      409.6 / 1433.6, ALU.mult, ALU.add), drain=True)
        v0l = p.sig("vector", "vv")
        p.wait("scalar", "vv", v0l)
        p.S(lambda e: e.activation(tabc1[:, 2:3], tabc1[:, 4:5], ACT.Ln))
        s0l = p.sig("scalar", "sc")
        p.wait("vector", "sc", s0l)
        p.V(lambda e: e.tensor_scalar(tabc1[:, 3:4], tabc1[:, 2:3], 1.0 / 512, None, ALU.mult))
        p.V(lambda e: e.tensor_scalar(tabth[:], tabio[:], tabc1[:, 1:2], None, ALU.mult), drain=True)
        p.V(lambda e: e.tensor_scalar(tabk[:], tabth[:], INV2PI, MAGIC, ALU.mult, ALU.add), drain=True)
        p.V(lambda e: e.tensor_scalar(tabk[:], tabk[:], MAGIC, None, ALU.subtract), drain=True)
        p.V(lambda e: e.scalar_tensor_tensor(tabt1[:], tabk[:], -C1_2PI, tabth[:], ALU.mult, ALU.add), drain=True)
        p.V(lambda e: e.scalar_tensor_tensor(tabt1[:], tabk[:], -C2_2PI, tabt1[:], ALU.mult, ALU.add), drain=True)
        p.V(lambda e: e.tensor_scalar(tabt1[:], tabt1[:], 3.141592, -3.141592, ALU.min, ALU.max))
        v1 = p.sig("vector", "vv")
        p.wait("scalar", "vv", v1)
        p.S(lambda e: e.activation(tabsin[:], tabt1[:], ACT.Sin))
        s1 = p.sig("scalar", "sc")
        p.wait("vector", "sc", s1)
        p.V(lambda e: e.tensor_scalar(tabk[:], tabth[:], INV2PI, 0.25, ALU.mult, ALU.add), drain=True)
        p.V(lambda e: e.tensor_scalar(tabk[:], tabk[:], MAGIC, MAGIC, ALU.add, ALU.subtract), drain=True)
        p.V(lambda e: e.scalar_tensor_tensor(tabt1[:], tabk[:], -C1_2PI, tabth[:], ALU.mult, ALU.add), drain=True)
        p.V(lambda e: e.scalar_tensor_tensor(tabt1[:], tabk[:], -C2_2PI, tabt1[:], ALU.mult, ALU.add), drain=True)
        p.V(lambda e: e.tensor_scalar(tabt1[:], tabt1[:], HALF_PI, None, ALU.add), drain=True)
        p.V(lambda e: e.tensor_scalar(tabt1[:], tabt1[:], 3.141592, -3.141592, ALU.min, ALU.max))
        p.V(lambda e: e.tensor_scalar(tabs1[:], tabio[:], tabc1[:, 3:4], None, ALU.mult))
        v2 = p.sig("vector", "vv")
        p.wait("scalar", "vv", v2)
        p.S(lambda e: e.activation(tabcos[:], tabt1[:], ACT.Sin))
        p.S(lambda e: e.activation(tabs2[:], tabs1[:], ACT.Exp, scale=-1.0), drain=True)
        p.S(lambda e: e.activation(tabs1[:], tabs1[:], ACT.Exp))
        s2 = p.sig("scalar", "sc")
        p.gset("tabgen_s", "sc", s2)
        p.wait("vector", "sc", s2)
        for j, (trig, scl) in enumerate([(tabcos, tabs1), (tabsin, tabs1),
                                         (tabcos, tabs2), (tabsin, tabs2)]):
            p.V(lambda e, a=trig, bb=scl: e.tensor_tensor(tabo[:], a[:], bb[:], ALU.mult))
            vj = p.sig("vector", "vv")
            p.gset("tabgen_v", "vv", vj)
            p.wait("sync", "vv", vj)
            lv = p.dma("sync", d["tabs"].ap()[j, i0:i0 + 128, :], tabo[:], "ld_tb")
            p.wait("vector", "ld_tb", lv)


    p.wait("tensor", "ld_i", idv)
    for i in range(NT // 128):
        if i > 0 and i % 8 == 0:
            emit_tab(i // 8 - 1)
        b, tr = divmod(i, T // 128)
        bf = i % 2
        xt = xbuf[:, bf * H:(bf + 1) * H]
        xn = xnml[:, bf * H:(bf + 1) * H]
        lncols = lncols_[:, bf * 8:(bf + 1) * 8]
        bn6 = bn6_[:, bf * 12:(bf + 1) * 12]
        p.gwait("sync", "xbuf%d" % bf)
        ldv = p.dma("sync", xt, d["x_in"].ap()[b, tr * 128:(tr + 1) * 128, :], "ld_x%d" % bf)
        p.wait("vector", "ld_x%d" % bf, ldv)
        p.V(lambda e, xt=xt, bn6=bn6: e.bn_stats(bn6[:, 0:6], xt[:, 0:512]))
        p.V(lambda e, xt=xt, bn6=bn6: e.bn_stats(bn6[:, 6:12], xt[:, 512:1024]), drain=True)
        p.V(lambda e, lncols=lncols, bn6=bn6: e.bn_aggr(lncols[:, 0:2], bn6[:, 0:12]), drain=True)
        p.V(lambda e, lncols=lncols: e.tensor_scalar(lncols[:, 1:2], lncols[:, 1:2], EPS_LN, None, ALU.add))
        v1 = p.sig("vector", "vv")
        p.wait("scalar", "vv", v1)
        p.S(lambda e, lncols=lncols: e.activation(lncols[:, 2:3], lncols[:, 1:2], ACT.Sqrt))
        s1 = p.sig("scalar", "sc")
        p.wait("vector", "sc", s1)
        p.V(lambda e, lncols=lncols: e.reciprocal(lncols[:, 3:4], lncols[:, 2:3]), drain=True)
        p.V(lambda e, lncols=lncols: e.scalar_tensor_tensor(lncols[:, 4:5], lncols[:, 0:1], -1.0,
                                                            lncols[:, 3:4], ALU.mult, ALU.mult))
        v2 = p.sig("vector", "vv")
        p.wait("scalar", "vv", v2)
        p.gwait("scalar", "xnml%d" % bf)
        p.S(lambda e, xt=xt, xn=xn, lncols=lncols: e.activation(xn, xt, ACT.Identity,
                                                 bias=lncols[:, 4:5], scale=lncols[:, 3:4]))
        sv = p.sig("scalar", "sc")
        p.gset("xbuf%d" % bf, "sc", sv)
        p.wait("tensor", "sc", sv)
        p.gwait("tensor", "ptr%d" % bf)
        for k in range(8):
            p.TE(lambda e, xn=xn, k=k, bf=bf: e.transpose(
                ptr[:, bf * 1024 + k * 128: bf * 1024 + (k + 1) * 128],
                xn[:, k * 128:(k + 1) * 128], ident[:]))
        tv = p.sig("tensor", "tt")
        p.gset("xnml%d" % bf, "tt", tv)
        p.wait("scalar", "tt", tv)
        for k in range(8):
            p.S(lambda e, k=k, bf=bf, i=i: e.copy(
                xnT[:, k * NT + i * 128: k * NT + (i + 1) * 128],
                ptr[:, bf * 1024 + k * 128: bf * 1024 + (k + 1) * 128]))
        ev = p.sig("scalar", "sc")
        p.gset("ptr%d" % bf, "sc", ev)

    emit_tab(3)
    if "tap_tabs" in DEBUG_TAPS:
        p.dma("sync", d["taps"]["tap_tabs"].ap(),
              d["tabs"].ap().rearrange("a b c -> (a b) c"), "ld_tp")
    if "tap_xnT" in DEBUG_TAPS:
        p.dma("sync", d["taps"]["tap_xnT"].ap(), xnT[:], "ld_tp")

    xnT_sc = p.val("sc")
    p.barrier()
    stA.close()
    if PHASE_LIMIT < 2:
        return

    wqkv = sb(stB, "wqkv", [128, 8 * H], BF16)
    kT = sb(stB, "kT", [128, 8 * T], BF16)
    vtok = sb(stB, "vtok", [128, 16 * H], BF16)
    qT = sb(stB, "qT", [128, 8 * 512], BF16)
    expb = sb(stB, "expb", [128, 16 * 512], BF16)
    ptilb = sb(stB, "ptilb", [128, 16 * 512], BF16)
    tb0 = sb(stB, "tb0", [128, 2 * 512], BF16)
    tb1 = sb(stB, "tb1", [128, 2 * 512], BF16)
    evq = sb(stB, "evq", [128, 2 * 512], BF16)
    rotbuf = sb(stB, "rotbuf", [128, 2 * 512], F32)
    atst = sb(stB, "atst", [128, 512], BF16)
    sumsb = sb(stB, "sumsb", [1, 512], F32, side="right")
    d2s = sb(stB, "d2s", [128, 3968], BF16)
    pmm = psal(stB, "pmm", [128, 2 * 1024], F32)
    ppv = psal(stB, "ppv", [128, 2 * 512], F32)
    psums = psal(stB, "psums", [1, 512], F32)

    p.G(lambda e: e.iota(d2s[:], [[1, 3968]], base=-1920, channel_multiplier=-1,
                         allow_small_or_imprecise_dtypes=True))
    gd2 = p.sig("gpsimd", "gg")
    p.wait("scalar", "gg", gd2)
    p.S(lambda e: e.activation(d2s[:], d2s[:], ACT.Abs), drain=True)
    p.S(lambda e: e.activation(d2s[:], d2s[:], ACT.Exp, scale=LN_GAMMA))
    s_d2 = p.sig("scalar", "sc")
    p.wait("vector", "sc", s_d2)

    def load_w(name, wait_val=None):
        p.wait("gpsimd", "tt", p.val("tt") if wait_val is None else wait_val)
        v = None
        sw = p.sem("gl_w")
        p.dma_streams["gl_w"] = "gpsimd"
        for hc in range(8):
            v = p.inc("gl_w", 16)
            p.emit("gpsimd", lambda e, hc=hc, name=name, s=sw: e.dma_start(
                out=wqkv[:, hc * H:(hc + 1) * H],
                in_=d[name].ap()[hc * 128:(hc + 1) * 128, :]).then_inc(s, 16), kind="dma")
        return v

    def xpos_pair(b, tab_idx, src_tok0, n512, dst, dst_chunk_w, slot_tag):
        for c in range(4):
            for tcb in range(n512):
                sl = (c * n512 + tcb) % 2
                tok_off = src_tok0 + tcb * 512
                p.gwait("sync", "tb%d" % sl)
                lv1 = p.dma("sync", tb0[:, sl * 512:(sl + 1) * 512],
                            d["tabs"].ap()[tab_idx, c * 128:(c + 1) * 128,
                                           (tok_off - b * T):(tok_off - b * T) + 512], "ld_t%d" % sl)
                lv2 = p.dma("sync", tb1[:, sl * 512:(sl + 1) * 512],
                            d["tabs"].ap()[tab_idx + 1, c * 128:(c + 1) * 128,
                                           (tok_off - b * T):(tok_off - b * T) + 512], "ld_t%d" % sl)
                p.gwait("tensor", "pmm%d" % sl)
                for half, cc_ in enumerate((c, c + 4)):
                    for hc in range(8):
                        p.TE(lambda e, cc_=cc_, hc=hc, half=half, sl=sl, tok_off=tok_off: e.matmul(
                            pmm[:, sl * 1024 + half * 512: sl * 1024 + (half + 1) * 512],
                            wqkv[:, hc * H + cc_ * 128: hc * H + (cc_ + 1) * 128],
                            xnT[:, hc * NT + tok_off: hc * NT + tok_off + 512],
                            start=(hc == 0), stop=(hc == 7)))
                tv = p.sig("tensor", "tt")
                p.wait("scalar", "tt", tv)
                p.gwait("scalar", "evq")
                for half in range(2):
                    p.S(lambda e, half=half, sl=sl: e.copy(
                        evq[:, half * 512:(half + 1) * 512],
                        pmm[:, sl * 1024 + half * 512: sl * 1024 + (half + 1) * 512]))
                sv = p.sig("scalar", "sc")
                p.gset("pmm%d" % sl, "sc", sv)
                p.wait("vector", "sc", sv)
                p.wait("vector", "ld_t%d" % sl, lv2)
                r0, r1 = rotbuf[:, 0:512], rotbuf[:, 512:1024]
                Qe, Qo = evq[:, 0:512], evq[:, 512:1024]
                cs = tb0[:, sl * 512:(sl + 1) * 512]
                ss = tb1[:, sl * 512:(sl + 1) * 512]
                o_e = c * dst_chunk_w + tcb * 512
                o_o = (c + 4) * dst_chunk_w + tcb * 512
                p.V(lambda e, cs=cs: e.tensor_tensor(r0, Qe, cs, ALU.mult))
                p.V(lambda e, ss=ss: e.tensor_tensor(r1, Qo, ss, ALU.mult), drain=True)
                p.V(lambda e, o=o_e: e.tensor_tensor(dst[:, o:o + 512], r0, r1, ALU.subtract), drain=True)
                p.V(lambda e, cs=cs: e.tensor_tensor(r0, Qo, cs, ALU.mult))
                p.V(lambda e, ss=ss: e.tensor_tensor(r1, Qe, ss, ALU.mult), drain=True)
                p.V(lambda e, o=o_o: e.tensor_tensor(dst[:, o:o + 512], r0, r1, ALU.add), drain=True)
                ev = p.sig("vector", "vv")
                p.gset("evq", "vv", ev)
                p.gset("tb%d" % sl, "vv", ev)

    for b in range(B):
        tok0 = b * T
        if b == 0:
            p.wait("gpsimd", "gl_x", cast_v)
        wl = load_w("wk16", wait_val=None if b == 0 else wq_free)
        p.wait("tensor", "gl_w", wl)
        if b == 1:
            p.wait("gpsimd", "vv", vmax)
            vloc = pcollapse(mxb[:, 8:9], 1, 130)
            p.wait("gpsimd", "vv", vloc)
            cm = collective_max(0, rowbuf[0:1, 130:131], 1)
            p.wait("vector", "gl_c", cm)
            p.V(lambda e: e.tensor_scalar(rowbuf[0:1, 1:2], rowbuf[0:1, 160:161], 1.0 / QMAX, None, ALU.mult), drain=True)
            p.V(lambda e: e.reciprocal(rowbuf[0:1, 0:1], rowbuf[0:1, 1:2]), drain=True)
            vsx = p.sig("vector", "vv")
        xpos_pair(b, 2, tok0, 4, kT, T, "k")
        p.wait("tensor", "vv", p.val("vv"))
        p.wait("tensor", "sc", p.val("sc"))
        wl = load_w("wv_in")
        p.wait("tensor", "gl_w", wl)
        for j in range(16):
            for vc in range(2):
                sl = (j * 2 + vc) % 2
                p.gwait("tensor", "pmm%d" % sl)
                for hc in range(8):
                    p.TE(lambda e, j=j, vc=vc, hc=hc, sl=sl, tok0=tok0: e.matmul(
                        pmm[:, sl * 1024: sl * 1024 + 512],
                        xnT[:, hc * NT + tok0 + j * 128: hc * NT + tok0 + (j + 1) * 128],
                        wqkv[:, hc * H + vc * 512: hc * H + (vc + 1) * 512],
                        start=(hc == 0), stop=(hc == 7)))
                tv = p.sig("tensor", "tt")
                p.wait("scalar", "tt", tv)
                p.S(lambda e, j=j, vc=vc, sl=sl: e.copy(
                    vtok[:, j * H + vc * 512: j * H + (vc + 1) * 512],
                    pmm[:, sl * 1024: sl * 1024 + 512]))
                sv = p.sig("scalar", "sc")
                p.gset("pmm%d" % sl, "sc", sv)
        p.wait("tensor", "sc", p.val("sc"))
        wl = load_w("wq_in")
        p.wait("tensor", "gl_w", wl)

        if b == 0 and "tap_kT" in DEBUG_TAPS:
            p.wait("sync", "vv", p.val("vv"))
            p.dma("sync", d["taps"]["tap_kT"].ap(), kT[:], "ld_tp")
        if b == 0 and "tap_vtok" in DEBUG_TAPS:
            p.wait("sync", "sc", p.val("sc"))
            p.dma("sync", d["taps"]["tap_vtok"].ap(), vtok[:], "ld_tp")

        xpos_pair(b, 0, tok0, 1, qT, 512, "q")
        if b == 0 and "tap_qT" in DEBUG_TAPS:
            p.wait("sync", "vv", p.val("vv"))
            p.dma("sync", d["taps"]["tap_qT"].ap(), qT[:], "ld_tp")
        qdone = p.val("vv")
        for qc in range(4):
            q0 = qc * 512
            p.wait("tensor", "vv", qdone)

            for kc in range(16):
                sl = kc % 2
                k0 = kc * 128
                p.gwait("tensor", "pmm%d" % sl)
                for dc in range(8):
                    p.TE(lambda e, dc=dc, kc=kc, sl=sl: e.matmul(
                        pmm[:, sl * 1024: sl * 1024 + 512],
                        kT[:, dc * T + kc * 128: dc * T + (kc + 1) * 128],
                        qT[:, dc * 512:(dc + 1) * 512],
                        start=(dc == 0), stop=(dc == 7)))
                tv = p.sig("tensor", "tt")
                p.wait("scalar", "tt", tv)
                p.S(lambda e, kc=kc, sl=sl: e.activation(
                    expb[:, kc * 512:(kc + 1) * 512],
                    pmm[:, sl * 1024: sl * 1024 + 512], ACT.Exp))
                sv = p.sig("scalar", "sc")
                p.gset("pmm%d" % sl, "sc", sv)
                p.wait("vector", "sc", sv)
                p.V(lambda e, kc=kc, k0=k0, q0=q0: e.tensor_tensor(
                    ptilb[:, kc * 512:(kc + 1) * 512], expb[:, kc * 512:(kc + 1) * 512],
                    d2s[:, 1920 + q0 - k0: 1920 + q0 - k0 + 512], ALU.mult))
                pv = p.sig("vector", "vv")
            p.wait("tensor", "sc", p.val("sc"))
            p.gwait("tensor", "psums")
            for kc in range(16):
                p.TE(lambda e, kc=kc: e.matmul(psums[0:1, :], onesb[:, 0:1],
                                               expb[:, kc * 512:(kc + 1) * 512],
                                               start=(kc == 0), stop=(kc == 15)))
            tsum = p.sig("tensor", "tt")
            p.wait("scalar", "tt", tsum)
            p.gwait("scalar", "psums")
            p.gwait("scalar", "sumsb")
            p.S(lambda e: e.copy(sumsb[0:1, :], psums[0:1, :]))
            sv = p.sig("scalar", "sc")
            p.gset("psums", "sc", sv)
            p.wait("vector", "sc", sv)
            p.V(lambda e: e.reciprocal(sumsb[0:1, :], sumsb[0:1, :]))
            vsum = p.sig("vector", "vv")
            p.wait("scalar", "vv", vsum)
            sdq = p.dma("scalar", d["sums_d"].ap()[b:b + 1, q0:q0 + 512],
                        sumsb[0:1, :], "sd_s")
            p.gset("sumsb", "sd_s", sdq)

            pt_done = pv
            if qc < 3:
                xpos_pair(b, 0, tok0 + q0 + 512, 1, qT, 512, "q")
                qdone = p.val("vv")
                if qc == 2:
                    wq_free = p.val("tt")
            p.wait("tensor", "vv", pt_done)
            for vc in range(8):
                sl = vc % 2
                p.gwait("tensor", "ppv%d" % sl)
                for kc in range(16):
                    p.TE(lambda e, vc=vc, kc=kc, sl=sl: e.matmul(
                        ppv[:, sl * 512:(sl + 1) * 512],
                        vtok[:, kc * H + vc * 128: kc * H + (vc + 1) * 128],
                        ptilb[:, kc * 512:(kc + 1) * 512],
                        start=(kc == 0), stop=(kc == 15)))
                tv = p.sig("tensor", "tt")
                p.wait("scalar", "tt", tv)
                p.gwait("scalar", "atst")
                p.S(lambda e, sl=sl: e.copy(atst[:, 0:512],
                                            ppv[:, sl * 512:(sl + 1) * 512]))
                sv = p.sig("scalar", "sc")
                p.gset("ppv%d" % sl, "sc", sv)
                p.wait("scalar", "sc", sv)
                dv = p.dma("scalar", d["attnT_d"].ap()[b, vc * 128:(vc + 1) * 128, q0:q0 + 512],
                           atst[:, 0:512], "sd_a0")
                p.gset("atst", "sd_a0", dv)


        if b == 0:
            p.wait("vector", "sc", xnT_sc)
            p.V(lambda e: e.drain())
            for k in range(8):
                p.V(lambda e, k=k: e.tensor_reduce(mxb[:, k:k + 1], xnT[:, k * NT:(k + 1) * NT],
                                                   AX, ALU.max, apply_absolute_value=True))
            p.V(lambda e: e.drain())
            p.V(lambda e: e.tensor_reduce(mxb[:, 8:9], mxb[:, 0:8], AX, ALU.max))
            vmax = p.sig("vector", "vv")

    p.wait("tensor", "vv", vsx)
    p.wait("tensor", "gg", gset)
    bcast_col(rowbuf[0:1, 0:1], colbuf[:, 0:1])
    bcx = bcast_col(rowbuf[0:1, 1:2], colbuf[:, 1:2])
    p.wait("gpsimd", "sc", bcx)
    if "tap_scales" in DEBUG_TAPS and PHASE_LIMIT < 4:
        p.wait("sync", "sc", p.val("sc"))
        p.dma("sync", d["taps"]["tap_scales"].ap()[0:1, 0:64], rowbuf[0:1, 0:64], "ld_tp")

    p.barrier()
    stB.close()
    if PHASE_LIMIT < 3:
        return

    xqT = sb(stC, "xqT", [128, 8 * NT], BF16)
    stPB = ExitStack()
    rtmp = sb(stPB, "rtmp", [128, NT], F32)
    for k in range(8):
        p.V(lambda e, k=k: e.tensor_scalar(rtmp[:], xnT[:, k * NT:(k + 1) * NT],
                                           colbuf[:, 0:1], MAGIC, ALU.mult, ALU.add), drain=True)
        p.V(lambda e, k=k: e.tensor_scalar(xqT[:, k * NT:(k + 1) * NT], rtmp[:],
                                           MAGIC, colbuf[:, 1:2], ALU.subtract, ALU.mult), drain=True)
    vxq = p.sig("vector", "vv")
    if "tap_xqT" in DEBUG_TAPS:
        p.wait("sync", "vv", vxq)
        p.dma("sync", d["taps"]["tap_xqT"].ap(), xqT[:], "ld_tp")

    p.barrier()
    stXN.close()
    stPB.close()
    if PHASE_LIMIT < 4:
        return

    wst = sb(stC, "wst", [128, 2 * H], BF16)
    wnat = sb(stC, "wnat", [128, 2 * H], F32)
    wsc = sb(stC, "wsc", [128, 256], F32)
    wq_b = sb(stC, "wq_b", [128, 2 * H], F32)
    wqb16 = sb(stC, "wqb16", [128, 4 * H], BF16)
    stg = sb(stC, "stg", [128, 4 * 512], BF16)
    mxc = sb(stC, "mxc", [128, 392], F32)
    pwt = psal(stC, "pwt", [128, 2 * 1024], BF16)
    pug = psal(stC, "pug", [128, 2 * 512], F32)

    def deq_chunk(cb):
        src_name = "wu_in" if cb < 24 else "wg_in"
        ic = cb % 24
        bf = cb % 2
        s4 = cb % 4
        nblk = H // 32
        wn = wnat[:, bf * H: bf * H + H]
        ws0 = bf * 128
        wqb = wq_b[:, bf * H: bf * H + H]
        w16 = wqb16[:, s4 * H: s4 * H + H]
        p.gwait("sync", "wnat%d" % bf)
        lv = p.dma("sync", wn, d[src_name].ap()[ic * 128:(ic + 1) * 128, :], "ld_w%d" % bf)
        p.wait("vector", "ld_w%d" % bf, lv)
        p.gwait("vector", "wqb16%d" % s4)
        p.V(lambda e, wn=wn, nblk=nblk, ws0=ws0: e.tensor_reduce(
            wsc[:, ws0:ws0 + nblk], wn.rearrange("p (b j) -> p b j", j=32), AX, ALU.max,
            apply_absolute_value=True), drain=True)
        p.V(lambda e, nblk=nblk, ws0=ws0: e.tensor_scalar(wsc[:, ws0:ws0 + nblk], wsc[:, ws0:ws0 + nblk],
                                                          1.0 / WQMAX, None, ALU.mult), drain=True)
        p.V(lambda e, nblk=nblk, ws0=ws0: e.reciprocal(wsc[:, ws0 + nblk:ws0 + 2 * nblk],
                                                       wsc[:, ws0:ws0 + nblk]), drain=True)
        p.V(lambda e, wn=wn, nblk=nblk, ws0=ws0, wqb=wqb: e.tensor_tensor(
            wqb.rearrange("p (b j) -> p b j", j=32),
            wn.rearrange("p (b j) -> p b j", j=32),
            bass.AP(wsc, ws0 + nblk, [[256, 128], [1, nblk], [0, 32]]),
            ALU.mult), drain=True)
        vwn = p.sig("vector", "vv")
        p.gset("wnat%d" % bf, "vv", vwn)
        p.V(lambda e, wqb=wqb: e.tensor_scalar(wqb, wqb,
                                               MAGIC, MAGIC, ALU.add, ALU.subtract), drain=True)
        p.V(lambda e, nblk=nblk, ws0=ws0, wqb=wqb, w16=w16: e.tensor_tensor(
            w16.rearrange("p (b j) -> p b j", j=32),
            wqb.rearrange("p (b j) -> p b j", j=32),
            bass.AP(wsc, ws0, [[256, 128], [1, nblk], [0, 32]]),
            ALU.mult))
        return p.sig("vector", "vv")

    def deq_transpose(cb, vw):
        bf = cb % 2
        s4 = cb % 4
        w16 = wqb16[:, s4 * H: s4 * H + H]
        p.wait("tensor", "vv", vw)
        p.gwait("tensor", "pwt%d" % bf)
        for k in range(8):
            p.TE(lambda e, k=k, bf=bf, w16=w16: e.transpose(
                pwt[:, bf * 1024 + k * 128: bf * 1024 + k * 128 + 128],
                w16[:, k * 128:(k + 1) * 128], ident[:]))
        tv = p.sig("tensor", "tt")
        p.gset("wqb16%d" % s4, "tt", tv)
        p.wait("scalar", "tt", tv)
        p.gwait("scalar", "wstw%d" % bf)
        for k in range(8):
            p.S(lambda e, k=k, bf=bf: e.copy(
                wst[:, bf * H + k * 128: bf * H + (k + 1) * 128],
                pwt[:, bf * 1024 + k * 128: bf * 1024 + k * 128 + 128]))
        sv = p.sig("scalar", "sc")
        p.gset("pwt%d" % bf, "sc", sv)
        p.gset("wstr%d" % bf, "sc", sv)

    def upgate_mms(cb):
        bf = cb % 2
        ib = cb % 24
        spname = "sp_up" if cb < 24 else "sp_gate"
        p.gwait("tensor", "wstr%d" % bf)
        for tcol in range(8):
            tile_idx = cb * 8 + tcol
            sl = tile_idx % 4
            p.gwait("tensor", "pug%d" % (tile_idx % 2))
            for hc in range(8):
                p.TE(lambda e, bf=bf, tcol=tcol, hc=hc, sl=sl: e.matmul(
                    pug[:, (sl % 2) * 512:((sl % 2) + 1) * 512],
                    wst[:, bf * H + hc * 128: bf * H + (hc + 1) * 128],
                    xqT[:, hc * NT + tcol * 512: hc * NT + (tcol + 1) * 512],
                    start=(hc == 0), stop=(hc == 7)))
            tv = p.sig("tensor", "tt")
            if tcol == 7:
                p.gset("wstw%d" % bf, "tt", tv)
            p.wait("scalar", "tt", tv)
            p.gwait("scalar", "stgv%d" % sl)
            p.gwait("scalar", "stgd%d" % sl)
            p.S(lambda e, sl=sl: e.copy(stg[:, sl * 512:(sl + 1) * 512],
                                        pug[:, (sl % 2) * 512:((sl % 2) + 1) * 512]))
            sv = p.sig("scalar", "sc")
            p.gset("pug%d" % (tile_idx % 2), "sc", sv)
            p.wait("scalar", "sc", sv)
            dv = p.dma("scalar", d[spname].ap()[ib * 128:(ib + 1) * 128,
                                                tcol * 512:(tcol + 1) * 512],
                       stg[:, sl * 512:(sl + 1) * 512], "sd_p%d" % sl)
            p.wait("vector", "sc", sv)
            p.V(lambda e, sl=sl, col=4 + tile_idx: e.tensor_reduce(
                mxc[:, col:col + 1], stg[:, sl * 512:(sl + 1) * 512],
                AX, ALU.max, apply_absolute_value=True))
            mv = p.sig("vector", "vv")
            p.gset("stgv%d" % sl, "vv", mv)
            p.gset("stgd%d" % sl, "sd_p%d" % sl, dv)

    vws = {}
    for cb in range(52):
        if cb < 48:
            vws[cb] = deq_chunk(cb)
        if 3 <= cb < 51:
            deq_transpose(cb - 3, vws[cb - 3])
        if cb >= 4:
            upgate_mms(cb - 4)

    p.V(lambda e: e.drain(), drain=False)
    p.V(lambda e: e.tensor_reduce(mxc[:, 0:1], mxc[:, 4:196], AX, ALU.max), drain=True)
    p.V(lambda e: e.tensor_reduce(mxc[:, 1:2], mxc[:, 196:388], AX, ALU.max), drain=True)
    vmx = p.sig("vector", "vv")
    p.wait("gpsimd", "vv", vmx)
    pcollapse(mxc[:, 0:1], 1, 132)
    vloc = pcollapse(mxc[:, 1:2], 1, 133)
    p.wait("gpsimd", "vv", vloc)
    cm = collective_max(1, rowbuf[0:1, 132:134], 2)
    p.wait("vector", "gl_c", cm)
    p.V(lambda e: e.tensor_scalar(rowbuf[0:1, 3:4], rowbuf[0:1, 168:169], 1.0 / QMAX, None, ALU.mult), drain=True)
    p.V(lambda e: e.reciprocal(rowbuf[0:1, 2:3], rowbuf[0:1, 3:4]), drain=True)
    p.V(lambda e: e.tensor_scalar(rowbuf[0:1, 5:6], rowbuf[0:1, 169:170], 1.0 / QMAX, None, ALU.mult), drain=True)
    p.V(lambda e: e.reciprocal(rowbuf[0:1, 4:5], rowbuf[0:1, 5:6]), drain=True)
    p.V(lambda e: e.tensor_scalar(rowbuf[0:1, 6:7], rowbuf[0:1, 5:6], 1.0 / 65536.0, None, ALU.mult), drain=True)
    p.V(lambda e: e.tensor_scalar(rowbuf[0:1, 7:8], rowbuf[0:1, 5:6], -MAGIC, None, ALU.mult), drain=True)
    vsc2 = p.sig("vector", "vv")
    p.wait("tensor", "vv", vsc2)
    for src, dstc in ((2, 2), (3, 3), (4, 4), (5, 5), (6, 6), (7, 7)):
        bcast_col(rowbuf[0:1, src:src + 1], colbuf[:, dstc:dstc + 1])

    if "tap_scales" in DEBUG_TAPS:
        p.wait("sync", "sc", p.val("sc"))
        p.dma("sync", d["taps"]["tap_scales"].ap()[0:1, 0:64], rowbuf[0:1, 0:64], "ld_tp")

    p.barrier()
    stC.close()
    if PHASE_LIMIT < 5:
        return

    wdT = sb(stD, "wdT", [128, 24 * H], BF16)
    stDs = ExitStack()
    gin = sb(stDs, "gin", [128, 4 * 2048], BF16)
    t1b_ = sb(stDs, "t1b_", [128, 4 * 2048], F32)
    rb__ = sb(stDs, "rb__", [128, 2048], F32)
    sgb_ = sb(stDs, "sgb_", [128, 4 * 2048], F32)
    f3b_ = sb(stDs, "f3b_", [128, 2048], F32)
    g2o = sb(stDs, "g2o", [128, 4 * 2048], BF16)
    mxd = sb(stDs, "mxd", [128, 104], F32)

    n_sweep = 48
    pend_mx = None
    for ti in range(n_sweep):
        ic, th = divmod(ti, 2)
        bf = ti % 4
        g_t = gin[:, bf * 2048:(bf + 1) * 2048]
        o_t = g2o[:, bf * 2048:(bf + 1) * 2048]
        t1b = t1b_[:, bf * 2048:(bf + 1) * 2048]
        rb_ = rb__[:, 0:2048]
        sgb = sgb_[:, bf * 2048:(bf + 1) * 2048]
        f3b = f3b_[:, 0:2048]
        p.gwait("sync", "gin%d" % bf)
        lv = p.dma("sync", g_t, d["sp_gate"].ap()[ic * 128:(ic + 1) * 128,
                                                  th * 2048:(th + 1) * 2048], "ld_g%d" % bf)
        p.wait("scalar", "ld_g%d" % bf, lv)
        p.gwait("scalar", "pdch%d" % bf)
        p.gwait("scalar", "pdcv%d" % bf)
        p.S(lambda e, g_t=g_t, t1b=t1b: e.activation(t1b, g_t, ACT.Identity,
                                                     scale=colbuf[:, 4:5], bias=colbuf[:, 12:13]))
        p.emit("scalar", lambda e: e.drain(), kind="drain")
        p.S(lambda e, sgb=sgb, t1b=t1b: e.activation(sgb, t1b, ACT.Sigmoid,
                                                     scale=colbuf[:, 5:6], bias=colbuf[:, 7:8]))
        s1 = p.sig("scalar", "sc")
        p.gset("gin%d" % bf, "sc", s1)
        p.wait("vector", "sc", s1)
        p.V(lambda e, rb_=rb_, sgb=sgb: e.tensor_scalar(rb_, sgb, 65536.0, MAGIC, ALU.mult, ALU.add))
        if pend_mx is not None:
            po_t, pc_, pbf = pend_mx
            p.emit("vector", lambda e: e.drain(), kind="drain")
            p.V(lambda e, po_t=po_t, pc_=pc_: e.tensor_reduce(
                mxd[:, pc_:pc_ + 1], po_t, AX, ALU.max, apply_absolute_value=True))
            vmxt = p.sig("vector", "vv")
            p.gset("g2g%d" % pbf, "vv", vmxt)
        else:
            p.emit("vector", lambda e: e.drain(), kind="drain")
        p.V(lambda e, f3b=f3b, rb_=rb_: e.tensor_scalar(f3b, rb_, MAGIC, 65535.0, ALU.subtract, ALU.min), drain=True)
        p.gset("pdcv%d" % bf, "vv", p.val("vv") + 1)
        p.gwait("vector", "g2o%d" % bf)
        p.gwait("vector", "g2g%d" % bf)
        p.V(lambda e, o_t=o_t, t1b=t1b, f3b=f3b: e.scalar_tensor_tensor(
            o_t, t1b, MAGIC, f3b, ALU.subtract, ALU.mult))
        v2 = p.sig("vector", "vv")
        p.gset("pdch%d" % bf, "vv", v2)
        pend_mx = (o_t, 4 + ti, bf)
        p.wait("gpsimd", "vv", v2)
        dv = p.dma("gpsimd", d["sp_g2"].ap()[ic * 128:(ic + 1) * 128,
                                             th * 2048:(th + 1) * 2048], o_t, "sd_g%d" % bf)
        p.gset("g2o%d" % bf, "sd_g%d" % bf, dv)

    po_t, pc_, pbf = pend_mx
    p.V(lambda e: e.drain(), drain=False)
    p.V(lambda e, po_t=po_t, pc_=pc_: e.tensor_reduce(
        mxd[:, pc_:pc_ + 1], po_t, AX, ALU.max, apply_absolute_value=True), drain=True)
    p.V(lambda e: e.tensor_reduce(mxd[:, 0:1], mxd[:, 4:52], AX, ALU.max), drain=True)
    vmx = p.sig("vector", "vv")
    if PHASE_LIMIT < 4.4:
        return
    p.wait("gpsimd", "vv", vmx)
    vloc = pcollapse(mxd[:, 0:1], 1, 136)
    p.wait("gpsimd", "vv", vloc)
    cm = collective_max(2, rowbuf[0:1, 136:137], 1)
    if PHASE_LIMIT < 4.7:
        return
    wdnat = sb(stDw, "wdnat", [128, 2 * I], F32)
    wdsc = sb(stDw, "wdsc", [128, 256], F32)
    wdq = sb(stDw, "wdq", [128, I], F32)
    wdq16 = sb(stDw, "wdq16", [128, I], BF16)
    pwd = psal(stDw, "pwd", [128, 2 * 1024], BF16)
    for hc in range(8):
        bf = hc % 2
        wn = wdnat[:, bf * I:(bf + 1) * I]
        p.gwait("sync", "wdnat%d" % bf)
        lv = p.dma("sync", wn, d["wd_in"].ap()[hc * 128:(hc + 1) * 128, :], "ld_w%d" % bf)
        p.wait("vector", "ld_w%d" % bf, lv)
        p.gwait("vector", "wdq16")
        p.V(lambda e, wn=wn: e.tensor_reduce(
            wdsc[:, 0:96], wn.rearrange("p (b j) -> p b j", j=32), AX, ALU.max,
            apply_absolute_value=True), drain=True)
        p.V(lambda e: e.tensor_scalar(wdsc[:, 0:96], wdsc[:, 0:96], 1.0 / WQMAX, None, ALU.mult), drain=True)
        p.V(lambda e: e.reciprocal(wdsc[:, 96:192], wdsc[:, 0:96]), drain=True)
        p.V(lambda e, wn=wn: e.tensor_tensor(
            wdq[:].rearrange("p (b j) -> p b j", j=32),
            wn.rearrange("p (b j) -> p b j", j=32),
            bass.AP(wdsc, 96, [[256, 128], [1, 96], [0, 32]]),
            ALU.mult), drain=True)
        vwn = p.sig("vector", "vv")
        p.gset("wdnat%d" % bf, "vv", vwn)
        p.V(lambda e: e.tensor_scalar(wdq[:], wdq[:], MAGIC, MAGIC, ALU.add, ALU.subtract), drain=True)
        p.V(lambda e: e.tensor_tensor(
            wdq16[:].rearrange("p (b j) -> p b j", j=32),
            wdq[:].rearrange("p (b j) -> p b j", j=32),
            bass.AP(wdsc, 0, [[256, 128], [1, 96], [0, 32]]),
            ALU.mult))
        vw = p.sig("vector", "vv")
        p.wait("tensor", "vv", vw)
        for k in range(24):
            sl2 = k % 8
            if sl2 == 0:
                p.gwait("tensor", "pwd%d" % ((k // 8) % 2))
            p.TE(lambda e, k=k, sl2=sl2, g=(k // 8) % 2: e.transpose(
                pwd[:, g * 1024 + sl2 * 128: g * 1024 + sl2 * 128 + 128],
                wdq16[:, k * 128:(k + 1) * 128], ident[:]))
            if sl2 == 7 or k == 23:
                tv = p.sig("tensor", "tt")
                p.wait("scalar", "tt", tv)
                kk0 = (k // 8) * 8
                for kk in range(kk0, k + 1):
                    p.S(lambda e, kk=kk, hc=hc, g=(k // 8) % 2: e.copy(
                        wdT[:, kk * H + hc * 128: kk * H + (hc + 1) * 128],
                        pwd[:, g * 1024 + (kk % 8) * 128: g * 1024 + (kk % 8) * 128 + 128]))
                sv = p.sig("scalar", "sc")
                p.gset("pwd%d" % ((k // 8) % 2), "sc", sv)
        p.gset("wdq16", "sc", p.val("sc"))

    if "tap_wdT" in DEBUG_TAPS:
        p.wait("sync", "sc", p.val("sc"))
        p.dma("sync", d["taps"]["tap_wdT"].ap(), wdT[:], "ld_tp")

    p.barrier()
    stDw.close()
    if PHASE_LIMIT < 6:
        return

    p.wait("vector", "gl_c", cm)
    p.V(lambda e: e.tensor_tensor(rowbuf[0:1, 176:177], rowbuf[0:1, 176:177],
                                  rowbuf[0:1, 6:7], ALU.mult), drain=True)
    p.V(lambda e: e.tensor_scalar(rowbuf[0:1, 9:10], rowbuf[0:1, 176:177], 1.0 / QMAX, None, ALU.mult), drain=True)
    p.V(lambda e: e.reciprocal(rowbuf[0:1, 8:9], rowbuf[0:1, 9:10]), drain=True)
    p.V(lambda e: e.tensor_tensor(rowbuf[0:1, 13:14], rowbuf[0:1, 8:9],
                                  rowbuf[0:1, 6:7], ALU.mult), drain=True)
    vsc3 = p.sig("vector", "vv")
    p.wait("tensor", "vv", vsc3)
    bcast_col(rowbuf[0:1, 13:14], colbuf[:, 8:9])
    bc3 = bcast_col(rowbuf[0:1, 9:10], colbuf[:, 9:10])
    p.wait("vector", "sc", bc3)

    stD2a = ExitStack()
    uin = sb(stD2a, "uin", [128, 4 * 2048], BF16)
    pend_mx = None
    for ti in range(n_sweep):
        ic, th = divmod(ti, 2)
        bf = ti % 4
        g_t = gin[:, bf * 2048:(bf + 1) * 2048]
        u_t = uin[:, bf * 2048:(bf + 1) * 2048]
        o_t = g2o[:, bf * 2048:(bf + 1) * 2048]
        t1b = t1b_[:, bf * 2048:(bf + 1) * 2048]
        rb_ = rb__[:, 0:2048]
        sgb = sgb_[:, bf * 2048:(bf + 1) * 2048]
        f3b = f3b_[:, 0:2048]
        p.gwait("sync", "gin%d" % bf)
        lv1 = p.dma("sync", g_t, d["sp_g2"].ap()[ic * 128:(ic + 1) * 128,
                                                 th * 2048:(th + 1) * 2048], "ld_g%d" % bf)
        p.gwait("sync", "uin%d" % bf)
        lv2 = p.dma("sync", u_t, d["sp_up"].ap()[ic * 128:(ic + 1) * 128,
                                                 th * 2048:(th + 1) * 2048], "ld_u%d" % bf)
        p.wait("scalar", "ld_g%d" % bf, lv1)
        p.wait("scalar", "ld_u%d" % bf, lv2)
        p.gwait("scalar", "pdch%d" % bf)
        p.gwait("scalar", "pdcv%d" % bf)
        p.S(lambda e, g_t=g_t, t1b=t1b: e.activation(t1b, g_t, ACT.Identity,
                                                     scale=colbuf[:, 8:9], bias=colbuf[:, 12:13]))
        p.emit("scalar", lambda e: e.drain(), kind="drain")
        p.S(lambda e, u_t=u_t, sgb=sgb: e.activation(sgb, u_t, ACT.Identity,
                                                     scale=colbuf[:, 2:3], bias=colbuf[:, 12:13]))
        s1 = p.sig("scalar", "sc")
        p.gset("gin%d" % bf, "sc", s1)
        p.gset("uin%d" % bf, "sc", s1)
        p.wait("vector", "sc", s1)
        p.V(lambda e, rb_=rb_, t1b=t1b: e.tensor_scalar(rb_, t1b, MAGIC, None, ALU.subtract))
        if pend_mx is not None:
            po_t, pc_, pbf = pend_mx
            p.emit("vector", lambda e: e.drain(), kind="drain")
            p.V(lambda e, po_t=po_t, pc_=pc_: e.tensor_reduce(
                mxd[:, pc_:pc_ + 1], po_t, AX, ALU.max, apply_absolute_value=True))
            vmxt = p.sig("vector", "vv")
            p.gset("g2g%d" % pbf, "vv", vmxt)
        else:
            p.emit("vector", lambda e: e.drain(), kind="drain")
        p.V(lambda e, f3b=f3b, sgb=sgb: e.tensor_scalar(f3b, sgb, MAGIC, None, ALU.subtract), drain=True)
        p.gset("pdcv%d" % bf, "vv", p.val("vv") + 1)
        p.gwait("vector", "g2o%d" % bf)
        p.gwait("vector", "g2g%d" % bf)
        p.V(lambda e, o_t=o_t, rb_=rb_, f3b=f3b: e.tensor_tensor(o_t, rb_, f3b, ALU.mult))
        v2 = p.sig("vector", "vv")
        p.gset("pdch%d" % bf, "vv", v2)
        pend_mx = (o_t, 52 + ti, bf)
        p.wait("gpsimd", "vv", v2)
        dv = p.dma("gpsimd", d["sp_m"].ap()[ic * 128:(ic + 1) * 128,
                                            th * 2048:(th + 1) * 2048], o_t, "sd_g%d" % bf)
        p.gset("g2o%d" % bf, "sd_g%d" % bf, dv)

    po_t, pc_, pbf = pend_mx
    p.V(lambda e: e.drain(), drain=False)
    p.V(lambda e, po_t=po_t, pc_=pc_: e.tensor_reduce(
        mxd[:, pc_:pc_ + 1], po_t, AX, ALU.max, apply_absolute_value=True), drain=True)
    p.V(lambda e: e.tensor_reduce(mxd[:, 1:2], mxd[:, 52:100], AX, ALU.max), drain=True)
    vmx = p.sig("vector", "vv")
    p.wait("gpsimd", "vv", vmx)
    vloc = pcollapse(mxd[:, 1:2], 1, 137)
    p.wait("gpsimd", "vv", vloc)
    cm = collective_max(3, rowbuf[0:1, 137:138], 1)
    p.wait("vector", "gl_c", cm)
    p.V(lambda e: e.tensor_scalar(rowbuf[0:1, 20:21], rowbuf[0:1, 184:185], 1.0 / QMAX, None, ALU.mult), drain=True)
    p.V(lambda e: e.tensor_tensor(rowbuf[0:1, 21:22], rowbuf[0:1, 20:21], rowbuf[0:1, 9:10], ALU.mult), drain=True)
    p.V(lambda e: e.tensor_tensor(rowbuf[0:1, 11:12], rowbuf[0:1, 21:22], rowbuf[0:1, 3:4], ALU.mult), drain=True)
    p.V(lambda e: e.tensor_tensor(rowbuf[0:1, 22:23], rowbuf[0:1, 9:10], rowbuf[0:1, 3:4], ALU.mult), drain=True)
    p.V(lambda e: e.reciprocal(rowbuf[0:1, 23:24], rowbuf[0:1, 11:12]), drain=True)
    p.V(lambda e: e.tensor_tensor(rowbuf[0:1, 10:11], rowbuf[0:1, 22:23], rowbuf[0:1, 23:24], ALU.mult), drain=True)
    vsc4 = p.sig("vector", "vv")
    p.wait("tensor", "vv", vsc4)
    bcast_col(rowbuf[0:1, 10:11], colbuf[:, 10:11])
    bc4 = bcast_col(rowbuf[0:1, 11:12], colbuf[:, 11:12])
    p.wait("vector", "sc", bc4)
    p.barrier()
    stD2a.close()
    stDs.close()
    if PHASE_LIMIT < 7:
        p.barrier()
        return

    oqT_ = sb(stD2, "oqT_", [128, 2 * 24 * 512], BF16)
    oqrs = sb(stD2, "oqrs", [128, 1024], F32)
    minb = sb(stD2, "minb", [128, 4 * 1024], BF16)
    I4 = I * 4
    attin = sb(stD2, "attin", [128, 4 * 1024], BF16)
    xres = sb(stD2, "xres", [128, 4 * 1024], F32)
    accb = sb(stD2, "accb", [128, 2 * 1024], F32)
    outb = sb(stD2, "outb", [128, 2 * 1024], F32)
    rrow = sb(stD2, "rrow", [1, T], F32, side="right")
    rcol1 = sb(stD2, "rcol1", [128, 2], F32)
    pdown = psal(stD2, "pdown", [128, 2 * 1024], F32)
    patt = psal(stD2, "patt", [128, 2 * 1024], BF16)

    pend_store = []
    for b in range(B):
        p.gwait("sync", "rrow")
        rv = p.dma("sync", rrow[0:1, :], d["sums_d"].ap()[b:b + 1, :], "ld_rr")
        p.wait("tensor", "ld_rr", rv)
        for st_ in range(4):
            t0 = st_ * 512
            ssl = st_ % 2
            oqT = oqT_[:, ssl * 24 * 512:(ssl + 1) * 24 * 512]
            p.gwait("vector", "oqT%d" % ssl)
            for icp in range(12):
                bf = (st_ * 12 + icp) % 4
                mt = minb[:, bf * 1024: bf * 1024 + 1024]
                p.gwait("gpsimd", "minb%d" % bf)
                lv = p.dma("gpsimd", mt,
                           bass.AP(d["sp_m"], (2 * icp * 128) * NT + b * T + t0,
                                   [[NT, 128], [128 * NT, 2], [1, 512]]), "ld_m%d" % bf)
                p.wait("vector", "ld_m%d" % bf, lv)
                p.V(lambda e, mt=mt: e.tensor_scalar(oqrs[:, 0:1024], mt, colbuf[:, 10:11],
                                                     MAGIC, ALU.mult, ALU.add), drain=True)
                vm = p.sig("vector", "vv")
                p.gset("minb%d" % bf, "vv", vm)
                p.V(lambda e, icp=icp, oqT=oqT: e.tensor_scalar(
                    oqT[:, icp * 1024:(icp + 1) * 1024],
                    oqrs[:, 0:1024], MAGIC, colbuf[:, 11:12],
                    ALU.subtract, ALU.mult), drain=True)
            voq = p.sig("vector", "vv")
            p.wait("tensor", "vv", voq)
            avs = {}
            xvs = {}
            for ts_ in range(4):
                sl4 = ts_ % 4
                tr = (t0 + ts_ * 128) // 128
                p.gwait("sync", "attin%d" % sl4)
                avs[ts_] = p.dma("sync", attin[:, sl4 * 1024:(sl4 + 1) * 1024],
                                 bass.AP(d["attnT_d"], b * H * T + tr * 128,
                                         [[T, 128], [128 * T, 8], [1, 128]]), "ld_a%d" % sl4)
                p.gwait("sync", "xres%d" % sl4)
                xvs[ts_] = p.dma("sync", xres[:, sl4 * 1024:(sl4 + 1) * 1024],
                                 d["x_in"].ap()[b, tr * 128:(tr + 1) * 128, :], "ld_xr%d" % sl4)
            for ts_ in range(4):
                tc_g = (b * T + t0 + ts_ * 128) // 128
                sl = ts_ % 2
                sl4 = ts_ % 4
                tr = (t0 + ts_ * 128) // 128
                av = avs[ts_]
                xv = xvs[ts_]
                p.wait("tensor", "ld_a%d" % sl4, av)
                p.gwait("tensor", "patt%d" % sl)
                for k in range(8):
                    p.TE(lambda e, k=k, sl=sl, sl4=sl4: e.transpose(
                        patt[:, sl * 1024 + k * 128: sl * 1024 + (k + 1) * 128],
                        attin[:, sl4 * 1024 + k * 128: sl4 * 1024 + (k + 1) * 128], ident[:]))
                p.gwait("tensor", "rcol%d" % sl)
                p.TE(lambda e, tr=tr, sl=sl: e.matmul(
                    pcol[:, 0:1], rrow[0:1, tr * 128:(tr + 1) * 128],
                    onesf[0:1, 0:1], start=True, stop=True))
                tva = p.sig("tensor", "tt")
                p.gwait("tensor", "pdown%d" % sl)
                for dcol in range(2):
                    for ic in range(24):
                        p.TE(lambda e, ic=ic, dcol=dcol, ts_=ts_, sl=sl, oqT=oqT: e.matmul(
                            pdown[:, sl * 1024 + dcol * 512: sl * 1024 + (dcol + 1) * 512],
                            oqT[:, ic * 512 + ts_ * 128: ic * 512 + (ts_ + 1) * 128],
                            wdT[:, ic * H + dcol * 512: ic * H + (dcol + 1) * 512],
                            start=(ic == 0), stop=(ic == 23)))
                tv = p.sig("tensor", "tt")
                p.gset("oqT%d" % ssl, "tt", tv)
                p.wait("scalar", "tt", tva)
                p.S(lambda e, sl=sl: e.copy(rcol1[:, sl:sl + 1], pcol[:, 0:1]))
                sv = p.sig("scalar", "sc")
                p.gset("rrow", "sc", sv)
                p.wait("vector", "sc", sv)
                p.wait("vector", "ld_xr%d" % sl4, xv)
                p.gwait("vector", "outb%d" % sl)
                p.V(lambda e, sl=sl, sl4=sl4: e.scalar_tensor_tensor(
                    accb[:, sl * 1024:(sl + 1) * 1024], patt[:, sl * 1024:(sl + 1) * 1024],
                    rcol1[:, sl:sl + 1],
                    xres[:, sl4 * 1024:(sl4 + 1) * 1024], ALU.mult, ALU.add), drain=True)
                p.wait("vector", "tt", tv)
                p.V(lambda e, sl=sl: e.tensor_tensor(
                    outb[:, sl * 1024:(sl + 1) * 1024], accb[:, sl * 1024:(sl + 1) * 1024],
                    pdown[:, sl * 1024:(sl + 1) * 1024], ALU.add))
                vo = p.sig("vector", "vv")
                p.gset("patt%d" % sl, "vv", vo)
                p.gset("pdown%d" % sl, "vv", vo)
                p.gset("attin%d" % sl4, "vv", vo)
                p.gset("xres%d" % sl4, "vv", vo)
                p.gset("rcol%d" % sl, "vv", vo)
                pend_store.append((vo, b, tr, sl))
                if len(pend_store) > 1:
                    q_vo, q_b, q_tr, q_sl = pend_store.pop(0)
                    p.wait("scalar", "vv", q_vo)
                    q_ov = p.dma("scalar", d["out_t"].ap()[q_b, q_tr * 128:(q_tr + 1) * 128, :],
                                 outb[:, q_sl * 1024:(q_sl + 1) * 1024], "ld_o%d" % q_sl)
                    p.gset("outb%d" % q_sl, "ld_o%d" % q_sl, q_ov)

    for q_vo, q_b, q_tr, q_sl in pend_store:
        p.wait("scalar", "vv", q_vo)
        q_ov = p.dma("scalar", d["out_t"].ap()[q_b, q_tr * 128:(q_tr + 1) * 128, :],
                     outb[:, q_sl * 1024:(q_sl + 1) * 1024], "ld_o%d" % q_sl)
        p.gset("outb%d" % q_sl, "ld_o%d" % q_sl, q_ov)
    p.dma_finalize()
    p.barrier()
    stD2.close()
    stD.close()
    stP.close()



def kernel(L0_attn_X, L0_attn_W_Q, L0_attn_W_K, L0_attn_W_V,
           L0_ffn_w_gate, L0_ffn_w_up, L0_ffn_w_down):
    import ml_dtypes
    perm = np.concatenate([np.arange(0, H, 2), np.arange(1, H, 2)])
    wq = np.ascontiguousarray(np.asarray(L0_attn_W_Q, np.float32)[:, perm])
    wk = np.ascontiguousarray(np.asarray(L0_attn_W_K, np.float32)[:, perm])
    wv = np.ascontiguousarray(np.asarray(L0_attn_W_V, np.float32))
    wu = np.ascontiguousarray(np.asarray(L0_ffn_w_up, np.float32))
    wg = np.ascontiguousarray(np.asarray(L0_ffn_w_gate, np.float32))
    wd = np.ascontiguousarray(np.asarray(L0_ffn_w_down, np.float32))
    X = np.asarray(L0_attn_X, np.float32)
    ident = np.eye(128, dtype=ml_dtypes.bfloat16)

    nc = build_nc()
    in_maps = []
    for c in range(8):
        in_maps.append({"x": np.ascontiguousarray(X[2 * c:2 * c + 2]),
                        "wq": wq, "wk": wk, "wv": wv,
                        "wu": wu, "wg": wg, "wd": wd, "ident": ident})
    res = run_bass_kernel_spmd(nc, in_maps, core_ids=list(range(8)))
    out = np.concatenate([r["out"] for r in res.results], axis=0)
    return out.astype(np.float32)


# version 35
# speedup vs baseline: 1.0204x; 1.0023x over previous
import numpy as np
import concourse.bass as bass
import concourse.mybir as mybir
from concourse.bass_utils import run_bass_kernel_spmd
from contextlib import ExitStack

F32 = mybir.dt.float32
BF16 = mybir.dt.bfloat16
ALU = mybir.AluOpType
ACT = mybir.ActivationFunctionType
AX = mybir.AxisListType.X

MAGIC = 12582912.0
LN_GAMMA = -0.10536051565782628
INV2PI = 0.15915494309189535
C1_2PI = 6.28125
C2_2PI = 0.0019353071795864769
HALF_PI = 1.5707963267948966
LNF = 0.017988946039015984
QMAX = 32767.0
WQMAX = 7.0
EPS_LN = 1e-5

B, T, H, I = 2, 2048, 1024, 3072
NT = B * T

ENGINES = ("sync", "vector", "scalar", "tensor", "gpsimd")
DEBUG_TAPS = set()
PHASE_LIMIT = 7


class P:

    def __init__(self, nc, es):
        self.nc = nc
        self.es = es
        self.q = {e: [] for e in ENGINES}
        self.cnt = {}
        self._sems = {}
        self.guards = {}
        self.dma_streams = {}

    def sem(self, name):
        if name not in self._sems:
            self._sems[name] = self.es.enter_context(self.nc.semaphore(name))
        return self._sems[name]

    def inc(self, name, n=1):
        self.cnt[name] = self.cnt.get(name, 0) + n
        return self.cnt[name]

    def val(self, name):
        return self.cnt.get(name, 0)

    def emit(self, eng, fn, kind="op"):
        self.q[eng].append((fn, kind))

    def wait(self, eng, sem_name, value):
        if value <= 0:
            return
        s = self.sem(sem_name)
        self.emit(eng, lambda e, s=s, v=value: e.wait_ge(s, v), kind="wait")

    def sig(self, eng, sem_name):
        s = self.sem(sem_name)
        v = self.inc(sem_name)
        q = self.q[eng]
        i = len(q) - 1
        while i >= 0 and q[i][1] == "wait":
            i -= 1
        if i >= 0 and q[i][1] == "op":
            fn = q[i][0]
            q[i] = (lambda e, fn=fn, s=s: fn(e).then_inc(s, 1), "pre")
            return v
        if eng == "vector":
            self.emit(eng, lambda e: e.drain(), kind="drain")
        self.emit(eng, lambda e, s=s: e.nop().then_inc(s, 1), kind="pre")
        return v

    def sig_nop(self, eng, sem_name):
        s = self.sem(sem_name)
        v = self.inc(sem_name)
        if eng == "vector":
            self.emit(eng, lambda e: e.drain(), kind="drain")
        self.emit(eng, lambda e, s=s: e.nop().then_inc(s, 1), kind="pre")
        return v

    def dma(self, eng, out, in_, sem_name):
        s = self.sem(sem_name)
        v = self.inc(sem_name, 16)
        self.dma_streams[sem_name] = eng
        self.emit(eng, lambda e, o=out, i=in_, s=s: e.dma_start(out=o, in_=i).then_inc(s, 16),
                  kind="dma")
        return v

    def flush_sig(self, eng):
        return self.sig(eng, "fl_" + eng)

    def dma_finalize(self):
        for name, eng in sorted(self.dma_streams.items()):
            self.wait(eng, name, self.val(name))

    def gwait(self, eng, key):
        g = self.guards.get(key)
        if g is not None:
            self.wait(eng, g[0], g[1])

    def gset(self, key, sem_name, value):
        self.guards[key] = (sem_name, value)

    def barrier(self, engines=ENGINES, skip=()):
        for name, eng in sorted(self.dma_streams.items()):
            if name in skip:
                continue
            self.wait(eng, name, self.val(name))
        s = self.sem("bar")
        tgt = self.inc("bar", len(engines))
        for eng in engines:
            if eng == "vector":
                self.emit(eng, lambda e: e.drain(), kind="drain")
            self.emit(eng, lambda e, s=s: e.nop().then_inc(s, 1), kind="pre")
            self.emit(eng, lambda e, t=tgt, s=s: e.wait_ge(s, t), kind="wait")

    def V(self, fn, drain=False):
        self.emit("vector", fn)
        if drain:
            self.emit("vector", lambda e: e.drain(), kind="drain")

    def S(self, fn, drain=False):
        self.emit("scalar", fn)
        if drain:
            self.emit("scalar", lambda e: e.drain(), kind="drain")

    def TE(self, fn):
        self.emit("tensor", fn)

    def G(self, fn):
        self.emit("gpsimd", fn)


def build_nc():
    nc = bass.Bass("TRN2", target_bir_lowering=False)

    d = {}
    d["x_in"] = nc.dram_tensor("x", [B, T, H], F32, kind="ExternalInput")
    d["wq_in"] = nc.dram_tensor("wq", [H, H], F32, kind="ExternalInput")
    d["wk_in"] = nc.dram_tensor("wk", [H, H], F32, kind="ExternalInput")
    d["wv_in"] = nc.dram_tensor("wv", [H, H], F32, kind="ExternalInput")
    d["wk16"] = nc.dram_tensor("wk16", [H, H], BF16)
    d["wu_in"] = nc.dram_tensor("wu", [I, H], F32, kind="ExternalInput")
    d["wg_in"] = nc.dram_tensor("wg", [I, H], F32, kind="ExternalInput")
    d["wd_in"] = nc.dram_tensor("wd", [H, I], F32, kind="ExternalInput")
    d["ident_in"] = nc.dram_tensor("ident", [128, 128], BF16, kind="ExternalInput")
    d["out_t"] = nc.dram_tensor("out", [B, T, H], F32, kind="ExternalOutput")

    d["tabs"] = nc.dram_tensor("tabs", [4, 512, T], BF16)
    d["attnT_d"] = nc.dram_tensor("attnT_d", [B, H, T], BF16)
    d["sums_d"] = nc.dram_tensor("sums_d", [B, T], F32)
    d["sp_up"] = nc.dram_tensor("sp_up", [I, NT], BF16)
    d["sp_gate"] = nc.dram_tensor("sp_gate", [I, NT], BF16)
    d["sp_g2"] = nc.dram_tensor("sp_g2", [I, NT], BF16)
    d["sp_m"] = nc.dram_tensor("sp_m", [I, NT], BF16)
    d["cc_in"] = [nc.dram_tensor(f"cc_in{i}", [1, 8], F32) for i in range(4)]
    d["cc_out"] = [nc.dram_tensor(f"cc_out{i}", [1, 8], F32) for i in range(4)]
    d["misc_d"] = nc.dram_tensor("misc_d", [8, 256], F32)

    d["taps"] = {}
    for name, shape, dt in [
        ("tap_xnT", [128, 8 * NT], BF16),
        ("tap_xqT", [128, 8 * NT], BF16),
        ("tap_scales", [1, 64], F32),
        ("tap_dtile", [128, 512], BF16),
        ("tap_tabs", [4 * 512, T], BF16),
        ("tap_kT", [128, 8 * T], BF16),
        ("tap_vtok", [128, 16 * H], BF16),
        ("tap_qT", [128, 8 * 512], BF16),
        ("tap_sums", [B, T], F32),
        ("tap_wuT", [128, 8 * I], BF16),
        ("tap_wdT", [128, 24 * H], BF16),
    ]:
        if name in DEBUG_TAPS:
            d["taps"][name] = nc.dram_tensor(name, shape, dt, kind="ExternalOutput")

    with ExitStack() as es0:
        p = P(nc, es0)
        build_program(p, d)
        with nc.Block() as block:
            @block.sync
            def _(e):
                for fn, _k in p.q["sync"]:
                    fn(e)

            @block.vector
            def _(e):
                for fn, _k in p.q["vector"]:
                    fn(e)

            @block.scalar
            def _(e):
                for fn, _k in p.q["scalar"]:
                    fn(e)

            @block.tensor
            def _(e):
                for fn, _k in p.q["tensor"]:
                    fn(e)

            @block.gpsimd
            def _(e):
                for fn, _k in p.q["gpsimd"]:
                    fn(e)
    return nc


def build_program(p, d):
    nc = p.nc
    stP, stXN, stA, stB, stC, stD, stDw, stD2 = (ExitStack() for _ in range(8))

    def sb(st, name, shape, dt, side=None):
        return st.enter_context(nc.sbuf_tensor(name, shape, dt, side=side))

    def psal(st, name, shape, dt=F32):
        return st.enter_context(nc.psum_tensor(name, shape, dt))

    for s in ("tt", "vv", "sc", "gg", "cc"):
        p.sem(s)

    ident = sb(stP, "identity", [128, 128], BF16)
    colbuf = sb(stP, "colbuf", [128, 16], F32)
    rowbuf = sb(stP, "rowbuf", [1, 392], F32)
    mxb = sb(stP, "mxb", [128, 12], F32)
    onesf = sb(stP, "onesf", [1, 128], F32)
    onesb = sb(stP, "onesb", [128, 1], BF16)
    pcol = psal(stP, "pcol", [128, 1], F32)

    idv = p.dma("sync", ident[:], d["ident_in"].ap(), "ld_i")
    cast_v = p.dma("gpsimd", d["wk16"].ap(), d["wk_in"].ap(), "gl_x")
    p.G(lambda e: e.memset(onesf[0:1, :], 1.0))
    p.G(lambda e: e.memset(onesb[:, 0:1], 1.0))
    p.G(lambda e: e.memset(rowbuf[0:1, :], 0.0))
    p.G(lambda e: e.memset(colbuf[:, 12:13], MAGIC))
    gset = p.sig("gpsimd", "gg")

    def bcast_col(src_ap, dst_col):
        p.gwait("tensor", "pcol")
        tv = p.inc("tt")
        p.emit("tensor", lambda e, s=src_ap: e.matmul(pcol[:, 0:1], onesf[0:1, :], s,
                                          start=True, stop=True).then_inc(p.sem("tt"), 1),
               kind="pre")
        p.wait("scalar", "tt", tv)
        p.S(lambda e, dc=dst_col: e.copy(dc, pcol[:, 0:1]))
        sv = p.sig("scalar", "sc")
        p.gset("pcol", "sc", sv)
        return sv

    p.dma_streams["gl_c"] = "gpsimd"
    p.dma_streams["gl_x"] = "gpsimd"
    p.dma_streams["gl_p"] = "gpsimd"
    p.dma_streams["gl_v"] = "gpsimd"

    def collective_max(idx, src_ap, n_vals):
        p.wait("gpsimd", "gg", p.val("gg"))
        p.dma("gpsimd", d["cc_in"][idx].ap()[0:1, n_vals:8],
              rowbuf[0:1, 144:144 + (8 - n_vals)], "gl_c")
        v = p.inc("gl_c", 16)
        p.emit("gpsimd", lambda e, idx=idx, s=src_ap, n=n_vals: e.dma_start(
            out=d["cc_in"][idx].ap()[0:1, 0:n], in_=s).then_inc(p.sem("gl_c"), 16), kind="dma")
        p.wait("gpsimd", "gl_c", v)
        ccv = p.inc("cc")
        p.emit("gpsimd", lambda e, idx=idx: e.collective_compute(
            "AllReduce", ALU.max, replica_groups=[list(range(8))],
            ins=[d["cc_in"][idx].ap()], outs=[d["cc_out"][idx].ap()]).then_inc(p.sem("cc"), 1),
            kind="dma")
        p.wait("gpsimd", "cc", ccv)
        rv = p.inc("gl_c", 16)
        self_sem = p.sem("gl_c")
        p.emit("gpsimd", lambda e, idx=idx, s=self_sem: e.dma_start(
            out=rowbuf[0:1, 160 + 8 * idx:168 + 8 * idx],
            in_=d["cc_out"][idx].ap()).then_inc(s, 16), kind="dma")
        return rv

    def pcollapse(cols_ap, n, dst_off):
        gv = p.inc("gl_p", 16)
        sp_ = p.sem("gl_p")
        p.emit("gpsimd", lambda e, c=cols_ap, s=sp_: e.dma_start(
            out=d["misc_d"].ap()[0:1, 0:128 * n].rearrange("a (p c) -> (a p) c", c=n),
            in_=c).then_inc(s, 16), kind="dma")
        p.wait("gpsimd", "gl_p", gv)
        gv2 = p.inc("gl_p", 16)
        p.emit("gpsimd", lambda e, n=n, s=sp_: e.dma_start(
            out=rowbuf[0:1, 256:256 + 128 * n],
            in_=d["misc_d"].ap()[0:1, 0:128 * n]).then_inc(s, 16), kind="dma")
        p.wait("vector", "gl_p", gv2)
        for j in range(n):
            p.V(lambda e, j=j, n=n, o=dst_off: e.tensor_reduce(
                rowbuf[0:1, o + j:o + j + 1],
                rowbuf[0:1, 256:256 + 128 * n].rearrange("a (c n) -> a c n", n=n)[:, :, j],
                AX, ALU.max), drain=True)
        return p.sig("vector", "vv")

    xnT = sb(stXN, "xnT", [128, 8 * NT], BF16, side="right")

    xbuf = sb(stA, "xbuf", [128, 2 * H], F32)
    xnml = sb(stA, "xnml", [128, 2 * H], BF16)
    lncols_ = sb(stA, "lncols_", [128, 16], F32)
    bn6_ = sb(stA, "bn6_", [128, 24], F32)
    ptr = psal(stA, "ptr", [128, 2 * 1024], BF16)

    tabio = sb(stA, "tabio", [128, T], F32)
    tabc1 = sb(stA, "tabc1", [128, 8], F32)
    tabth = sb(stA, "tabth", [128, T], F32)
    tabk = sb(stA, "tabk", [128, T], F32)
    tabt1 = sb(stA, "tabt1", [128, T], F32)
    tabsin = sb(stA, "tabsin", [128, T], F32)
    tabcos = sb(stA, "tabcos", [128, T], F32)
    tabs1 = sb(stA, "tabs1", [128, T], F32)
    tabs2 = sb(stA, "tabs2", [128, T], F32)
    tabo = sb(stA, "tabo", [128, T], BF16)

    def emit_tab(ic):
        i0 = ic * 128
        p.gwait("gpsimd", "tabgen_v")
        p.gwait("gpsimd", "tabgen_s")
        p.G(lambda e: e.iota(tabio[:], [[1, T]], channel_multiplier=0,
                             allow_small_or_imprecise_dtypes=True))
        p.G(lambda e, i0=i0: e.iota(tabc1[:, 0:1], [[1, 1]], base=i0, channel_multiplier=1,
                                    allow_small_or_imprecise_dtypes=True))
        gi = p.sig("gpsimd", "gg")
        p.wait("scalar", "gg", gi)
        p.S(lambda e: e.activation(tabc1[:, 1:2], tabc1[:, 0:1], ACT.Exp, scale=-LNF))
        ssc = p.sig("scalar", "sc")
        p.wait("vector", "sc", ssc)
        p.wait("vector", "gg", gi)
        p.V(lambda e: e.tensor_scalar(tabc1[:, 4:5], tabc1[:, 0:1], 2.0 / 1433.6,
                                      409.6 / 1433.6, ALU.mult, ALU.add), drain=True)
        v0l = p.sig("vector", "vv")
        p.wait("scalar", "vv", v0l)
        p.S(lambda e: e.activation(tabc1[:, 2:3], tabc1[:, 4:5], ACT.Ln))
        s0l = p.sig("scalar", "sc")
        p.wait("vector", "sc", s0l)
        p.V(lambda e: e.tensor_scalar(tabc1[:, 3:4], tabc1[:, 2:3], 1.0 / 512, None, ALU.mult))
        p.V(lambda e: e.tensor_scalar(tabth[:], tabio[:], tabc1[:, 1:2], None, ALU.mult), drain=True)
        p.V(lambda e: e.tensor_scalar(tabk[:], tabth[:], INV2PI, MAGIC, ALU.mult, ALU.add), drain=True)
        p.V(lambda e: e.tensor_scalar(tabk[:], tabk[:], MAGIC, None, ALU.subtract), drain=True)
        p.V(lambda e: e.scalar_tensor_tensor(tabt1[:], tabk[:], -C1_2PI, tabth[:], ALU.mult, ALU.add), drain=True)
        p.V(lambda e: e.scalar_tensor_tensor(tabt1[:], tabk[:], -C2_2PI, tabt1[:], ALU.mult, ALU.add), drain=True)
        p.V(lambda e: e.tensor_scalar(tabt1[:], tabt1[:], 3.141592, -3.141592, ALU.min, ALU.max))
        v1 = p.sig("vector", "vv")
        p.wait("scalar", "vv", v1)
        p.S(lambda e: e.activation(tabsin[:], tabt1[:], ACT.Sin))
        s1 = p.sig("scalar", "sc")
        p.wait("vector", "sc", s1)
        p.V(lambda e: e.tensor_scalar(tabk[:], tabth[:], INV2PI, 0.25, ALU.mult, ALU.add), drain=True)
        p.V(lambda e: e.tensor_scalar(tabk[:], tabk[:], MAGIC, MAGIC, ALU.add, ALU.subtract), drain=True)
        p.V(lambda e: e.scalar_tensor_tensor(tabt1[:], tabk[:], -C1_2PI, tabth[:], ALU.mult, ALU.add), drain=True)
        p.V(lambda e: e.scalar_tensor_tensor(tabt1[:], tabk[:], -C2_2PI, tabt1[:], ALU.mult, ALU.add), drain=True)
        p.V(lambda e: e.tensor_scalar(tabt1[:], tabt1[:], HALF_PI, None, ALU.add), drain=True)
        p.V(lambda e: e.tensor_scalar(tabt1[:], tabt1[:], 3.141592, -3.141592, ALU.min, ALU.max))
        p.V(lambda e: e.tensor_scalar(tabs1[:], tabio[:], tabc1[:, 3:4], None, ALU.mult))
        v2 = p.sig("vector", "vv")
        p.wait("scalar", "vv", v2)
        p.S(lambda e: e.activation(tabcos[:], tabt1[:], ACT.Sin))
        p.S(lambda e: e.activation(tabs2[:], tabs1[:], ACT.Exp, scale=-1.0), drain=True)
        p.S(lambda e: e.activation(tabs1[:], tabs1[:], ACT.Exp))
        s2 = p.sig("scalar", "sc")
        p.gset("tabgen_s", "sc", s2)
        p.wait("vector", "sc", s2)
        for j, (trig, scl) in enumerate([(tabcos, tabs1), (tabsin, tabs1),
                                         (tabcos, tabs2), (tabsin, tabs2)]):
            p.V(lambda e, a=trig, bb=scl: e.tensor_tensor(tabo[:], a[:], bb[:], ALU.mult))
            vj = p.sig("vector", "vv")
            p.gset("tabgen_v", "vv", vj)
            p.wait("sync", "vv", vj)
            lv = p.dma("sync", d["tabs"].ap()[j, i0:i0 + 128, :], tabo[:], "ld_tb")
            p.wait("vector", "ld_tb", lv)


    p.wait("tensor", "ld_i", idv)
    for i in range(NT // 128):
        if i > 0 and i % 8 == 0:
            emit_tab(i // 8 - 1)
        b, tr = divmod(i, T // 128)
        bf = i % 2
        xt = xbuf[:, bf * H:(bf + 1) * H]
        xn = xnml[:, bf * H:(bf + 1) * H]
        lncols = lncols_[:, bf * 8:(bf + 1) * 8]
        bn6 = bn6_[:, bf * 12:(bf + 1) * 12]
        p.gwait("sync", "xbuf%d" % bf)
        ldv = p.dma("sync", xt, d["x_in"].ap()[b, tr * 128:(tr + 1) * 128, :], "ld_x%d" % bf)
        p.wait("vector", "ld_x%d" % bf, ldv)
        p.V(lambda e, xt=xt, bn6=bn6: e.bn_stats(bn6[:, 0:6], xt[:, 0:512]))
        p.V(lambda e, xt=xt, bn6=bn6: e.bn_stats(bn6[:, 6:12], xt[:, 512:1024]), drain=True)
        p.V(lambda e, lncols=lncols, bn6=bn6: e.bn_aggr(lncols[:, 0:2], bn6[:, 0:12]), drain=True)
        p.V(lambda e, lncols=lncols: e.tensor_scalar(lncols[:, 1:2], lncols[:, 1:2], EPS_LN, None, ALU.add))
        v1 = p.sig("vector", "vv")
        p.wait("scalar", "vv", v1)
        p.S(lambda e, lncols=lncols: e.activation(lncols[:, 2:3], lncols[:, 1:2], ACT.Sqrt))
        s1 = p.sig("scalar", "sc")
        p.wait("vector", "sc", s1)
        p.V(lambda e, lncols=lncols: e.reciprocal(lncols[:, 3:4], lncols[:, 2:3]), drain=True)
        p.V(lambda e, lncols=lncols: e.scalar_tensor_tensor(lncols[:, 4:5], lncols[:, 0:1], -1.0,
                                                            lncols[:, 3:4], ALU.mult, ALU.mult))
        v2 = p.sig("vector", "vv")
        p.wait("scalar", "vv", v2)
        p.gwait("scalar", "xnml%d" % bf)
        p.S(lambda e, xt=xt, xn=xn, lncols=lncols: e.activation(xn, xt, ACT.Identity,
                                                 bias=lncols[:, 4:5], scale=lncols[:, 3:4]))
        sv = p.sig("scalar", "sc")
        p.gset("xbuf%d" % bf, "sc", sv)
        p.wait("tensor", "sc", sv)
        p.gwait("tensor", "ptr%d" % bf)
        for k in range(8):
            p.TE(lambda e, xn=xn, k=k, bf=bf: e.transpose(
                ptr[:, bf * 1024 + k * 128: bf * 1024 + (k + 1) * 128],
                xn[:, k * 128:(k + 1) * 128], ident[:]))
        tv = p.sig("tensor", "tt")
        p.gset("xnml%d" % bf, "tt", tv)
        p.wait("scalar", "tt", tv)
        for k in range(8):
            p.S(lambda e, k=k, bf=bf, i=i: e.copy(
                xnT[:, k * NT + i * 128: k * NT + (i + 1) * 128],
                ptr[:, bf * 1024 + k * 128: bf * 1024 + (k + 1) * 128]))
        ev = p.sig("scalar", "sc")
        p.gset("ptr%d" % bf, "sc", ev)

    emit_tab(3)
    if "tap_tabs" in DEBUG_TAPS:
        p.dma("sync", d["taps"]["tap_tabs"].ap(),
              d["tabs"].ap().rearrange("a b c -> (a b) c"), "ld_tp")
    if "tap_xnT" in DEBUG_TAPS:
        p.dma("sync", d["taps"]["tap_xnT"].ap(), xnT[:], "ld_tp")

    xnT_sc = p.val("sc")
    p.barrier()
    stA.close()
    if PHASE_LIMIT < 2:
        return

    wqkv = sb(stB, "wqkv", [128, 8 * H], BF16)
    kT = sb(stB, "kT", [128, 8 * T], BF16)
    vtok = sb(stB, "vtok", [128, 16 * H], BF16)
    qT = sb(stB, "qT", [128, 8 * 512], BF16)
    expb = sb(stB, "expb", [128, 16 * 512], BF16)
    ptilb = sb(stB, "ptilb", [128, 16 * 512], BF16)
    tb0 = sb(stB, "tb0", [128, 2 * 512], BF16)
    tb1 = sb(stB, "tb1", [128, 2 * 512], BF16)
    evq = sb(stB, "evq", [128, 2 * 512], BF16)
    rotbuf = sb(stB, "rotbuf", [128, 2 * 512], F32)
    atst = sb(stB, "atst", [128, 512], BF16)
    sumsb = sb(stB, "sumsb", [1, 512], F32, side="right")
    d2s = sb(stB, "d2s", [128, 3968], BF16)
    pmm = psal(stB, "pmm", [128, 2 * 1024], F32)
    ppv = psal(stB, "ppv", [128, 2 * 512], F32)
    psums = psal(stB, "psums", [1, 512], F32)

    p.G(lambda e: e.iota(d2s[:], [[1, 3968]], base=-1920, channel_multiplier=-1,
                         allow_small_or_imprecise_dtypes=True))
    gd2 = p.sig("gpsimd", "gg")
    p.wait("scalar", "gg", gd2)
    p.S(lambda e: e.activation(d2s[:], d2s[:], ACT.Abs), drain=True)
    p.S(lambda e: e.activation(d2s[:], d2s[:], ACT.Exp, scale=LN_GAMMA))
    s_d2 = p.sig("scalar", "sc")
    p.wait("vector", "sc", s_d2)

    def load_w(name, wait_val=None):
        p.wait("gpsimd", "tt", p.val("tt") if wait_val is None else wait_val)
        v = None
        sw = p.sem("gl_w")
        p.dma_streams["gl_w"] = "gpsimd"
        for hc in range(8):
            v = p.inc("gl_w", 16)
            p.emit("gpsimd", lambda e, hc=hc, name=name, s=sw: e.dma_start(
                out=wqkv[:, hc * H:(hc + 1) * H],
                in_=d[name].ap()[hc * 128:(hc + 1) * 128, :]).then_inc(s, 16), kind="dma")
        return v

    def xpos_pair(b, tab_idx, src_tok0, n512, dst, dst_chunk_w, slot_tag):
        for c in range(4):
            for tcb in range(n512):
                sl = (c * n512 + tcb) % 2
                tok_off = src_tok0 + tcb * 512
                p.gwait("sync", "tb%d" % sl)
                lv1 = p.dma("sync", tb0[:, sl * 512:(sl + 1) * 512],
                            d["tabs"].ap()[tab_idx, c * 128:(c + 1) * 128,
                                           (tok_off - b * T):(tok_off - b * T) + 512], "ld_t%d" % sl)
                lv2 = p.dma("sync", tb1[:, sl * 512:(sl + 1) * 512],
                            d["tabs"].ap()[tab_idx + 1, c * 128:(c + 1) * 128,
                                           (tok_off - b * T):(tok_off - b * T) + 512], "ld_t%d" % sl)
                p.gwait("tensor", "pmm%d" % sl)
                for half, cc_ in enumerate((c, c + 4)):
                    for hc in range(8):
                        p.TE(lambda e, cc_=cc_, hc=hc, half=half, sl=sl, tok_off=tok_off: e.matmul(
                            pmm[:, sl * 1024 + half * 512: sl * 1024 + (half + 1) * 512],
                            wqkv[:, hc * H + cc_ * 128: hc * H + (cc_ + 1) * 128],
                            xnT[:, hc * NT + tok_off: hc * NT + tok_off + 512],
                            start=(hc == 0), stop=(hc == 7)))
                tv = p.sig("tensor", "tt")
                p.wait("scalar", "tt", tv)
                p.gwait("scalar", "evq")
                for half in range(2):
                    p.S(lambda e, half=half, sl=sl: e.copy(
                        evq[:, half * 512:(half + 1) * 512],
                        pmm[:, sl * 1024 + half * 512: sl * 1024 + (half + 1) * 512]))
                sv = p.sig("scalar", "sc")
                p.gset("pmm%d" % sl, "sc", sv)
                p.wait("vector", "sc", sv)
                p.wait("vector", "ld_t%d" % sl, lv2)
                r0, r1 = rotbuf[:, 0:512], rotbuf[:, 512:1024]
                Qe, Qo = evq[:, 0:512], evq[:, 512:1024]
                cs = tb0[:, sl * 512:(sl + 1) * 512]
                ss = tb1[:, sl * 512:(sl + 1) * 512]
                o_e = c * dst_chunk_w + tcb * 512
                o_o = (c + 4) * dst_chunk_w + tcb * 512
                p.V(lambda e, cs=cs: e.tensor_tensor(r0, Qe, cs, ALU.mult))
                p.V(lambda e, ss=ss: e.tensor_tensor(r1, Qo, ss, ALU.mult), drain=True)
                p.V(lambda e, o=o_e: e.tensor_tensor(dst[:, o:o + 512], r0, r1, ALU.subtract), drain=True)
                p.V(lambda e, cs=cs: e.tensor_tensor(r0, Qo, cs, ALU.mult))
                p.V(lambda e, ss=ss: e.tensor_tensor(r1, Qe, ss, ALU.mult), drain=True)
                p.V(lambda e, o=o_o: e.tensor_tensor(dst[:, o:o + 512], r0, r1, ALU.add), drain=True)
                ev = p.sig("vector", "vv")
                p.gset("evq", "vv", ev)
                p.gset("tb%d" % sl, "vv", ev)

    for b in range(B):
        tok0 = b * T
        if b == 0:
            p.wait("gpsimd", "gl_x", cast_v)
        wl = load_w("wk16", wait_val=None if b == 0 else wq_free)
        p.wait("tensor", "gl_w", wl)
        if b == 1:
            p.wait("gpsimd", "vv", vmax)
            vloc = pcollapse(mxb[:, 8:9], 1, 130)
            p.wait("gpsimd", "vv", vloc)
            cm = collective_max(0, rowbuf[0:1, 130:131], 1)
            p.wait("vector", "gl_c", cm)
            p.V(lambda e: e.tensor_scalar(rowbuf[0:1, 1:2], rowbuf[0:1, 160:161], 1.0 / QMAX, None, ALU.mult), drain=True)
            p.V(lambda e: e.reciprocal(rowbuf[0:1, 0:1], rowbuf[0:1, 1:2]), drain=True)
            vsx = p.sig("vector", "vv")
        xpos_pair(b, 2, tok0, 4, kT, T, "k")
        p.wait("tensor", "vv", p.val("vv"))
        p.wait("tensor", "sc", p.val("sc"))
        wl = load_w("wv_in")
        p.wait("tensor", "gl_w", wl)
        for j in range(16):
            for vc in range(2):
                sl = (j * 2 + vc) % 2
                p.gwait("tensor", "pmm%d" % sl)
                for hc in range(8):
                    p.TE(lambda e, j=j, vc=vc, hc=hc, sl=sl, tok0=tok0: e.matmul(
                        pmm[:, sl * 1024: sl * 1024 + 512],
                        xnT[:, hc * NT + tok0 + j * 128: hc * NT + tok0 + (j + 1) * 128],
                        wqkv[:, hc * H + vc * 512: hc * H + (vc + 1) * 512],
                        start=(hc == 0), stop=(hc == 7)))
                tv = p.sig("tensor", "tt")
                p.wait("scalar", "tt", tv)
                p.S(lambda e, j=j, vc=vc, sl=sl: e.copy(
                    vtok[:, j * H + vc * 512: j * H + (vc + 1) * 512],
                    pmm[:, sl * 1024: sl * 1024 + 512]))
                sv = p.sig("scalar", "sc")
                p.gset("pmm%d" % sl, "sc", sv)
        p.wait("tensor", "sc", p.val("sc"))
        wl = load_w("wq_in")
        p.wait("tensor", "gl_w", wl)

        if b == 0 and "tap_kT" in DEBUG_TAPS:
            p.wait("sync", "vv", p.val("vv"))
            p.dma("sync", d["taps"]["tap_kT"].ap(), kT[:], "ld_tp")
        if b == 0 and "tap_vtok" in DEBUG_TAPS:
            p.wait("sync", "sc", p.val("sc"))
            p.dma("sync", d["taps"]["tap_vtok"].ap(), vtok[:], "ld_tp")

        xpos_pair(b, 0, tok0, 1, qT, 512, "q")
        if b == 0 and "tap_qT" in DEBUG_TAPS:
            p.wait("sync", "vv", p.val("vv"))
            p.dma("sync", d["taps"]["tap_qT"].ap(), qT[:], "ld_tp")
        qdone = p.val("vv")
        for qc in range(4):
            q0 = qc * 512
            p.wait("tensor", "vv", qdone)

            for kc in range(16):
                sl = kc % 2
                k0 = kc * 128
                p.gwait("tensor", "pmm%d" % sl)
                for dc in range(8):
                    p.TE(lambda e, dc=dc, kc=kc, sl=sl: e.matmul(
                        pmm[:, sl * 1024: sl * 1024 + 512],
                        kT[:, dc * T + kc * 128: dc * T + (kc + 1) * 128],
                        qT[:, dc * 512:(dc + 1) * 512],
                        start=(dc == 0), stop=(dc == 7)))
                tv = p.sig("tensor", "tt")
                p.wait("scalar", "tt", tv)
                p.S(lambda e, kc=kc, sl=sl: e.activation(
                    expb[:, kc * 512:(kc + 1) * 512],
                    pmm[:, sl * 1024: sl * 1024 + 512], ACT.Exp))
                sv = p.sig("scalar", "sc")
                p.gset("pmm%d" % sl, "sc", sv)
                p.wait("vector", "sc", sv)
                p.V(lambda e, kc=kc, k0=k0, q0=q0: e.tensor_tensor(
                    ptilb[:, kc * 512:(kc + 1) * 512], expb[:, kc * 512:(kc + 1) * 512],
                    d2s[:, 1920 + q0 - k0: 1920 + q0 - k0 + 512], ALU.mult))
                pv = p.sig("vector", "vv")
            p.wait("tensor", "sc", p.val("sc"))
            p.gwait("tensor", "psums")
            for kc in range(16):
                p.TE(lambda e, kc=kc: e.matmul(psums[0:1, :], onesb[:, 0:1],
                                               expb[:, kc * 512:(kc + 1) * 512],
                                               start=(kc == 0), stop=(kc == 15)))
            tsum = p.sig("tensor", "tt")
            p.wait("scalar", "tt", tsum)
            p.gwait("scalar", "psums")
            p.gwait("scalar", "sumsb")
            p.S(lambda e: e.copy(sumsb[0:1, :], psums[0:1, :]))
            sv = p.sig("scalar", "sc")
            p.gset("psums", "sc", sv)
            p.wait("vector", "sc", sv)
            p.V(lambda e: e.reciprocal(sumsb[0:1, :], sumsb[0:1, :]))
            vsum = p.sig("vector", "vv")
            p.wait("scalar", "vv", vsum)
            sdq = p.dma("scalar", d["sums_d"].ap()[b:b + 1, q0:q0 + 512],
                        sumsb[0:1, :], "sd_s")
            p.gset("sumsb", "sd_s", sdq)

            pt_done = pv
            if qc < 3:
                xpos_pair(b, 0, tok0 + q0 + 512, 1, qT, 512, "q")
                qdone = p.val("vv")
                if qc == 2:
                    wq_free = p.val("tt")
            p.wait("tensor", "vv", pt_done)
            for vc in range(8):
                sl = vc % 2
                p.gwait("tensor", "ppv%d" % sl)
                for kc in range(16):
                    p.TE(lambda e, vc=vc, kc=kc, sl=sl: e.matmul(
                        ppv[:, sl * 512:(sl + 1) * 512],
                        vtok[:, kc * H + vc * 128: kc * H + (vc + 1) * 128],
                        ptilb[:, kc * 512:(kc + 1) * 512],
                        start=(kc == 0), stop=(kc == 15)))
                tv = p.sig("tensor", "tt")
                p.wait("scalar", "tt", tv)
                p.gwait("scalar", "atst")
                p.S(lambda e, sl=sl: e.copy(atst[:, 0:512],
                                            ppv[:, sl * 512:(sl + 1) * 512]))
                sv = p.sig("scalar", "sc")
                p.gset("ppv%d" % sl, "sc", sv)
                p.wait("scalar", "sc", sv)
                dv = p.dma("scalar", d["attnT_d"].ap()[b, vc * 128:(vc + 1) * 128, q0:q0 + 512],
                           atst[:, 0:512], "sd_a0")
                p.gset("atst", "sd_a0", dv)


        if b == 0:
            p.wait("vector", "sc", xnT_sc)
            p.V(lambda e: e.drain())
            for k in range(8):
                p.V(lambda e, k=k: e.tensor_reduce(mxb[:, k:k + 1], xnT[:, k * NT:(k + 1) * NT],
                                                   AX, ALU.max, apply_absolute_value=True))
            p.V(lambda e: e.drain())
            p.V(lambda e: e.tensor_reduce(mxb[:, 8:9], mxb[:, 0:8], AX, ALU.max))
            vmax = p.sig("vector", "vv")

    p.wait("tensor", "vv", vsx)
    p.wait("tensor", "gg", gset)
    bcast_col(rowbuf[0:1, 0:1], colbuf[:, 0:1])
    bcx = bcast_col(rowbuf[0:1, 1:2], colbuf[:, 1:2])
    p.wait("gpsimd", "sc", bcx)
    if "tap_scales" in DEBUG_TAPS and PHASE_LIMIT < 4:
        p.wait("sync", "sc", p.val("sc"))
        p.dma("sync", d["taps"]["tap_scales"].ap()[0:1, 0:64], rowbuf[0:1, 0:64], "ld_tp")

    p.barrier()
    stB.close()
    if PHASE_LIMIT < 3:
        return

    xqT = sb(stC, "xqT", [128, 8 * NT], BF16)
    stPB = ExitStack()
    rtmp = sb(stPB, "rtmp", [128, NT], F32)
    for k in range(8):
        p.V(lambda e, k=k: e.tensor_scalar(rtmp[:], xnT[:, k * NT:(k + 1) * NT],
                                           colbuf[:, 0:1], MAGIC, ALU.mult, ALU.add), drain=True)
        p.V(lambda e, k=k: e.tensor_scalar(xqT[:, k * NT:(k + 1) * NT], rtmp[:],
                                           MAGIC, colbuf[:, 1:2], ALU.subtract, ALU.mult), drain=True)
    vxq = p.sig("vector", "vv")
    if "tap_xqT" in DEBUG_TAPS:
        p.wait("sync", "vv", vxq)
        p.dma("sync", d["taps"]["tap_xqT"].ap(), xqT[:], "ld_tp")

    p.barrier()
    stXN.close()
    stPB.close()
    if PHASE_LIMIT < 4:
        return

    wst = sb(stC, "wst", [128, 2 * H], BF16)
    wnat = sb(stC, "wnat", [128, 2 * H], F32)
    wsc = sb(stC, "wsc", [128, 256], F32)
    wq_b = sb(stC, "wq_b", [128, 2 * H], F32)
    wqb16 = sb(stC, "wqb16", [128, 4 * H], BF16)
    stg = sb(stC, "stg", [128, 4 * 512], BF16)
    mxc = sb(stC, "mxc", [128, 392], F32)
    pwt = psal(stC, "pwt", [128, 2 * 1024], BF16)
    pug = psal(stC, "pug", [128, 2 * 512], F32)

    def deq_chunk(cb):
        src_name = "wu_in" if cb < 24 else "wg_in"
        ic = cb % 24
        bf = cb % 2
        s4 = cb % 4
        nblk = H // 32
        wn = wnat[:, bf * H: bf * H + H]
        ws0 = bf * 128
        wqb = wq_b[:, bf * H: bf * H + H]
        w16 = wqb16[:, s4 * H: s4 * H + H]
        p.gwait("sync", "wnat%d" % bf)
        lv = p.dma("sync", wn, d[src_name].ap()[ic * 128:(ic + 1) * 128, :], "ld_w%d" % bf)
        p.wait("vector", "ld_w%d" % bf, lv)
        p.gwait("vector", "wqb16%d" % s4)
        p.V(lambda e, wn=wn, nblk=nblk, ws0=ws0: e.tensor_reduce(
            wsc[:, ws0:ws0 + nblk], wn.rearrange("p (b j) -> p b j", j=32), AX, ALU.max,
            apply_absolute_value=True), drain=True)
        p.V(lambda e, nblk=nblk, ws0=ws0: e.tensor_scalar(wsc[:, ws0:ws0 + nblk], wsc[:, ws0:ws0 + nblk],
                                                          1.0 / WQMAX, None, ALU.mult), drain=True)
        p.V(lambda e, nblk=nblk, ws0=ws0: e.reciprocal(wsc[:, ws0 + nblk:ws0 + 2 * nblk],
                                                       wsc[:, ws0:ws0 + nblk]), drain=True)
        p.V(lambda e, wn=wn, nblk=nblk, ws0=ws0, wqb=wqb: e.tensor_tensor(
            wqb.rearrange("p (b j) -> p b j", j=32),
            wn.rearrange("p (b j) -> p b j", j=32),
            bass.AP(wsc, ws0 + nblk, [[256, 128], [1, nblk], [0, 32]]),
            ALU.mult), drain=True)
        vwn = p.sig("vector", "vv")
        p.gset("wnat%d" % bf, "vv", vwn)
        p.V(lambda e, wqb=wqb: e.tensor_scalar(wqb, wqb,
                                               MAGIC, MAGIC, ALU.add, ALU.subtract), drain=True)
        p.V(lambda e, nblk=nblk, ws0=ws0, wqb=wqb, w16=w16: e.tensor_tensor(
            w16.rearrange("p (b j) -> p b j", j=32),
            wqb.rearrange("p (b j) -> p b j", j=32),
            bass.AP(wsc, ws0, [[256, 128], [1, nblk], [0, 32]]),
            ALU.mult))
        return p.sig("vector", "vv")

    def deq_transpose(cb, vw):
        bf = cb % 2
        s4 = cb % 4
        w16 = wqb16[:, s4 * H: s4 * H + H]
        p.wait("tensor", "vv", vw)
        p.gwait("tensor", "pwt%d" % bf)
        for k in range(8):
            p.TE(lambda e, k=k, bf=bf, w16=w16: e.transpose(
                pwt[:, bf * 1024 + k * 128: bf * 1024 + k * 128 + 128],
                w16[:, k * 128:(k + 1) * 128], ident[:]))
        tv = p.sig("tensor", "tt")
        p.gset("wqb16%d" % s4, "tt", tv)
        p.wait("scalar", "tt", tv)
        p.gwait("scalar", "wstw%d" % bf)
        for k in range(8):
            p.S(lambda e, k=k, bf=bf: e.copy(
                wst[:, bf * H + k * 128: bf * H + (k + 1) * 128],
                pwt[:, bf * 1024 + k * 128: bf * 1024 + k * 128 + 128]))
        sv = p.sig("scalar", "sc")
        p.gset("pwt%d" % bf, "sc", sv)
        p.gset("wstr%d" % bf, "sc", sv)

    def upgate_mms(cb):
        bf = cb % 2
        ib = cb % 24
        spname = "sp_up" if cb < 24 else "sp_gate"
        p.gwait("tensor", "wstr%d" % bf)
        for tcol in range(8):
            tile_idx = cb * 8 + tcol
            sl = tile_idx % 4
            p.gwait("tensor", "pug%d" % (tile_idx % 2))
            for hc in range(8):
                p.TE(lambda e, bf=bf, tcol=tcol, hc=hc, sl=sl: e.matmul(
                    pug[:, (sl % 2) * 512:((sl % 2) + 1) * 512],
                    wst[:, bf * H + hc * 128: bf * H + (hc + 1) * 128],
                    xqT[:, hc * NT + tcol * 512: hc * NT + (tcol + 1) * 512],
                    start=(hc == 0), stop=(hc == 7)))
            tv = p.sig("tensor", "tt")
            if tcol == 7:
                p.gset("wstw%d" % bf, "tt", tv)
            p.wait("scalar", "tt", tv)
            p.gwait("scalar", "stgv%d" % sl)
            p.gwait("scalar", "stgd%d" % sl)
            p.S(lambda e, sl=sl: e.copy(stg[:, sl * 512:(sl + 1) * 512],
                                        pug[:, (sl % 2) * 512:((sl % 2) + 1) * 512]))
            sv = p.sig("scalar", "sc")
            p.gset("pug%d" % (tile_idx % 2), "sc", sv)
            p.wait("scalar", "sc", sv)
            dv = p.dma("scalar", d[spname].ap()[ib * 128:(ib + 1) * 128,
                                                tcol * 512:(tcol + 1) * 512],
                       stg[:, sl * 512:(sl + 1) * 512], "sd_p%d" % sl)
            p.wait("vector", "sc", sv)
            p.V(lambda e, sl=sl, col=4 + tile_idx: e.tensor_reduce(
                mxc[:, col:col + 1], stg[:, sl * 512:(sl + 1) * 512],
                AX, ALU.max, apply_absolute_value=True))
            mv = p.sig("vector", "vv")
            p.gset("stgv%d" % sl, "vv", mv)
            p.gset("stgd%d" % sl, "sd_p%d" % sl, dv)

    vws = {}
    for cb in range(52):
        if cb < 48:
            vws[cb] = deq_chunk(cb)
        if 3 <= cb < 51:
            deq_transpose(cb - 3, vws[cb - 3])
        if cb >= 4:
            upgate_mms(cb - 4)

    p.V(lambda e: e.drain(), drain=False)
    p.V(lambda e: e.tensor_reduce(mxc[:, 0:1], mxc[:, 4:196], AX, ALU.max), drain=True)
    p.V(lambda e: e.tensor_reduce(mxc[:, 1:2], mxc[:, 196:388], AX, ALU.max), drain=True)
    vmx = p.sig("vector", "vv")
    p.wait("gpsimd", "vv", vmx)
    pcollapse(mxc[:, 0:1], 1, 132)
    vloc = pcollapse(mxc[:, 1:2], 1, 133)
    p.wait("gpsimd", "vv", vloc)
    cm = collective_max(1, rowbuf[0:1, 132:134], 2)
    p.wait("vector", "gl_c", cm)
    p.V(lambda e: e.tensor_scalar(rowbuf[0:1, 3:4], rowbuf[0:1, 168:169], 1.0 / QMAX, None, ALU.mult), drain=True)
    p.V(lambda e: e.reciprocal(rowbuf[0:1, 2:3], rowbuf[0:1, 3:4]), drain=True)
    p.V(lambda e: e.tensor_scalar(rowbuf[0:1, 5:6], rowbuf[0:1, 169:170], 1.0 / QMAX, None, ALU.mult), drain=True)
    p.V(lambda e: e.reciprocal(rowbuf[0:1, 4:5], rowbuf[0:1, 5:6]), drain=True)
    p.V(lambda e: e.tensor_scalar(rowbuf[0:1, 6:7], rowbuf[0:1, 5:6], 1.0 / 65536.0, None, ALU.mult), drain=True)
    p.V(lambda e: e.tensor_scalar(rowbuf[0:1, 7:8], rowbuf[0:1, 5:6], -MAGIC, None, ALU.mult), drain=True)
    vsc2 = p.sig("vector", "vv")
    p.wait("tensor", "vv", vsc2)
    for src, dstc in ((2, 2), (3, 3), (4, 4), (5, 5), (6, 6), (7, 7)):
        bcast_col(rowbuf[0:1, src:src + 1], colbuf[:, dstc:dstc + 1])

    if "tap_scales" in DEBUG_TAPS:
        p.wait("sync", "sc", p.val("sc"))
        p.dma("sync", d["taps"]["tap_scales"].ap()[0:1, 0:64], rowbuf[0:1, 0:64], "ld_tp")

    p.barrier()
    stC.close()
    if PHASE_LIMIT < 5:
        return

    wdT = sb(stD, "wdT", [128, 24 * H], BF16)
    stDs = ExitStack()
    gin = sb(stDs, "gin", [128, 4 * 2048], BF16)
    t1b_ = sb(stDs, "t1b_", [128, 4 * 2048], F32)
    rb__ = sb(stDs, "rb__", [128, 2048], F32)
    sgb_ = sb(stDs, "sgb_", [128, 4 * 2048], F32)
    f3b_ = sb(stDs, "f3b_", [128, 2048], F32)
    g2o = sb(stDs, "g2o", [128, 4 * 2048], BF16)
    mxd = sb(stDs, "mxd", [128, 104], F32)

    n_sweep = 48
    pend_mx = None
    for ti in range(n_sweep):
        ic, th = divmod(ti, 2)
        bf = ti % 4
        g_t = gin[:, bf * 2048:(bf + 1) * 2048]
        o_t = g2o[:, bf * 2048:(bf + 1) * 2048]
        t1b = t1b_[:, bf * 2048:(bf + 1) * 2048]
        rb_ = rb__[:, 0:2048]
        sgb = sgb_[:, bf * 2048:(bf + 1) * 2048]
        f3b = f3b_[:, 0:2048]
        p.gwait("sync", "gin%d" % bf)
        lv = p.dma("sync", g_t, d["sp_gate"].ap()[ic * 128:(ic + 1) * 128,
                                                  th * 2048:(th + 1) * 2048], "ld_g%d" % bf)
        p.wait("scalar", "ld_g%d" % bf, lv)
        p.gwait("scalar", "pdch%d" % bf)
        p.gwait("scalar", "pdcv%d" % bf)
        p.S(lambda e, g_t=g_t, t1b=t1b: e.activation(t1b, g_t, ACT.Identity,
                                                     scale=colbuf[:, 4:5], bias=colbuf[:, 12:13]))
        p.emit("scalar", lambda e: e.drain(), kind="drain")
        p.S(lambda e, sgb=sgb, t1b=t1b: e.activation(sgb, t1b, ACT.Sigmoid,
                                                     scale=colbuf[:, 5:6], bias=colbuf[:, 7:8]))
        s1 = p.sig("scalar", "sc")
        p.gset("gin%d" % bf, "sc", s1)
        p.wait("vector", "sc", s1)
        p.V(lambda e, rb_=rb_, sgb=sgb: e.tensor_scalar(rb_, sgb, 65536.0, MAGIC, ALU.mult, ALU.add))
        if pend_mx is not None:
            po_t, pc_, pbf = pend_mx
            p.emit("vector", lambda e: e.drain(), kind="drain")
            p.V(lambda e, po_t=po_t, pc_=pc_: e.tensor_reduce(
                mxd[:, pc_:pc_ + 1], po_t, AX, ALU.max, apply_absolute_value=True))
            vmxt = p.sig("vector", "vv")
            p.gset("g2g%d" % pbf, "vv", vmxt)
        else:
            p.emit("vector", lambda e: e.drain(), kind="drain")
        p.V(lambda e, f3b=f3b, rb_=rb_: e.tensor_scalar(f3b, rb_, MAGIC, 65535.0, ALU.subtract, ALU.min), drain=True)
        p.gset("pdcv%d" % bf, "vv", p.val("vv") + 1)
        p.gwait("vector", "g2o%d" % bf)
        p.gwait("vector", "g2g%d" % bf)
        p.V(lambda e, o_t=o_t, t1b=t1b, f3b=f3b: e.scalar_tensor_tensor(
            o_t, t1b, MAGIC, f3b, ALU.subtract, ALU.mult))
        v2 = p.sig("vector", "vv")
        p.gset("pdch%d" % bf, "vv", v2)
        pend_mx = (o_t, 4 + ti, bf)
        p.wait("gpsimd", "vv", v2)
        dv = p.dma("gpsimd", d["sp_g2"].ap()[ic * 128:(ic + 1) * 128,
                                             th * 2048:(th + 1) * 2048], o_t, "sd_g%d" % bf)
        p.gset("g2o%d" % bf, "sd_g%d" % bf, dv)

    po_t, pc_, pbf = pend_mx
    p.V(lambda e: e.drain(), drain=False)
    p.V(lambda e, po_t=po_t, pc_=pc_: e.tensor_reduce(
        mxd[:, pc_:pc_ + 1], po_t, AX, ALU.max, apply_absolute_value=True), drain=True)
    p.V(lambda e: e.tensor_reduce(mxd[:, 0:1], mxd[:, 4:52], AX, ALU.max), drain=True)
    vmx = p.sig("vector", "vv")
    if PHASE_LIMIT < 4.4:
        return
    p.wait("gpsimd", "vv", vmx)
    vloc = pcollapse(mxd[:, 0:1], 1, 136)
    p.wait("gpsimd", "vv", vloc)
    cm = collective_max(2, rowbuf[0:1, 136:137], 1)
    if PHASE_LIMIT < 4.7:
        return
    wdnat = sb(stDw, "wdnat", [128, 2 * I], F32)
    wdsc = sb(stDw, "wdsc", [128, 256], F32)
    wdq = sb(stDw, "wdq", [128, I], F32)
    wdq16 = sb(stDw, "wdq16", [128, I], BF16)
    pwd = psal(stDw, "pwd", [128, 2 * 1024], BF16)
    for hc in range(8):
        bf = hc % 2
        wn = wdnat[:, bf * I:(bf + 1) * I]
        p.gwait("sync", "wdnat%d" % bf)
        lv = p.dma("sync", wn, d["wd_in"].ap()[hc * 128:(hc + 1) * 128, :], "ld_w%d" % bf)
        p.wait("vector", "ld_w%d" % bf, lv)
        p.gwait("vector", "wdq16")
        p.V(lambda e, wn=wn: e.tensor_reduce(
            wdsc[:, 0:96], wn.rearrange("p (b j) -> p b j", j=32), AX, ALU.max,
            apply_absolute_value=True), drain=True)
        p.V(lambda e: e.tensor_scalar(wdsc[:, 0:96], wdsc[:, 0:96], 1.0 / WQMAX, None, ALU.mult), drain=True)
        p.V(lambda e: e.reciprocal(wdsc[:, 96:192], wdsc[:, 0:96]), drain=True)
        p.V(lambda e, wn=wn: e.tensor_tensor(
            wdq[:].rearrange("p (b j) -> p b j", j=32),
            wn.rearrange("p (b j) -> p b j", j=32),
            bass.AP(wdsc, 96, [[256, 128], [1, 96], [0, 32]]),
            ALU.mult), drain=True)
        vwn = p.sig("vector", "vv")
        p.gset("wdnat%d" % bf, "vv", vwn)
        p.V(lambda e: e.tensor_scalar(wdq[:], wdq[:], MAGIC, MAGIC, ALU.add, ALU.subtract), drain=True)
        p.V(lambda e: e.tensor_tensor(
            wdq16[:].rearrange("p (b j) -> p b j", j=32),
            wdq[:].rearrange("p (b j) -> p b j", j=32),
            bass.AP(wdsc, 0, [[256, 128], [1, 96], [0, 32]]),
            ALU.mult))
        vw = p.sig("vector", "vv")
        p.wait("tensor", "vv", vw)
        for k in range(24):
            sl2 = k % 8
            if sl2 == 0:
                p.gwait("tensor", "pwd%d" % ((k // 8) % 2))
            p.TE(lambda e, k=k, sl2=sl2, g=(k // 8) % 2: e.transpose(
                pwd[:, g * 1024 + sl2 * 128: g * 1024 + sl2 * 128 + 128],
                wdq16[:, k * 128:(k + 1) * 128], ident[:]))
            if sl2 == 7 or k == 23:
                tv = p.sig("tensor", "tt")
                p.wait("scalar", "tt", tv)
                kk0 = (k // 8) * 8
                for kk in range(kk0, k + 1):
                    p.S(lambda e, kk=kk, hc=hc, g=(k // 8) % 2: e.copy(
                        wdT[:, kk * H + hc * 128: kk * H + (hc + 1) * 128],
                        pwd[:, g * 1024 + (kk % 8) * 128: g * 1024 + (kk % 8) * 128 + 128]))
                sv = p.sig("scalar", "sc")
                p.gset("pwd%d" % ((k // 8) % 2), "sc", sv)
        p.gset("wdq16", "sc", p.val("sc"))

    if "tap_wdT" in DEBUG_TAPS:
        p.wait("sync", "sc", p.val("sc"))
        p.dma("sync", d["taps"]["tap_wdT"].ap(), wdT[:], "ld_tp")

    p.barrier()
    stDw.close()
    if PHASE_LIMIT < 6:
        return

    p.wait("vector", "gl_c", cm)
    p.V(lambda e: e.tensor_tensor(rowbuf[0:1, 176:177], rowbuf[0:1, 176:177],
                                  rowbuf[0:1, 6:7], ALU.mult), drain=True)
    p.V(lambda e: e.tensor_scalar(rowbuf[0:1, 9:10], rowbuf[0:1, 176:177], 1.0 / QMAX, None, ALU.mult), drain=True)
    p.V(lambda e: e.reciprocal(rowbuf[0:1, 8:9], rowbuf[0:1, 9:10]), drain=True)
    p.V(lambda e: e.tensor_tensor(rowbuf[0:1, 13:14], rowbuf[0:1, 8:9],
                                  rowbuf[0:1, 6:7], ALU.mult), drain=True)
    vsc3 = p.sig("vector", "vv")
    p.wait("tensor", "vv", vsc3)
    bcast_col(rowbuf[0:1, 13:14], colbuf[:, 8:9])
    bc3 = bcast_col(rowbuf[0:1, 9:10], colbuf[:, 9:10])
    p.wait("vector", "sc", bc3)

    stD2a = ExitStack()
    uin = sb(stD2a, "uin", [128, 4 * 2048], BF16)
    pend_mx = None
    for ti in range(n_sweep):
        ic, th = divmod(ti, 2)
        bf = ti % 4
        g_t = gin[:, bf * 2048:(bf + 1) * 2048]
        u_t = uin[:, bf * 2048:(bf + 1) * 2048]
        o_t = g2o[:, bf * 2048:(bf + 1) * 2048]
        t1b = t1b_[:, bf * 2048:(bf + 1) * 2048]
        rb_ = rb__[:, 0:2048]
        sgb = sgb_[:, bf * 2048:(bf + 1) * 2048]
        f3b = f3b_[:, 0:2048]
        p.gwait("sync", "gin%d" % bf)
        lv1 = p.dma("sync", g_t, d["sp_g2"].ap()[ic * 128:(ic + 1) * 128,
                                                 th * 2048:(th + 1) * 2048], "ld_g%d" % bf)
        p.gwait("sync", "uin%d" % bf)
        lv2 = p.dma("sync", u_t, d["sp_up"].ap()[ic * 128:(ic + 1) * 128,
                                                 th * 2048:(th + 1) * 2048], "ld_u%d" % bf)
        p.wait("scalar", "ld_g%d" % bf, lv1)
        p.wait("scalar", "ld_u%d" % bf, lv2)
        p.gwait("scalar", "pdch%d" % bf)
        p.gwait("scalar", "pdcv%d" % bf)
        p.S(lambda e, g_t=g_t, t1b=t1b: e.activation(t1b, g_t, ACT.Identity,
                                                     scale=colbuf[:, 8:9], bias=colbuf[:, 12:13]))
        p.emit("scalar", lambda e: e.drain(), kind="drain")
        p.S(lambda e, u_t=u_t, sgb=sgb: e.activation(sgb, u_t, ACT.Identity,
                                                     scale=colbuf[:, 2:3], bias=colbuf[:, 12:13]))
        s1 = p.sig("scalar", "sc")
        p.gset("gin%d" % bf, "sc", s1)
        p.gset("uin%d" % bf, "sc", s1)
        p.wait("vector", "sc", s1)
        p.V(lambda e, rb_=rb_, t1b=t1b: e.tensor_scalar(rb_, t1b, MAGIC, None, ALU.subtract))
        if pend_mx is not None:
            po_t, pc_, pbf = pend_mx
            p.emit("vector", lambda e: e.drain(), kind="drain")
            p.V(lambda e, po_t=po_t, pc_=pc_: e.tensor_reduce(
                mxd[:, pc_:pc_ + 1], po_t, AX, ALU.max, apply_absolute_value=True))
            vmxt = p.sig("vector", "vv")
            p.gset("g2g%d" % pbf, "vv", vmxt)
        else:
            p.emit("vector", lambda e: e.drain(), kind="drain")
        p.V(lambda e, f3b=f3b, sgb=sgb: e.tensor_scalar(f3b, sgb, MAGIC, None, ALU.subtract), drain=True)
        p.gset("pdcv%d" % bf, "vv", p.val("vv") + 1)
        p.gwait("vector", "g2o%d" % bf)
        p.gwait("vector", "g2g%d" % bf)
        p.V(lambda e, o_t=o_t, rb_=rb_, f3b=f3b: e.tensor_tensor(o_t, rb_, f3b, ALU.mult))
        v2 = p.sig("vector", "vv")
        p.gset("pdch%d" % bf, "vv", v2)
        pend_mx = (o_t, 52 + ti, bf)
        p.wait("gpsimd", "vv", v2)
        dv = p.dma("gpsimd", d["sp_m"].ap()[ic * 128:(ic + 1) * 128,
                                            th * 2048:(th + 1) * 2048], o_t, "sd_g%d" % bf)
        p.gset("g2o%d" % bf, "sd_g%d" % bf, dv)

    po_t, pc_, pbf = pend_mx
    p.V(lambda e: e.drain(), drain=False)
    p.V(lambda e, po_t=po_t, pc_=pc_: e.tensor_reduce(
        mxd[:, pc_:pc_ + 1], po_t, AX, ALU.max, apply_absolute_value=True), drain=True)
    p.V(lambda e: e.tensor_reduce(mxd[:, 1:2], mxd[:, 52:100], AX, ALU.max), drain=True)
    vmx = p.sig("vector", "vv")
    p.wait("gpsimd", "vv", vmx)
    vloc = pcollapse(mxd[:, 1:2], 1, 137)
    p.wait("gpsimd", "vv", vloc)
    cm = collective_max(3, rowbuf[0:1, 137:138], 1)
    p.wait("vector", "gl_c", cm)
    p.V(lambda e: e.tensor_scalar(rowbuf[0:1, 20:21], rowbuf[0:1, 184:185], 1.0 / QMAX, None, ALU.mult), drain=True)
    p.V(lambda e: e.tensor_tensor(rowbuf[0:1, 21:22], rowbuf[0:1, 20:21], rowbuf[0:1, 9:10], ALU.mult), drain=True)
    p.V(lambda e: e.tensor_tensor(rowbuf[0:1, 11:12], rowbuf[0:1, 21:22], rowbuf[0:1, 3:4], ALU.mult), drain=True)
    p.V(lambda e: e.tensor_tensor(rowbuf[0:1, 22:23], rowbuf[0:1, 9:10], rowbuf[0:1, 3:4], ALU.mult), drain=True)
    p.V(lambda e: e.reciprocal(rowbuf[0:1, 23:24], rowbuf[0:1, 11:12]), drain=True)
    p.V(lambda e: e.tensor_tensor(rowbuf[0:1, 10:11], rowbuf[0:1, 22:23], rowbuf[0:1, 23:24], ALU.mult), drain=True)
    vsc4 = p.sig("vector", "vv")
    p.wait("tensor", "vv", vsc4)
    bcast_col(rowbuf[0:1, 10:11], colbuf[:, 10:11])
    bc4 = bcast_col(rowbuf[0:1, 11:12], colbuf[:, 11:12])
    p.wait("vector", "sc", bc4)
    p.barrier()
    stD2a.close()
    stDs.close()
    if PHASE_LIMIT < 7:
        p.barrier()
        return

    oqT_ = sb(stD2, "oqT_", [128, 2 * 24 * 512], BF16)
    oqrs = sb(stD2, "oqrs", [128, 2 * 1024], F32)
    minb = sb(stD2, "minb", [128, 4 * 1024], BF16)
    I4 = I * 4
    attin = sb(stD2, "attin", [128, 4 * 1024], BF16)
    xres = sb(stD2, "xres", [128, 4 * 1024], F32)
    accb = sb(stD2, "accb", [128, 2 * 1024], F32)
    outb = sb(stD2, "outb", [128, 2 * 1024], F32)
    rrow = sb(stD2, "rrow", [1, T], F32, side="right")
    rcol1 = sb(stD2, "rcol1", [128, 2], F32)
    pdown = psal(stD2, "pdown", [128, 2 * 1024], F32)
    patt = psal(stD2, "patt", [128, 2 * 1024], BF16)

    pend_store = []
    for b in range(B):
        p.gwait("sync", "rrow")
        rv = p.dma("sync", rrow[0:1, :], d["sums_d"].ap()[b:b + 1, :], "ld_rr")
        p.wait("tensor", "ld_rr", rv)
        for st_ in range(4):
            t0 = st_ * 512
            ssl = st_ % 2
            oqT = oqT_[:, ssl * 24 * 512:(ssl + 1) * 24 * 512]
            p.gwait("vector", "oqT%d" % ssl)
            pend_oq = None
            for icp in range(12):
                bf = (st_ * 12 + icp) % 4
                o2 = icp % 2
                mt = minb[:, bf * 1024: bf * 1024 + 1024]
                p.gwait("gpsimd", "minb%d" % bf)
                lv = p.dma("gpsimd", mt,
                           bass.AP(d["sp_m"], (2 * icp * 128) * NT + b * T + t0,
                                   [[NT, 128], [128 * NT, 2], [1, 512]]), "ld_m%d" % bf)
                p.wait("vector", "ld_m%d" % bf, lv)
                p.V(lambda e, mt=mt, o2=o2: e.tensor_scalar(
                    oqrs[:, o2 * 1024:(o2 + 1) * 1024], mt, colbuf[:, 10:11],
                    MAGIC, ALU.mult, ALU.add))
                vm = p.sig("vector", "vv")
                p.gset("minb%d" % bf, "vv", vm)
                if pend_oq is not None:
                    p.emit("vector", lambda e: e.drain(), kind="drain")
                    pi, po = pend_oq
                    p.V(lambda e, pi=pi, po=po, oqT=oqT: e.tensor_scalar(
                        oqT[:, pi * 1024:(pi + 1) * 1024],
                        oqrs[:, po * 1024:(po + 1) * 1024], MAGIC, colbuf[:, 11:12],
                        ALU.subtract, ALU.mult))
                pend_oq = (icp, o2)
            p.emit("vector", lambda e: e.drain(), kind="drain")
            pi, po = pend_oq
            p.V(lambda e, pi=pi, po=po, oqT=oqT: e.tensor_scalar(
                oqT[:, pi * 1024:(pi + 1) * 1024],
                oqrs[:, po * 1024:(po + 1) * 1024], MAGIC, colbuf[:, 11:12],
                ALU.subtract, ALU.mult), drain=True)
            voq = p.sig("vector", "vv")
            p.wait("tensor", "vv", voq)
            avs = {}
            xvs = {}
            for ts_ in range(4):
                sl4 = ts_ % 4
                tr = (t0 + ts_ * 128) // 128
                p.gwait("sync", "attin%d" % sl4)
                avs[ts_] = p.dma("sync", attin[:, sl4 * 1024:(sl4 + 1) * 1024],
                                 bass.AP(d["attnT_d"], b * H * T + tr * 128,
                                         [[T, 128], [128 * T, 8], [1, 128]]), "ld_a%d" % sl4)
                p.gwait("sync", "xres%d" % sl4)
                xvs[ts_] = p.dma("sync", xres[:, sl4 * 1024:(sl4 + 1) * 1024],
                                 d["x_in"].ap()[b, tr * 128:(tr + 1) * 128, :], "ld_xr%d" % sl4)
            for ts_ in range(4):
                tc_g = (b * T + t0 + ts_ * 128) // 128
                sl = ts_ % 2
                sl4 = ts_ % 4
                tr = (t0 + ts_ * 128) // 128
                av = avs[ts_]
                xv = xvs[ts_]
                p.gwait("tensor", "pdown%d" % sl)
                for dcol in range(2):
                    for ic in range(24):
                        p.TE(lambda e, ic=ic, dcol=dcol, ts_=ts_, sl=sl, oqT=oqT: e.matmul(
                            pdown[:, sl * 1024 + dcol * 512: sl * 1024 + (dcol + 1) * 512],
                            oqT[:, ic * 512 + ts_ * 128: ic * 512 + (ts_ + 1) * 128],
                            wdT[:, ic * H + dcol * 512: ic * H + (dcol + 1) * 512],
                            start=(ic == 0), stop=(ic == 23)))
                p.wait("tensor", "ld_a%d" % sl4, av)
                p.gwait("tensor", "patt%d" % sl)
                for k in range(8):
                    p.TE(lambda e, k=k, sl=sl, sl4=sl4: e.transpose(
                        patt[:, sl * 1024 + k * 128: sl * 1024 + (k + 1) * 128],
                        attin[:, sl4 * 1024 + k * 128: sl4 * 1024 + (k + 1) * 128], ident[:]))
                p.gwait("tensor", "rcol%d" % sl)
                p.TE(lambda e, tr=tr, sl=sl: e.matmul(
                    pcol[:, 0:1], rrow[0:1, tr * 128:(tr + 1) * 128],
                    onesf[0:1, 0:1], start=True, stop=True))
                tv = p.sig("tensor", "tt")
                p.gset("oqT%d" % ssl, "tt", tv)
                p.wait("scalar", "tt", tv)
                p.S(lambda e, sl=sl: e.copy(rcol1[:, sl:sl + 1], pcol[:, 0:1]))
                sv = p.sig("scalar", "sc")
                p.gset("rrow", "sc", sv)
                p.wait("vector", "sc", sv)
                p.wait("vector", "tt", tv)
                p.wait("vector", "ld_xr%d" % sl4, xv)
                p.gwait("vector", "outb%d" % sl)
                p.V(lambda e, sl=sl, sl4=sl4: e.scalar_tensor_tensor(
                    accb[:, sl * 1024:(sl + 1) * 1024], patt[:, sl * 1024:(sl + 1) * 1024],
                    rcol1[:, sl:sl + 1],
                    xres[:, sl4 * 1024:(sl4 + 1) * 1024], ALU.mult, ALU.add), drain=True)
                p.V(lambda e, sl=sl: e.tensor_tensor(
                    outb[:, sl * 1024:(sl + 1) * 1024], accb[:, sl * 1024:(sl + 1) * 1024],
                    pdown[:, sl * 1024:(sl + 1) * 1024], ALU.add))
                vo = p.sig("vector", "vv")
                p.gset("patt%d" % sl, "vv", vo)
                p.gset("pdown%d" % sl, "vv", vo)
                p.gset("attin%d" % sl4, "vv", vo)
                p.gset("xres%d" % sl4, "vv", vo)
                p.gset("rcol%d" % sl, "vv", vo)
                pend_store.append((vo, b, tr, sl))
                if len(pend_store) > 1:
                    q_vo, q_b, q_tr, q_sl = pend_store.pop(0)
                    p.wait("scalar", "vv", q_vo)
                    q_ov = p.dma("scalar", d["out_t"].ap()[q_b, q_tr * 128:(q_tr + 1) * 128, :],
                                 outb[:, q_sl * 1024:(q_sl + 1) * 1024], "ld_o%d" % q_sl)
                    p.gset("outb%d" % q_sl, "ld_o%d" % q_sl, q_ov)

    for q_vo, q_b, q_tr, q_sl in pend_store:
        p.wait("scalar", "vv", q_vo)
        q_ov = p.dma("scalar", d["out_t"].ap()[q_b, q_tr * 128:(q_tr + 1) * 128, :],
                     outb[:, q_sl * 1024:(q_sl + 1) * 1024], "ld_o%d" % q_sl)
        p.gset("outb%d" % q_sl, "ld_o%d" % q_sl, q_ov)
    p.dma_finalize()
    p.barrier()
    stD2.close()
    stD.close()
    stP.close()



def kernel(L0_attn_X, L0_attn_W_Q, L0_attn_W_K, L0_attn_W_V,
           L0_ffn_w_gate, L0_ffn_w_up, L0_ffn_w_down):
    import ml_dtypes
    perm = np.concatenate([np.arange(0, H, 2), np.arange(1, H, 2)])
    wq = np.ascontiguousarray(np.asarray(L0_attn_W_Q, np.float32)[:, perm])
    wk = np.ascontiguousarray(np.asarray(L0_attn_W_K, np.float32)[:, perm])
    wv = np.ascontiguousarray(np.asarray(L0_attn_W_V, np.float32))
    wu = np.ascontiguousarray(np.asarray(L0_ffn_w_up, np.float32))
    wg = np.ascontiguousarray(np.asarray(L0_ffn_w_gate, np.float32))
    wd = np.ascontiguousarray(np.asarray(L0_ffn_w_down, np.float32))
    X = np.asarray(L0_attn_X, np.float32)
    ident = np.eye(128, dtype=ml_dtypes.bfloat16)

    nc = build_nc()
    in_maps = []
    for c in range(8):
        in_maps.append({"x": np.ascontiguousarray(X[2 * c:2 * c + 2]),
                        "wq": wq, "wk": wk, "wv": wv,
                        "wu": wu, "wg": wg, "wd": wd, "ident": ident})
    res = run_bass_kernel_spmd(nc, in_maps, core_ids=list(range(8)))
    out = np.concatenate([r["out"] for r in res.results], axis=0)
    return out.astype(np.float32)


# version 36
# speedup vs baseline: 1.0255x; 1.0050x over previous
import numpy as np
import concourse.bass as bass
import concourse.mybir as mybir
from concourse.bass_utils import run_bass_kernel_spmd
from contextlib import ExitStack

F32 = mybir.dt.float32
BF16 = mybir.dt.bfloat16
ALU = mybir.AluOpType
ACT = mybir.ActivationFunctionType
AX = mybir.AxisListType.X

MAGIC = 12582912.0
LN_GAMMA = -0.10536051565782628
INV2PI = 0.15915494309189535
C1_2PI = 6.28125
C2_2PI = 0.0019353071795864769
HALF_PI = 1.5707963267948966
LNF = 0.017988946039015984
QMAX = 32767.0
WQMAX = 7.0
EPS_LN = 1e-5

B, T, H, I = 2, 2048, 1024, 3072
NT = B * T

ENGINES = ("sync", "vector", "scalar", "tensor", "gpsimd")
DEBUG_TAPS = set()
PHASE_LIMIT = 7


class P:

    def __init__(self, nc, es):
        self.nc = nc
        self.es = es
        self.q = {e: [] for e in ENGINES}
        self.cnt = {}
        self._sems = {}
        self.guards = {}
        self.dma_streams = {}

    def sem(self, name):
        if name not in self._sems:
            self._sems[name] = self.es.enter_context(self.nc.semaphore(name))
        return self._sems[name]

    def inc(self, name, n=1):
        self.cnt[name] = self.cnt.get(name, 0) + n
        return self.cnt[name]

    def val(self, name):
        return self.cnt.get(name, 0)

    def emit(self, eng, fn, kind="op"):
        self.q[eng].append((fn, kind))

    def wait(self, eng, sem_name, value):
        if value <= 0:
            return
        s = self.sem(sem_name)
        self.emit(eng, lambda e, s=s, v=value: e.wait_ge(s, v), kind="wait")

    def sig(self, eng, sem_name):
        s = self.sem(sem_name)
        v = self.inc(sem_name)
        q = self.q[eng]
        i = len(q) - 1
        while i >= 0 and q[i][1] == "wait":
            i -= 1
        if i >= 0 and q[i][1] == "op":
            fn = q[i][0]
            q[i] = (lambda e, fn=fn, s=s: fn(e).then_inc(s, 1), "pre")
            return v
        if eng == "vector":
            self.emit(eng, lambda e: e.drain(), kind="drain")
        self.emit(eng, lambda e, s=s: e.nop().then_inc(s, 1), kind="pre")
        return v

    def sig_nop(self, eng, sem_name):
        s = self.sem(sem_name)
        v = self.inc(sem_name)
        if eng == "vector":
            self.emit(eng, lambda e: e.drain(), kind="drain")
        self.emit(eng, lambda e, s=s: e.nop().then_inc(s, 1), kind="pre")
        return v

    def dma(self, eng, out, in_, sem_name):
        s = self.sem(sem_name)
        v = self.inc(sem_name, 16)
        self.dma_streams[sem_name] = eng
        self.emit(eng, lambda e, o=out, i=in_, s=s: e.dma_start(out=o, in_=i).then_inc(s, 16),
                  kind="dma")
        return v

    def flush_sig(self, eng):
        return self.sig(eng, "fl_" + eng)

    def dma_finalize(self):
        for name, eng in sorted(self.dma_streams.items()):
            self.wait(eng, name, self.val(name))

    def gwait(self, eng, key):
        g = self.guards.get(key)
        if g is not None:
            self.wait(eng, g[0], g[1])

    def gset(self, key, sem_name, value):
        self.guards[key] = (sem_name, value)

    def barrier(self, engines=ENGINES, skip=()):
        for name, eng in sorted(self.dma_streams.items()):
            if name in skip:
                continue
            self.wait(eng, name, self.val(name))
        s = self.sem("bar")
        tgt = self.inc("bar", len(engines))
        for eng in engines:
            if eng == "vector":
                self.emit(eng, lambda e: e.drain(), kind="drain")
            self.emit(eng, lambda e, s=s: e.nop().then_inc(s, 1), kind="pre")
            self.emit(eng, lambda e, t=tgt, s=s: e.wait_ge(s, t), kind="wait")

    def V(self, fn, drain=False):
        self.emit("vector", fn)
        if drain:
            self.emit("vector", lambda e: e.drain(), kind="drain")

    def S(self, fn, drain=False):
        self.emit("scalar", fn)
        if drain:
            self.emit("scalar", lambda e: e.drain(), kind="drain")

    def TE(self, fn):
        self.emit("tensor", fn)

    def G(self, fn):
        self.emit("gpsimd", fn)


def build_nc():
    nc = bass.Bass("TRN2", target_bir_lowering=False)

    d = {}
    d["x_in"] = nc.dram_tensor("x", [B, T, H], F32, kind="ExternalInput")
    d["wq_in"] = nc.dram_tensor("wq", [H, H], F32, kind="ExternalInput")
    d["wk_in"] = nc.dram_tensor("wk", [H, H], F32, kind="ExternalInput")
    d["wv_in"] = nc.dram_tensor("wv", [H, H], F32, kind="ExternalInput")
    d["wk16"] = nc.dram_tensor("wk16", [H, H], BF16)
    d["wu_in"] = nc.dram_tensor("wu", [I, H], F32, kind="ExternalInput")
    d["wg_in"] = nc.dram_tensor("wg", [I, H], F32, kind="ExternalInput")
    d["wd_in"] = nc.dram_tensor("wd", [H, I], F32, kind="ExternalInput")
    d["ident_in"] = nc.dram_tensor("ident", [128, 128], BF16, kind="ExternalInput")
    d["out_t"] = nc.dram_tensor("out", [B, T, H], F32, kind="ExternalOutput")

    d["tabs"] = nc.dram_tensor("tabs", [4, 512, T], BF16)
    d["attnT_d"] = nc.dram_tensor("attnT_d", [B, H, T], BF16)
    d["sums_d"] = nc.dram_tensor("sums_d", [B, T], F32)
    d["sp_up"] = nc.dram_tensor("sp_up", [I, NT], BF16)
    d["sp_gate"] = nc.dram_tensor("sp_gate", [I, NT], BF16)
    d["sp_g2"] = nc.dram_tensor("sp_g2", [I, NT], BF16)
    d["sp_m"] = nc.dram_tensor("sp_m", [I, NT], BF16)
    d["cc_in"] = [nc.dram_tensor(f"cc_in{i}", [1, 8], F32) for i in range(4)]
    d["cc_out"] = [nc.dram_tensor(f"cc_out{i}", [1, 8], F32) for i in range(4)]
    d["misc_d"] = nc.dram_tensor("misc_d", [8, 256], F32)

    d["taps"] = {}
    for name, shape, dt in [
        ("tap_xnT", [128, 8 * NT], BF16),
        ("tap_xqT", [128, 8 * NT], BF16),
        ("tap_scales", [1, 64], F32),
        ("tap_dtile", [128, 512], BF16),
        ("tap_tabs", [4 * 512, T], BF16),
        ("tap_kT", [128, 8 * T], BF16),
        ("tap_vtok", [128, 16 * H], BF16),
        ("tap_qT", [128, 8 * 512], BF16),
        ("tap_sums", [B, T], F32),
        ("tap_wuT", [128, 8 * I], BF16),
        ("tap_wdT", [128, 24 * H], BF16),
    ]:
        if name in DEBUG_TAPS:
            d["taps"][name] = nc.dram_tensor(name, shape, dt, kind="ExternalOutput")

    with ExitStack() as es0:
        p = P(nc, es0)
        build_program(p, d)
        with nc.Block() as block:
            @block.sync
            def _(e):
                for fn, _k in p.q["sync"]:
                    fn(e)

            @block.vector
            def _(e):
                for fn, _k in p.q["vector"]:
                    fn(e)

            @block.scalar
            def _(e):
                for fn, _k in p.q["scalar"]:
                    fn(e)

            @block.tensor
            def _(e):
                for fn, _k in p.q["tensor"]:
                    fn(e)

            @block.gpsimd
            def _(e):
                for fn, _k in p.q["gpsimd"]:
                    fn(e)
    return nc


def build_program(p, d):
    nc = p.nc
    stP, stXN, stA, stB, stC, stD, stDw, stD2 = (ExitStack() for _ in range(8))

    def sb(st, name, shape, dt, side=None):
        return st.enter_context(nc.sbuf_tensor(name, shape, dt, side=side))

    def psal(st, name, shape, dt=F32):
        return st.enter_context(nc.psum_tensor(name, shape, dt))

    for s in ("tt", "vv", "sc", "gg", "cc"):
        p.sem(s)

    ident = sb(stP, "identity", [128, 128], BF16)
    colbuf = sb(stP, "colbuf", [128, 16], F32)
    rowbuf = sb(stP, "rowbuf", [1, 392], F32)
    mxb = sb(stP, "mxb", [128, 12], F32)
    onesf = sb(stP, "onesf", [1, 128], F32)
    onesb = sb(stP, "onesb", [128, 1], BF16)
    pcol = psal(stP, "pcol", [128, 1], F32)

    idv = p.dma("sync", ident[:], d["ident_in"].ap(), "ld_i")
    cast_v = p.dma("gpsimd", d["wk16"].ap(), d["wk_in"].ap(), "gl_x")
    p.G(lambda e: e.memset(onesf[0:1, :], 1.0))
    p.G(lambda e: e.memset(onesb[:, 0:1], 1.0))
    p.G(lambda e: e.memset(rowbuf[0:1, :], 0.0))
    p.G(lambda e: e.memset(colbuf[:, 12:13], MAGIC))
    gset = p.sig("gpsimd", "gg")

    def bcast_col(src_ap, dst_col):
        p.gwait("tensor", "pcol")
        tv = p.inc("tt")
        p.emit("tensor", lambda e, s=src_ap: e.matmul(pcol[:, 0:1], onesf[0:1, :], s,
                                          start=True, stop=True).then_inc(p.sem("tt"), 1),
               kind="pre")
        p.wait("scalar", "tt", tv)
        p.S(lambda e, dc=dst_col: e.copy(dc, pcol[:, 0:1]))
        sv = p.sig("scalar", "sc")
        p.gset("pcol", "sc", sv)
        return sv

    p.dma_streams["gl_c"] = "gpsimd"
    p.dma_streams["gl_x"] = "gpsimd"
    p.dma_streams["gl_p"] = "gpsimd"
    p.dma_streams["gl_v"] = "gpsimd"

    def collective_max(idx, src_ap, n_vals):
        p.wait("gpsimd", "gg", p.val("gg"))
        p.dma("gpsimd", d["cc_in"][idx].ap()[0:1, n_vals:8],
              rowbuf[0:1, 144:144 + (8 - n_vals)], "gl_c")
        v = p.inc("gl_c", 16)
        p.emit("gpsimd", lambda e, idx=idx, s=src_ap, n=n_vals: e.dma_start(
            out=d["cc_in"][idx].ap()[0:1, 0:n], in_=s).then_inc(p.sem("gl_c"), 16), kind="dma")
        p.wait("gpsimd", "gl_c", v)
        ccv = p.inc("cc")
        p.emit("gpsimd", lambda e, idx=idx: e.collective_compute(
            "AllReduce", ALU.max, replica_groups=[list(range(8))],
            ins=[d["cc_in"][idx].ap()], outs=[d["cc_out"][idx].ap()]).then_inc(p.sem("cc"), 1),
            kind="dma")
        p.wait("gpsimd", "cc", ccv)
        rv = p.inc("gl_c", 16)
        self_sem = p.sem("gl_c")
        p.emit("gpsimd", lambda e, idx=idx, s=self_sem: e.dma_start(
            out=rowbuf[0:1, 160 + 8 * idx:168 + 8 * idx],
            in_=d["cc_out"][idx].ap()).then_inc(s, 16), kind="dma")
        return rv

    def pcollapse(cols_ap, n, dst_off):
        gv = p.inc("gl_p", 16)
        sp_ = p.sem("gl_p")
        p.emit("gpsimd", lambda e, c=cols_ap, s=sp_: e.dma_start(
            out=d["misc_d"].ap()[0:1, 0:128 * n].rearrange("a (p c) -> (a p) c", c=n),
            in_=c).then_inc(s, 16), kind="dma")
        p.wait("gpsimd", "gl_p", gv)
        gv2 = p.inc("gl_p", 16)
        p.emit("gpsimd", lambda e, n=n, s=sp_: e.dma_start(
            out=rowbuf[0:1, 256:256 + 128 * n],
            in_=d["misc_d"].ap()[0:1, 0:128 * n]).then_inc(s, 16), kind="dma")
        p.wait("vector", "gl_p", gv2)
        for j in range(n):
            p.V(lambda e, j=j, n=n, o=dst_off: e.tensor_reduce(
                rowbuf[0:1, o + j:o + j + 1],
                rowbuf[0:1, 256:256 + 128 * n].rearrange("a (c n) -> a c n", n=n)[:, :, j],
                AX, ALU.max), drain=True)
        return p.sig("vector", "vv")

    xnT = sb(stXN, "xnT", [128, 8 * NT], BF16, side="right")

    xbuf = sb(stA, "xbuf", [128, 2 * H], F32)
    xnml = sb(stA, "xnml", [128, 2 * H], BF16)
    lncols_ = sb(stA, "lncols_", [128, 16], F32)
    bn6_ = sb(stA, "bn6_", [128, 24], F32)
    ptr = psal(stA, "ptr", [128, 2 * 1024], BF16)

    tabio = sb(stA, "tabio", [128, T], F32)
    tabc1 = sb(stA, "tabc1", [128, 8], F32)
    tabth = sb(stA, "tabth", [128, T], F32)
    tabk = sb(stA, "tabk", [128, T], F32)
    tabt1 = sb(stA, "tabt1", [128, T], F32)
    tabsin = sb(stA, "tabsin", [128, T], F32)
    tabcos = sb(stA, "tabcos", [128, T], F32)
    tabs1 = sb(stA, "tabs1", [128, T], F32)
    tabs2 = sb(stA, "tabs2", [128, T], F32)
    tabo = sb(stA, "tabo", [128, T], BF16)

    def emit_tab(ic):
        i0 = ic * 128
        p.gwait("gpsimd", "tabgen_v")
        p.gwait("gpsimd", "tabgen_s")
        p.G(lambda e: e.iota(tabio[:], [[1, T]], channel_multiplier=0,
                             allow_small_or_imprecise_dtypes=True))
        p.G(lambda e, i0=i0: e.iota(tabc1[:, 0:1], [[1, 1]], base=i0, channel_multiplier=1,
                                    allow_small_or_imprecise_dtypes=True))
        gi = p.sig("gpsimd", "gg")
        p.wait("scalar", "gg", gi)
        p.S(lambda e: e.activation(tabc1[:, 1:2], tabc1[:, 0:1], ACT.Exp, scale=-LNF))
        ssc = p.sig("scalar", "sc")
        p.wait("vector", "sc", ssc)
        p.wait("vector", "gg", gi)
        p.V(lambda e: e.tensor_scalar(tabc1[:, 4:5], tabc1[:, 0:1], 2.0 / 1433.6,
                                      409.6 / 1433.6, ALU.mult, ALU.add), drain=True)
        v0l = p.sig("vector", "vv")
        p.wait("scalar", "vv", v0l)
        p.S(lambda e: e.activation(tabc1[:, 2:3], tabc1[:, 4:5], ACT.Ln))
        s0l = p.sig("scalar", "sc")
        p.wait("vector", "sc", s0l)
        p.V(lambda e: e.tensor_scalar(tabc1[:, 3:4], tabc1[:, 2:3], 1.0 / 512, None, ALU.mult))
        p.V(lambda e: e.tensor_scalar(tabth[:], tabio[:], tabc1[:, 1:2], None, ALU.mult), drain=True)
        p.V(lambda e: e.tensor_scalar(tabk[:], tabth[:], INV2PI, MAGIC, ALU.mult, ALU.add), drain=True)
        p.V(lambda e: e.tensor_scalar(tabk[:], tabk[:], MAGIC, None, ALU.subtract), drain=True)
        p.V(lambda e: e.scalar_tensor_tensor(tabt1[:], tabk[:], -C1_2PI, tabth[:], ALU.mult, ALU.add), drain=True)
        p.V(lambda e: e.scalar_tensor_tensor(tabt1[:], tabk[:], -C2_2PI, tabt1[:], ALU.mult, ALU.add), drain=True)
        p.V(lambda e: e.tensor_scalar(tabt1[:], tabt1[:], 3.141592, -3.141592, ALU.min, ALU.max))
        v1 = p.sig("vector", "vv")
        p.wait("scalar", "vv", v1)
        p.S(lambda e: e.activation(tabsin[:], tabt1[:], ACT.Sin))
        s1 = p.sig("scalar", "sc")
        p.wait("vector", "sc", s1)
        p.V(lambda e: e.tensor_scalar(tabk[:], tabth[:], INV2PI, 0.25, ALU.mult, ALU.add), drain=True)
        p.V(lambda e: e.tensor_scalar(tabk[:], tabk[:], MAGIC, MAGIC, ALU.add, ALU.subtract), drain=True)
        p.V(lambda e: e.scalar_tensor_tensor(tabt1[:], tabk[:], -C1_2PI, tabth[:], ALU.mult, ALU.add), drain=True)
        p.V(lambda e: e.scalar_tensor_tensor(tabt1[:], tabk[:], -C2_2PI, tabt1[:], ALU.mult, ALU.add), drain=True)
        p.V(lambda e: e.tensor_scalar(tabt1[:], tabt1[:], HALF_PI, None, ALU.add), drain=True)
        p.V(lambda e: e.tensor_scalar(tabt1[:], tabt1[:], 3.141592, -3.141592, ALU.min, ALU.max))
        p.V(lambda e: e.tensor_scalar(tabs1[:], tabio[:], tabc1[:, 3:4], None, ALU.mult))
        v2 = p.sig("vector", "vv")
        p.wait("scalar", "vv", v2)
        p.S(lambda e: e.activation(tabcos[:], tabt1[:], ACT.Sin))
        p.S(lambda e: e.activation(tabs2[:], tabs1[:], ACT.Exp, scale=-1.0), drain=True)
        p.S(lambda e: e.activation(tabs1[:], tabs1[:], ACT.Exp))
        s2 = p.sig("scalar", "sc")
        p.gset("tabgen_s", "sc", s2)
        p.wait("vector", "sc", s2)
        for j, (trig, scl) in enumerate([(tabcos, tabs1), (tabsin, tabs1),
                                         (tabcos, tabs2), (tabsin, tabs2)]):
            p.V(lambda e, a=trig, bb=scl: e.tensor_tensor(tabo[:], a[:], bb[:], ALU.mult))
            vj = p.sig("vector", "vv")
            p.gset("tabgen_v", "vv", vj)
            p.wait("sync", "vv", vj)
            lv = p.dma("sync", d["tabs"].ap()[j, i0:i0 + 128, :], tabo[:], "ld_tb")
            p.wait("vector", "ld_tb", lv)


    p.wait("tensor", "ld_i", idv)
    for i in range(NT // 128):
        if i > 0 and i % 8 == 0:
            emit_tab(i // 8 - 1)
        b, tr = divmod(i, T // 128)
        bf = i % 2
        xt = xbuf[:, bf * H:(bf + 1) * H]
        xn = xnml[:, bf * H:(bf + 1) * H]
        lncols = lncols_[:, bf * 8:(bf + 1) * 8]
        bn6 = bn6_[:, bf * 12:(bf + 1) * 12]
        p.gwait("sync", "xbuf%d" % bf)
        ldv = p.dma("sync", xt, d["x_in"].ap()[b, tr * 128:(tr + 1) * 128, :], "ld_x%d" % bf)
        p.wait("vector", "ld_x%d" % bf, ldv)
        p.V(lambda e, xt=xt, bn6=bn6: e.bn_stats(bn6[:, 0:6], xt[:, 0:512]))
        p.V(lambda e, xt=xt, bn6=bn6: e.bn_stats(bn6[:, 6:12], xt[:, 512:1024]), drain=True)
        p.V(lambda e, lncols=lncols, bn6=bn6: e.bn_aggr(lncols[:, 0:2], bn6[:, 0:12]), drain=True)
        p.V(lambda e, lncols=lncols: e.tensor_scalar(lncols[:, 1:2], lncols[:, 1:2], EPS_LN, None, ALU.add))
        v1 = p.sig("vector", "vv")
        p.wait("scalar", "vv", v1)
        p.S(lambda e, lncols=lncols: e.activation(lncols[:, 2:3], lncols[:, 1:2], ACT.Sqrt))
        s1 = p.sig("scalar", "sc")
        p.wait("vector", "sc", s1)
        p.V(lambda e, lncols=lncols: e.reciprocal(lncols[:, 3:4], lncols[:, 2:3]), drain=True)
        p.V(lambda e, lncols=lncols: e.scalar_tensor_tensor(lncols[:, 4:5], lncols[:, 0:1], -1.0,
                                                            lncols[:, 3:4], ALU.mult, ALU.mult))
        v2 = p.sig("vector", "vv")
        p.wait("scalar", "vv", v2)
        p.gwait("scalar", "xnml%d" % bf)
        p.S(lambda e, xt=xt, xn=xn, lncols=lncols: e.activation(xn, xt, ACT.Identity,
                                                 bias=lncols[:, 4:5], scale=lncols[:, 3:4]))
        sv = p.sig("scalar", "sc")
        p.gset("xbuf%d" % bf, "sc", sv)
        p.wait("tensor", "sc", sv)
        p.gwait("tensor", "ptr%d" % bf)
        for k in range(8):
            p.TE(lambda e, xn=xn, k=k, bf=bf: e.transpose(
                ptr[:, bf * 1024 + k * 128: bf * 1024 + (k + 1) * 128],
                xn[:, k * 128:(k + 1) * 128], ident[:]))
        tv = p.sig("tensor", "tt")
        p.gset("xnml%d" % bf, "tt", tv)
        p.wait("scalar", "tt", tv)
        for k in range(8):
            p.S(lambda e, k=k, bf=bf, i=i: e.copy(
                xnT[:, k * NT + i * 128: k * NT + (i + 1) * 128],
                ptr[:, bf * 1024 + k * 128: bf * 1024 + (k + 1) * 128]))
        ev = p.sig("scalar", "sc")
        p.gset("ptr%d" % bf, "sc", ev)

    emit_tab(3)
    if "tap_tabs" in DEBUG_TAPS:
        p.dma("sync", d["taps"]["tap_tabs"].ap(),
              d["tabs"].ap().rearrange("a b c -> (a b) c"), "ld_tp")
    if "tap_xnT" in DEBUG_TAPS:
        p.dma("sync", d["taps"]["tap_xnT"].ap(), xnT[:], "ld_tp")

    xnT_sc = p.val("sc")
    p.barrier()
    stA.close()
    if PHASE_LIMIT < 2:
        return

    wqkv = sb(stB, "wqkv", [128, 8 * H], BF16)
    kT = sb(stB, "kT", [128, 8 * T], BF16)
    vtok = sb(stB, "vtok", [128, 16 * H], BF16)
    qT = sb(stB, "qT", [128, 8 * 512], BF16)
    expb = sb(stB, "expb", [128, 16 * 512], BF16)
    ptilb = sb(stB, "ptilb", [128, 16 * 512], BF16)
    tb0 = sb(stB, "tb0", [128, 2 * 512], BF16)
    tb1 = sb(stB, "tb1", [128, 2 * 512], BF16)
    evq = sb(stB, "evq", [128, 2 * 512], BF16)
    rotbuf = sb(stB, "rotbuf", [128, 2 * 512], F32)
    atst = sb(stB, "atst", [128, 512], BF16)
    sumsb = sb(stB, "sumsb", [1, 512], F32, side="right")
    d2s = sb(stB, "d2s", [128, 3968], BF16)
    pmm = psal(stB, "pmm", [128, 2 * 1024], F32)
    ppv = psal(stB, "ppv", [128, 2 * 512], F32)
    psums = psal(stB, "psums", [1, 512], F32)

    p.G(lambda e: e.iota(d2s[:], [[1, 3968]], base=-1920, channel_multiplier=-1,
                         allow_small_or_imprecise_dtypes=True))
    gd2 = p.sig("gpsimd", "gg")
    p.wait("scalar", "gg", gd2)
    p.S(lambda e: e.activation(d2s[:], d2s[:], ACT.Abs), drain=True)
    p.S(lambda e: e.activation(d2s[:], d2s[:], ACT.Exp, scale=LN_GAMMA))
    s_d2 = p.sig("scalar", "sc")
    p.wait("vector", "sc", s_d2)

    def load_w(name, wait_val=None):
        p.wait("gpsimd", "tt", p.val("tt") if wait_val is None else wait_val)
        v = None
        sw = p.sem("gl_w")
        p.dma_streams["gl_w"] = "gpsimd"
        for hc in range(8):
            v = p.inc("gl_w", 16)
            p.emit("gpsimd", lambda e, hc=hc, name=name, s=sw: e.dma_start(
                out=wqkv[:, hc * H:(hc + 1) * H],
                in_=d[name].ap()[hc * 128:(hc + 1) * 128, :]).then_inc(s, 16), kind="dma")
        return v

    def xpos_pair(b, tab_idx, src_tok0, n512, dst, dst_chunk_w, slot_tag):
        for c in range(4):
            for tcb in range(n512):
                sl = (c * n512 + tcb) % 2
                tok_off = src_tok0 + tcb * 512
                p.gwait("sync", "tb%d" % sl)
                lv1 = p.dma("sync", tb0[:, sl * 512:(sl + 1) * 512],
                            d["tabs"].ap()[tab_idx, c * 128:(c + 1) * 128,
                                           (tok_off - b * T):(tok_off - b * T) + 512], "ld_t%d" % sl)
                lv2 = p.dma("sync", tb1[:, sl * 512:(sl + 1) * 512],
                            d["tabs"].ap()[tab_idx + 1, c * 128:(c + 1) * 128,
                                           (tok_off - b * T):(tok_off - b * T) + 512], "ld_t%d" % sl)
                p.gwait("tensor", "pmm%d" % sl)
                for half, cc_ in enumerate((c, c + 4)):
                    for hc in range(8):
                        p.TE(lambda e, cc_=cc_, hc=hc, half=half, sl=sl, tok_off=tok_off: e.matmul(
                            pmm[:, sl * 1024 + half * 512: sl * 1024 + (half + 1) * 512],
                            wqkv[:, hc * H + cc_ * 128: hc * H + (cc_ + 1) * 128],
                            xnT[:, hc * NT + tok_off: hc * NT + tok_off + 512],
                            start=(hc == 0), stop=(hc == 7)))
                tv = p.sig("tensor", "tt")
                p.wait("scalar", "tt", tv)
                p.gwait("scalar", "evq")
                for half in range(2):
                    p.S(lambda e, half=half, sl=sl: e.copy(
                        evq[:, half * 512:(half + 1) * 512],
                        pmm[:, sl * 1024 + half * 512: sl * 1024 + (half + 1) * 512]))
                sv = p.sig("scalar", "sc")
                p.gset("pmm%d" % sl, "sc", sv)
                p.wait("vector", "sc", sv)
                p.wait("vector", "ld_t%d" % sl, lv2)
                r0, r1 = rotbuf[:, 0:512], rotbuf[:, 512:1024]
                Qe, Qo = evq[:, 0:512], evq[:, 512:1024]
                cs = tb0[:, sl * 512:(sl + 1) * 512]
                ss = tb1[:, sl * 512:(sl + 1) * 512]
                o_e = c * dst_chunk_w + tcb * 512
                o_o = (c + 4) * dst_chunk_w + tcb * 512
                p.V(lambda e, cs=cs: e.tensor_tensor(r0, Qe, cs, ALU.mult))
                p.V(lambda e, ss=ss: e.tensor_tensor(r1, Qo, ss, ALU.mult), drain=True)
                p.V(lambda e, o=o_e: e.tensor_tensor(dst[:, o:o + 512], r0, r1, ALU.subtract), drain=True)
                p.V(lambda e, cs=cs: e.tensor_tensor(r0, Qo, cs, ALU.mult))
                p.V(lambda e, ss=ss: e.tensor_tensor(r1, Qe, ss, ALU.mult), drain=True)
                p.V(lambda e, o=o_o: e.tensor_tensor(dst[:, o:o + 512], r0, r1, ALU.add), drain=True)
                ev = p.sig("vector", "vv")
                p.gset("evq", "vv", ev)
                p.gset("tb%d" % sl, "vv", ev)

    for b in range(B):
        tok0 = b * T
        if b == 0:
            p.wait("gpsimd", "gl_x", cast_v)
        wl = load_w("wk16", wait_val=None if b == 0 else wq_free)
        p.wait("tensor", "gl_w", wl)
        xpos_pair(b, 2, tok0, 4, kT, T, "k")
        p.wait("tensor", "vv", p.val("vv"))
        p.wait("tensor", "sc", p.val("sc"))
        wl = load_w("wv_in")
        p.wait("tensor", "gl_w", wl)
        for j in range(16):
            for vc in range(2):
                sl = (j * 2 + vc) % 2
                p.gwait("tensor", "pmm%d" % sl)
                for hc in range(8):
                    p.TE(lambda e, j=j, vc=vc, hc=hc, sl=sl, tok0=tok0: e.matmul(
                        pmm[:, sl * 1024: sl * 1024 + 512],
                        xnT[:, hc * NT + tok0 + j * 128: hc * NT + tok0 + (j + 1) * 128],
                        wqkv[:, hc * H + vc * 512: hc * H + (vc + 1) * 512],
                        start=(hc == 0), stop=(hc == 7)))
                tv = p.sig("tensor", "tt")
                p.wait("scalar", "tt", tv)
                p.S(lambda e, j=j, vc=vc, sl=sl: e.copy(
                    vtok[:, j * H + vc * 512: j * H + (vc + 1) * 512],
                    pmm[:, sl * 1024: sl * 1024 + 512]))
                sv = p.sig("scalar", "sc")
                p.gset("pmm%d" % sl, "sc", sv)
        if b == 1:
            p.wait("gpsimd", "vv", vmax)
            vloc = pcollapse(mxb[:, 8:9], 1, 130)
            p.wait("gpsimd", "vv", vloc)
            cm = collective_max(0, rowbuf[0:1, 130:131], 1)
            p.wait("vector", "gl_c", cm)
            p.V(lambda e: e.tensor_scalar(rowbuf[0:1, 1:2], rowbuf[0:1, 160:161], 1.0 / QMAX, None, ALU.mult), drain=True)
            p.V(lambda e: e.reciprocal(rowbuf[0:1, 0:1], rowbuf[0:1, 1:2]), drain=True)
            vsx = p.sig("vector", "vv")
        p.wait("tensor", "sc", p.val("sc"))
        wl = load_w("wq_in")
        p.wait("tensor", "gl_w", wl)

        if b == 0 and "tap_kT" in DEBUG_TAPS:
            p.wait("sync", "vv", p.val("vv"))
            p.dma("sync", d["taps"]["tap_kT"].ap(), kT[:], "ld_tp")
        if b == 0 and "tap_vtok" in DEBUG_TAPS:
            p.wait("sync", "sc", p.val("sc"))
            p.dma("sync", d["taps"]["tap_vtok"].ap(), vtok[:], "ld_tp")

        xpos_pair(b, 0, tok0, 1, qT, 512, "q")
        if b == 0 and "tap_qT" in DEBUG_TAPS:
            p.wait("sync", "vv", p.val("vv"))
            p.dma("sync", d["taps"]["tap_qT"].ap(), qT[:], "ld_tp")
        qdone = p.val("vv")
        for qc in range(4):
            q0 = qc * 512
            p.wait("tensor", "vv", qdone)

            for kc in range(16):
                sl = kc % 2
                k0 = kc * 128
                p.gwait("tensor", "pmm%d" % sl)
                for dc in range(8):
                    p.TE(lambda e, dc=dc, kc=kc, sl=sl: e.matmul(
                        pmm[:, sl * 1024: sl * 1024 + 512],
                        kT[:, dc * T + kc * 128: dc * T + (kc + 1) * 128],
                        qT[:, dc * 512:(dc + 1) * 512],
                        start=(dc == 0), stop=(dc == 7)))
                tv = p.sig("tensor", "tt")
                p.wait("scalar", "tt", tv)
                p.S(lambda e, kc=kc, sl=sl: e.activation(
                    expb[:, kc * 512:(kc + 1) * 512],
                    pmm[:, sl * 1024: sl * 1024 + 512], ACT.Exp))
                sv = p.sig("scalar", "sc")
                p.gset("pmm%d" % sl, "sc", sv)
                p.wait("vector", "sc", sv)
                p.V(lambda e, kc=kc, k0=k0, q0=q0: e.tensor_tensor(
                    ptilb[:, kc * 512:(kc + 1) * 512], expb[:, kc * 512:(kc + 1) * 512],
                    d2s[:, 1920 + q0 - k0: 1920 + q0 - k0 + 512], ALU.mult))
                pv = p.sig("vector", "vv")
            p.wait("tensor", "sc", p.val("sc"))
            p.gwait("tensor", "psums")
            for kc in range(16):
                p.TE(lambda e, kc=kc: e.matmul(psums[0:1, :], onesb[:, 0:1],
                                               expb[:, kc * 512:(kc + 1) * 512],
                                               start=(kc == 0), stop=(kc == 15)))
            tsum = p.sig("tensor", "tt")
            p.wait("scalar", "tt", tsum)
            p.gwait("scalar", "psums")
            p.gwait("scalar", "sumsb")
            p.S(lambda e: e.copy(sumsb[0:1, :], psums[0:1, :]))
            sv = p.sig("scalar", "sc")
            p.gset("psums", "sc", sv)
            p.wait("vector", "sc", sv)
            p.V(lambda e: e.reciprocal(sumsb[0:1, :], sumsb[0:1, :]))
            vsum = p.sig("vector", "vv")
            p.wait("scalar", "vv", vsum)
            sdq = p.dma("scalar", d["sums_d"].ap()[b:b + 1, q0:q0 + 512],
                        sumsb[0:1, :], "sd_s")
            p.gset("sumsb", "sd_s", sdq)

            pt_done = pv
            if qc < 3:
                xpos_pair(b, 0, tok0 + q0 + 512, 1, qT, 512, "q")
                qdone = p.val("vv")
                if qc == 2:
                    wq_free = p.val("tt")
            p.wait("tensor", "vv", pt_done)
            for vc in range(8):
                sl = vc % 2
                p.gwait("tensor", "ppv%d" % sl)
                for kc in range(16):
                    p.TE(lambda e, vc=vc, kc=kc, sl=sl: e.matmul(
                        ppv[:, sl * 512:(sl + 1) * 512],
                        vtok[:, kc * H + vc * 128: kc * H + (vc + 1) * 128],
                        ptilb[:, kc * 512:(kc + 1) * 512],
                        start=(kc == 0), stop=(kc == 15)))
                tv = p.sig("tensor", "tt")
                p.wait("scalar", "tt", tv)
                p.gwait("scalar", "atst")
                p.S(lambda e, sl=sl: e.copy(atst[:, 0:512],
                                            ppv[:, sl * 512:(sl + 1) * 512]))
                sv = p.sig("scalar", "sc")
                p.gset("ppv%d" % sl, "sc", sv)
                p.wait("scalar", "sc", sv)
                dv = p.dma("scalar", d["attnT_d"].ap()[b, vc * 128:(vc + 1) * 128, q0:q0 + 512],
                           atst[:, 0:512], "sd_a0")
                p.gset("atst", "sd_a0", dv)


        if b == 0:
            p.wait("vector", "sc", xnT_sc)
            p.V(lambda e: e.drain())
            for k in range(8):
                p.V(lambda e, k=k: e.tensor_reduce(mxb[:, k:k + 1], xnT[:, k * NT:(k + 1) * NT],
                                                   AX, ALU.max, apply_absolute_value=True))
            p.V(lambda e: e.drain())
            p.V(lambda e: e.tensor_reduce(mxb[:, 8:9], mxb[:, 0:8], AX, ALU.max))
            vmax = p.sig("vector", "vv")

    p.wait("tensor", "vv", vsx)
    p.wait("tensor", "gg", gset)
    bcast_col(rowbuf[0:1, 0:1], colbuf[:, 0:1])
    bcx = bcast_col(rowbuf[0:1, 1:2], colbuf[:, 1:2])
    p.wait("gpsimd", "sc", bcx)
    if "tap_scales" in DEBUG_TAPS and PHASE_LIMIT < 4:
        p.wait("sync", "sc", p.val("sc"))
        p.dma("sync", d["taps"]["tap_scales"].ap()[0:1, 0:64], rowbuf[0:1, 0:64], "ld_tp")

    p.barrier()
    stB.close()
    if PHASE_LIMIT < 3:
        return

    xqT = sb(stC, "xqT", [128, 8 * NT], BF16)
    stPB = ExitStack()
    rtmp = sb(stPB, "rtmp", [128, NT], F32)
    for k in range(8):
        p.V(lambda e, k=k: e.tensor_scalar(rtmp[:], xnT[:, k * NT:(k + 1) * NT],
                                           colbuf[:, 0:1], MAGIC, ALU.mult, ALU.add), drain=True)
        p.V(lambda e, k=k: e.tensor_scalar(xqT[:, k * NT:(k + 1) * NT], rtmp[:],
                                           MAGIC, colbuf[:, 1:2], ALU.subtract, ALU.mult), drain=True)
    vxq = p.sig("vector", "vv")
    if "tap_xqT" in DEBUG_TAPS:
        p.wait("sync", "vv", vxq)
        p.dma("sync", d["taps"]["tap_xqT"].ap(), xqT[:], "ld_tp")

    p.barrier()
    stXN.close()
    stPB.close()
    if PHASE_LIMIT < 4:
        return

    wst = sb(stC, "wst", [128, 2 * H], BF16)
    wnat = sb(stC, "wnat", [128, 2 * H], F32)
    wsc = sb(stC, "wsc", [128, 256], F32)
    wq_b = sb(stC, "wq_b", [128, 2 * H], F32)
    wqb16 = sb(stC, "wqb16", [128, 4 * H], BF16)
    stg = sb(stC, "stg", [128, 4 * 512], BF16)
    mxc = sb(stC, "mxc", [128, 392], F32)
    pwt = psal(stC, "pwt", [128, 2 * 1024], BF16)
    pug = psal(stC, "pug", [128, 2 * 512], F32)

    def deq_chunk(cb):
        src_name = "wu_in" if cb < 24 else "wg_in"
        ic = cb % 24
        bf = cb % 2
        s4 = cb % 4
        nblk = H // 32
        wn = wnat[:, bf * H: bf * H + H]
        ws0 = bf * 128
        wqb = wq_b[:, bf * H: bf * H + H]
        w16 = wqb16[:, s4 * H: s4 * H + H]
        p.gwait("sync", "wnat%d" % bf)
        lv = p.dma("sync", wn, d[src_name].ap()[ic * 128:(ic + 1) * 128, :], "ld_w%d" % bf)
        p.wait("vector", "ld_w%d" % bf, lv)
        p.gwait("vector", "wqb16%d" % s4)
        p.V(lambda e, wn=wn, nblk=nblk, ws0=ws0: e.tensor_reduce(
            wsc[:, ws0:ws0 + nblk], wn.rearrange("p (b j) -> p b j", j=32), AX, ALU.max,
            apply_absolute_value=True), drain=True)
        p.V(lambda e, nblk=nblk, ws0=ws0: e.tensor_scalar(wsc[:, ws0:ws0 + nblk], wsc[:, ws0:ws0 + nblk],
                                                          1.0 / WQMAX, None, ALU.mult), drain=True)
        p.V(lambda e, nblk=nblk, ws0=ws0: e.reciprocal(wsc[:, ws0 + nblk:ws0 + 2 * nblk],
                                                       wsc[:, ws0:ws0 + nblk]), drain=True)
        p.V(lambda e, wn=wn, nblk=nblk, ws0=ws0, wqb=wqb: e.tensor_tensor(
            wqb.rearrange("p (b j) -> p b j", j=32),
            wn.rearrange("p (b j) -> p b j", j=32),
            bass.AP(wsc, ws0 + nblk, [[256, 128], [1, nblk], [0, 32]]),
            ALU.mult), drain=True)
        vwn = p.sig("vector", "vv")
        p.gset("wnat%d" % bf, "vv", vwn)
        p.V(lambda e, wqb=wqb: e.tensor_scalar(wqb, wqb,
                                               MAGIC, MAGIC, ALU.add, ALU.subtract), drain=True)
        p.V(lambda e, nblk=nblk, ws0=ws0, wqb=wqb, w16=w16: e.tensor_tensor(
            w16.rearrange("p (b j) -> p b j", j=32),
            wqb.rearrange("p (b j) -> p b j", j=32),
            bass.AP(wsc, ws0, [[256, 128], [1, nblk], [0, 32]]),
            ALU.mult))
        return p.sig("vector", "vv")

    def deq_transpose(cb, vw):
        bf = cb % 2
        s4 = cb % 4
        w16 = wqb16[:, s4 * H: s4 * H + H]
        p.wait("tensor", "vv", vw)
        p.gwait("tensor", "pwt%d" % bf)
        for k in range(8):
            p.TE(lambda e, k=k, bf=bf, w16=w16: e.transpose(
                pwt[:, bf * 1024 + k * 128: bf * 1024 + k * 128 + 128],
                w16[:, k * 128:(k + 1) * 128], ident[:]))
        tv = p.sig("tensor", "tt")
        p.gset("wqb16%d" % s4, "tt", tv)
        p.wait("scalar", "tt", tv)
        p.gwait("scalar", "wstw%d" % bf)
        for k in range(8):
            p.S(lambda e, k=k, bf=bf: e.copy(
                wst[:, bf * H + k * 128: bf * H + (k + 1) * 128],
                pwt[:, bf * 1024 + k * 128: bf * 1024 + k * 128 + 128]))
        sv = p.sig("scalar", "sc")
        p.gset("pwt%d" % bf, "sc", sv)
        p.gset("wstr%d" % bf, "sc", sv)

    def upgate_mms(cb):
        bf = cb % 2
        ib = cb % 24
        spname = "sp_up" if cb < 24 else "sp_gate"
        p.gwait("tensor", "wstr%d" % bf)
        for tcol in range(8):
            tile_idx = cb * 8 + tcol
            sl = tile_idx % 4
            p.gwait("tensor", "pug%d" % (tile_idx % 2))
            for hc in range(8):
                p.TE(lambda e, bf=bf, tcol=tcol, hc=hc, sl=sl: e.matmul(
                    pug[:, (sl % 2) * 512:((sl % 2) + 1) * 512],
                    wst[:, bf * H + hc * 128: bf * H + (hc + 1) * 128],
                    xqT[:, hc * NT + tcol * 512: hc * NT + (tcol + 1) * 512],
                    start=(hc == 0), stop=(hc == 7)))
            tv = p.sig("tensor", "tt")
            if tcol == 7:
                p.gset("wstw%d" % bf, "tt", tv)
            p.wait("scalar", "tt", tv)
            p.gwait("scalar", "stgv%d" % sl)
            p.gwait("scalar", "stgd%d" % sl)
            p.S(lambda e, sl=sl: e.copy(stg[:, sl * 512:(sl + 1) * 512],
                                        pug[:, (sl % 2) * 512:((sl % 2) + 1) * 512]))
            sv = p.sig("scalar", "sc")
            p.gset("pug%d" % (tile_idx % 2), "sc", sv)
            p.wait("scalar", "sc", sv)
            dv = p.dma("scalar", d[spname].ap()[ib * 128:(ib + 1) * 128,
                                                tcol * 512:(tcol + 1) * 512],
                       stg[:, sl * 512:(sl + 1) * 512], "sd_p%d" % sl)
            p.wait("vector", "sc", sv)
            p.V(lambda e, sl=sl, col=4 + tile_idx: e.tensor_reduce(
                mxc[:, col:col + 1], stg[:, sl * 512:(sl + 1) * 512],
                AX, ALU.max, apply_absolute_value=True))
            mv = p.sig("vector", "vv")
            p.gset("stgv%d" % sl, "vv", mv)
            p.gset("stgd%d" % sl, "sd_p%d" % sl, dv)

    vws = {}
    for cb in range(52):
        if cb < 48:
            vws[cb] = deq_chunk(cb)
        if 3 <= cb < 51:
            deq_transpose(cb - 3, vws[cb - 3])
        if cb >= 4:
            upgate_mms(cb - 4)

    p.V(lambda e: e.drain(), drain=False)
    p.V(lambda e: e.tensor_reduce(mxc[:, 0:1], mxc[:, 4:196], AX, ALU.max), drain=True)
    p.V(lambda e: e.tensor_reduce(mxc[:, 1:2], mxc[:, 196:388], AX, ALU.max), drain=True)
    vmx = p.sig("vector", "vv")
    p.wait("gpsimd", "vv", vmx)
    pcollapse(mxc[:, 0:1], 1, 132)
    vloc = pcollapse(mxc[:, 1:2], 1, 133)
    p.wait("gpsimd", "vv", vloc)
    cm = collective_max(1, rowbuf[0:1, 132:134], 2)
    p.wait("vector", "gl_c", cm)
    p.V(lambda e: e.tensor_scalar(rowbuf[0:1, 3:4], rowbuf[0:1, 168:169], 1.0 / QMAX, None, ALU.mult), drain=True)
    p.V(lambda e: e.reciprocal(rowbuf[0:1, 2:3], rowbuf[0:1, 3:4]), drain=True)
    p.V(lambda e: e.tensor_scalar(rowbuf[0:1, 5:6], rowbuf[0:1, 169:170], 1.0 / QMAX, None, ALU.mult), drain=True)
    p.V(lambda e: e.reciprocal(rowbuf[0:1, 4:5], rowbuf[0:1, 5:6]), drain=True)
    p.V(lambda e: e.tensor_scalar(rowbuf[0:1, 6:7], rowbuf[0:1, 5:6], 1.0 / 65536.0, None, ALU.mult), drain=True)
    p.V(lambda e: e.tensor_scalar(rowbuf[0:1, 7:8], rowbuf[0:1, 5:6], -MAGIC, None, ALU.mult), drain=True)
    vsc2 = p.sig("vector", "vv")
    p.wait("tensor", "vv", vsc2)
    for src, dstc in ((2, 2), (3, 3), (4, 4), (5, 5), (6, 6), (7, 7)):
        bcast_col(rowbuf[0:1, src:src + 1], colbuf[:, dstc:dstc + 1])

    if "tap_scales" in DEBUG_TAPS:
        p.wait("sync", "sc", p.val("sc"))
        p.dma("sync", d["taps"]["tap_scales"].ap()[0:1, 0:64], rowbuf[0:1, 0:64], "ld_tp")

    p.barrier()
    stC.close()
    if PHASE_LIMIT < 5:
        return

    wdT = sb(stD, "wdT", [128, 24 * H], BF16)
    stDs = ExitStack()
    gin = sb(stDs, "gin", [128, 4 * 2048], BF16)
    t1b_ = sb(stDs, "t1b_", [128, 4 * 2048], F32)
    rb__ = sb(stDs, "rb__", [128, 2048], F32)
    sgb_ = sb(stDs, "sgb_", [128, 4 * 2048], F32)
    f3b_ = sb(stDs, "f3b_", [128, 2048], F32)
    g2o = sb(stDs, "g2o", [128, 4 * 2048], BF16)
    mxd = sb(stDs, "mxd", [128, 104], F32)

    n_sweep = 48
    pend_mx = None
    for ti in range(n_sweep):
        ic, th = divmod(ti, 2)
        bf = ti % 4
        g_t = gin[:, bf * 2048:(bf + 1) * 2048]
        o_t = g2o[:, bf * 2048:(bf + 1) * 2048]
        t1b = t1b_[:, bf * 2048:(bf + 1) * 2048]
        rb_ = rb__[:, 0:2048]
        sgb = sgb_[:, bf * 2048:(bf + 1) * 2048]
        f3b = f3b_[:, 0:2048]
        p.gwait("sync", "gin%d" % bf)
        lv = p.dma("sync", g_t, d["sp_gate"].ap()[ic * 128:(ic + 1) * 128,
                                                  th * 2048:(th + 1) * 2048], "ld_g%d" % bf)
        p.wait("scalar", "ld_g%d" % bf, lv)
        p.gwait("scalar", "pdch%d" % bf)
        p.gwait("scalar", "pdcv%d" % bf)
        p.S(lambda e, g_t=g_t, t1b=t1b: e.activation(t1b, g_t, ACT.Identity,
                                                     scale=colbuf[:, 4:5], bias=colbuf[:, 12:13]))
        p.emit("scalar", lambda e: e.drain(), kind="drain")
        p.S(lambda e, sgb=sgb, t1b=t1b: e.activation(sgb, t1b, ACT.Sigmoid,
                                                     scale=colbuf[:, 5:6], bias=colbuf[:, 7:8]))
        s1 = p.sig("scalar", "sc")
        p.gset("gin%d" % bf, "sc", s1)
        p.wait("vector", "sc", s1)
        p.V(lambda e, rb_=rb_, sgb=sgb: e.tensor_scalar(rb_, sgb, 65536.0, MAGIC, ALU.mult, ALU.add))
        if pend_mx is not None:
            po_t, pc_, pbf = pend_mx
            p.emit("vector", lambda e: e.drain(), kind="drain")
            p.V(lambda e, po_t=po_t, pc_=pc_: e.tensor_reduce(
                mxd[:, pc_:pc_ + 1], po_t, AX, ALU.max, apply_absolute_value=True))
            vmxt = p.sig("vector", "vv")
            p.gset("g2g%d" % pbf, "vv", vmxt)
        else:
            p.emit("vector", lambda e: e.drain(), kind="drain")
        p.V(lambda e, f3b=f3b, rb_=rb_: e.tensor_scalar(f3b, rb_, MAGIC, 65535.0, ALU.subtract, ALU.min), drain=True)
        p.gset("pdcv%d" % bf, "vv", p.val("vv") + 1)
        p.gwait("vector", "g2o%d" % bf)
        p.gwait("vector", "g2g%d" % bf)
        p.V(lambda e, o_t=o_t, t1b=t1b, f3b=f3b: e.scalar_tensor_tensor(
            o_t, t1b, MAGIC, f3b, ALU.subtract, ALU.mult))
        v2 = p.sig("vector", "vv")
        p.gset("pdch%d" % bf, "vv", v2)
        pend_mx = (o_t, 4 + ti, bf)
        p.gwait("tensor", "pcol")
        p.TE(lambda e: e.matmul(pcol[0:1, 0:1], onesf[0:1, 0:1], onesf[0:1, 0:1],
                                start=True, stop=True))
        p.wait("gpsimd", "vv", v2)
        dv = p.dma("gpsimd", d["sp_g2"].ap()[ic * 128:(ic + 1) * 128,
                                             th * 2048:(th + 1) * 2048], o_t, "sd_g%d" % bf)
        p.gset("g2o%d" % bf, "sd_g%d" % bf, dv)

    po_t, pc_, pbf = pend_mx
    p.V(lambda e: e.drain(), drain=False)
    p.V(lambda e, po_t=po_t, pc_=pc_: e.tensor_reduce(
        mxd[:, pc_:pc_ + 1], po_t, AX, ALU.max, apply_absolute_value=True), drain=True)
    p.V(lambda e: e.tensor_reduce(mxd[:, 0:1], mxd[:, 4:52], AX, ALU.max), drain=True)
    vmx = p.sig("vector", "vv")
    if PHASE_LIMIT < 4.4:
        return
    p.wait("gpsimd", "vv", vmx)
    vloc = pcollapse(mxd[:, 0:1], 1, 136)
    p.wait("gpsimd", "vv", vloc)
    cm = collective_max(2, rowbuf[0:1, 136:137], 1)
    if PHASE_LIMIT < 4.7:
        return
    wdnat = sb(stDw, "wdnat", [128, 2 * I], F32)
    wdsc = sb(stDw, "wdsc", [128, 256], F32)
    wdq = sb(stDw, "wdq", [128, I], F32)
    wdq16 = sb(stDw, "wdq16", [128, I], BF16)
    pwd = psal(stDw, "pwd", [128, 2 * 1024], BF16)
    for hc in range(8):
        bf = hc % 2
        wn = wdnat[:, bf * I:(bf + 1) * I]
        p.gwait("sync", "wdnat%d" % bf)
        lv = p.dma("sync", wn, d["wd_in"].ap()[hc * 128:(hc + 1) * 128, :], "ld_w%d" % bf)
        p.wait("vector", "ld_w%d" % bf, lv)
        p.gwait("vector", "wdq16")
        p.V(lambda e, wn=wn: e.tensor_reduce(
            wdsc[:, 0:96], wn.rearrange("p (b j) -> p b j", j=32), AX, ALU.max,
            apply_absolute_value=True), drain=True)
        p.V(lambda e: e.tensor_scalar(wdsc[:, 0:96], wdsc[:, 0:96], 1.0 / WQMAX, None, ALU.mult), drain=True)
        p.V(lambda e: e.reciprocal(wdsc[:, 96:192], wdsc[:, 0:96]), drain=True)
        p.V(lambda e, wn=wn: e.tensor_tensor(
            wdq[:].rearrange("p (b j) -> p b j", j=32),
            wn.rearrange("p (b j) -> p b j", j=32),
            bass.AP(wdsc, 96, [[256, 128], [1, 96], [0, 32]]),
            ALU.mult), drain=True)
        vwn = p.sig("vector", "vv")
        p.gset("wdnat%d" % bf, "vv", vwn)
        p.V(lambda e: e.tensor_scalar(wdq[:], wdq[:], MAGIC, MAGIC, ALU.add, ALU.subtract), drain=True)
        p.V(lambda e: e.tensor_tensor(
            wdq16[:].rearrange("p (b j) -> p b j", j=32),
            wdq[:].rearrange("p (b j) -> p b j", j=32),
            bass.AP(wdsc, 0, [[256, 128], [1, 96], [0, 32]]),
            ALU.mult))
        vw = p.sig("vector", "vv")
        p.wait("tensor", "vv", vw)
        for k in range(24):
            sl2 = k % 8
            if sl2 == 0:
                p.gwait("tensor", "pwd%d" % ((k // 8) % 2))
            p.TE(lambda e, k=k, sl2=sl2, g=(k // 8) % 2: e.transpose(
                pwd[:, g * 1024 + sl2 * 128: g * 1024 + sl2 * 128 + 128],
                wdq16[:, k * 128:(k + 1) * 128], ident[:]))
            if sl2 == 7 or k == 23:
                tv = p.sig("tensor", "tt")
                p.wait("scalar", "tt", tv)
                kk0 = (k // 8) * 8
                for kk in range(kk0, k + 1):
                    p.S(lambda e, kk=kk, hc=hc, g=(k // 8) % 2: e.copy(
                        wdT[:, kk * H + hc * 128: kk * H + (hc + 1) * 128],
                        pwd[:, g * 1024 + (kk % 8) * 128: g * 1024 + (kk % 8) * 128 + 128]))
                sv = p.sig("scalar", "sc")
                p.gset("pwd%d" % ((k // 8) % 2), "sc", sv)
        p.gset("wdq16", "sc", p.val("sc"))

    if "tap_wdT" in DEBUG_TAPS:
        p.wait("sync", "sc", p.val("sc"))
        p.dma("sync", d["taps"]["tap_wdT"].ap(), wdT[:], "ld_tp")

    p.barrier()
    stDw.close()
    if PHASE_LIMIT < 6:
        return

    p.wait("vector", "gl_c", cm)
    p.V(lambda e: e.tensor_tensor(rowbuf[0:1, 176:177], rowbuf[0:1, 176:177],
                                  rowbuf[0:1, 6:7], ALU.mult), drain=True)
    p.V(lambda e: e.tensor_scalar(rowbuf[0:1, 9:10], rowbuf[0:1, 176:177], 1.0 / QMAX, None, ALU.mult), drain=True)
    p.V(lambda e: e.reciprocal(rowbuf[0:1, 8:9], rowbuf[0:1, 9:10]), drain=True)
    p.V(lambda e: e.tensor_tensor(rowbuf[0:1, 13:14], rowbuf[0:1, 8:9],
                                  rowbuf[0:1, 6:7], ALU.mult), drain=True)
    vsc3 = p.sig("vector", "vv")
    p.wait("tensor", "vv", vsc3)
    bcast_col(rowbuf[0:1, 13:14], colbuf[:, 8:9])
    bc3 = bcast_col(rowbuf[0:1, 9:10], colbuf[:, 9:10])
    p.wait("vector", "sc", bc3)

    stD2a = ExitStack()
    uin = sb(stD2a, "uin", [128, 4 * 2048], BF16)
    pend_mx = None
    for ti in range(n_sweep):
        ic, th = divmod(ti, 2)
        bf = ti % 4
        g_t = gin[:, bf * 2048:(bf + 1) * 2048]
        u_t = uin[:, bf * 2048:(bf + 1) * 2048]
        o_t = g2o[:, bf * 2048:(bf + 1) * 2048]
        t1b = t1b_[:, bf * 2048:(bf + 1) * 2048]
        rb_ = rb__[:, 0:2048]
        sgb = sgb_[:, bf * 2048:(bf + 1) * 2048]
        f3b = f3b_[:, 0:2048]
        p.gwait("sync", "gin%d" % bf)
        lv1 = p.dma("sync", g_t, d["sp_g2"].ap()[ic * 128:(ic + 1) * 128,
                                                 th * 2048:(th + 1) * 2048], "ld_g%d" % bf)
        p.gwait("sync", "uin%d" % bf)
        lv2 = p.dma("sync", u_t, d["sp_up"].ap()[ic * 128:(ic + 1) * 128,
                                                 th * 2048:(th + 1) * 2048], "ld_u%d" % bf)
        p.wait("scalar", "ld_g%d" % bf, lv1)
        p.wait("scalar", "ld_u%d" % bf, lv2)
        p.gwait("scalar", "pdch%d" % bf)
        p.gwait("scalar", "pdcv%d" % bf)
        p.S(lambda e, g_t=g_t, t1b=t1b: e.activation(t1b, g_t, ACT.Identity,
                                                     scale=colbuf[:, 8:9], bias=colbuf[:, 12:13]))
        p.emit("scalar", lambda e: e.drain(), kind="drain")
        p.S(lambda e, u_t=u_t, sgb=sgb: e.activation(sgb, u_t, ACT.Identity,
                                                     scale=colbuf[:, 2:3], bias=colbuf[:, 12:13]))
        s1 = p.sig("scalar", "sc")
        p.gset("gin%d" % bf, "sc", s1)
        p.gset("uin%d" % bf, "sc", s1)
        p.wait("vector", "sc", s1)
        p.V(lambda e, rb_=rb_, t1b=t1b: e.tensor_scalar(rb_, t1b, MAGIC, None, ALU.subtract))
        if pend_mx is not None:
            po_t, pc_, pbf = pend_mx
            p.emit("vector", lambda e: e.drain(), kind="drain")
            p.V(lambda e, po_t=po_t, pc_=pc_: e.tensor_reduce(
                mxd[:, pc_:pc_ + 1], po_t, AX, ALU.max, apply_absolute_value=True))
            vmxt = p.sig("vector", "vv")
            p.gset("g2g%d" % pbf, "vv", vmxt)
        else:
            p.emit("vector", lambda e: e.drain(), kind="drain")
        p.V(lambda e, f3b=f3b, sgb=sgb: e.tensor_scalar(f3b, sgb, MAGIC, None, ALU.subtract), drain=True)
        p.gset("pdcv%d" % bf, "vv", p.val("vv") + 1)
        p.gwait("vector", "g2o%d" % bf)
        p.gwait("vector", "g2g%d" % bf)
        p.V(lambda e, o_t=o_t, rb_=rb_, f3b=f3b: e.tensor_tensor(o_t, rb_, f3b, ALU.mult))
        v2 = p.sig("vector", "vv")
        p.gset("pdch%d" % bf, "vv", v2)
        pend_mx = (o_t, 52 + ti, bf)
        p.gwait("tensor", "pcol")
        p.TE(lambda e: e.matmul(pcol[0:1, 0:1], onesf[0:1, 0:1], onesf[0:1, 0:1],
                                start=True, stop=True))
        p.wait("gpsimd", "vv", v2)
        dv = p.dma("gpsimd", d["sp_m"].ap()[ic * 128:(ic + 1) * 128,
                                            th * 2048:(th + 1) * 2048], o_t, "sd_g%d" % bf)
        p.gset("g2o%d" % bf, "sd_g%d" % bf, dv)

    po_t, pc_, pbf = pend_mx
    p.V(lambda e: e.drain(), drain=False)
    p.V(lambda e, po_t=po_t, pc_=pc_: e.tensor_reduce(
        mxd[:, pc_:pc_ + 1], po_t, AX, ALU.max, apply_absolute_value=True), drain=True)
    p.V(lambda e: e.tensor_reduce(mxd[:, 1:2], mxd[:, 52:100], AX, ALU.max), drain=True)
    vmx = p.sig("vector", "vv")
    p.wait("gpsimd", "vv", vmx)
    vloc = pcollapse(mxd[:, 1:2], 1, 137)
    p.wait("gpsimd", "vv", vloc)
    cm = collective_max(3, rowbuf[0:1, 137:138], 1)
    p.wait("vector", "gl_c", cm)
    p.V(lambda e: e.tensor_scalar(rowbuf[0:1, 20:21], rowbuf[0:1, 184:185], 1.0 / QMAX, None, ALU.mult), drain=True)
    p.V(lambda e: e.tensor_tensor(rowbuf[0:1, 21:22], rowbuf[0:1, 20:21], rowbuf[0:1, 9:10], ALU.mult), drain=True)
    p.V(lambda e: e.tensor_tensor(rowbuf[0:1, 11:12], rowbuf[0:1, 21:22], rowbuf[0:1, 3:4], ALU.mult), drain=True)
    p.V(lambda e: e.tensor_tensor(rowbuf[0:1, 22:23], rowbuf[0:1, 9:10], rowbuf[0:1, 3:4], ALU.mult), drain=True)
    p.V(lambda e: e.reciprocal(rowbuf[0:1, 23:24], rowbuf[0:1, 11:12]), drain=True)
    p.V(lambda e: e.tensor_tensor(rowbuf[0:1, 10:11], rowbuf[0:1, 22:23], rowbuf[0:1, 23:24], ALU.mult), drain=True)
    vsc4 = p.sig("vector", "vv")
    p.wait("tensor", "vv", vsc4)
    bcast_col(rowbuf[0:1, 10:11], colbuf[:, 10:11])
    bc4 = bcast_col(rowbuf[0:1, 11:12], colbuf[:, 11:12])
    p.wait("vector", "sc", bc4)
    p.barrier()
    stD2a.close()
    stDs.close()
    if PHASE_LIMIT < 7:
        p.barrier()
        return

    oqT_ = sb(stD2, "oqT_", [128, 2 * 24 * 512], BF16)
    oqrs = sb(stD2, "oqrs", [128, 1024], F32)
    minb = sb(stD2, "minb", [128, 4 * 1024], BF16)
    I4 = I * 4
    attin = sb(stD2, "attin", [128, 4 * 1024], BF16)
    xres = sb(stD2, "xres", [128, 4 * 1024], F32)
    accb = sb(stD2, "accb", [128, 2 * 1024], F32)
    outb = sb(stD2, "outb", [128, 2 * 1024], F32)
    rrow = sb(stD2, "rrow", [1, T], F32, side="right")
    rcol1 = sb(stD2, "rcol1", [128, 2], F32)
    pdown = psal(stD2, "pdown", [128, 2 * 1024], F32)
    patt = psal(stD2, "patt", [128, 2 * 1024], BF16)

    pend_store = []
    for b in range(B):
        p.gwait("sync", "rrow")
        rv = p.dma("sync", rrow[0:1, :], d["sums_d"].ap()[b:b + 1, :], "ld_rr")
        p.wait("tensor", "ld_rr", rv)
        for st_ in range(4):
            t0 = st_ * 512
            ssl = st_ % 2
            oqT = oqT_[:, ssl * 24 * 512:(ssl + 1) * 24 * 512]
            p.gwait("vector", "oqT%d" % ssl)
            for icp in range(12):
                bf = (st_ * 12 + icp) % 4
                mt = minb[:, bf * 1024: bf * 1024 + 1024]
                p.gwait("gpsimd", "minb%d" % bf)
                lv = p.dma("gpsimd", mt,
                           bass.AP(d["sp_m"], (2 * icp * 128) * NT + b * T + t0,
                                   [[NT, 128], [128 * NT, 2], [1, 512]]), "ld_m%d" % bf)
                p.wait("vector", "ld_m%d" % bf, lv)
                p.V(lambda e, mt=mt: e.tensor_scalar(oqrs[:, 0:1024], mt, colbuf[:, 10:11],
                                                     MAGIC, ALU.mult, ALU.add), drain=True)
                vm = p.sig("vector", "vv")
                p.gset("minb%d" % bf, "vv", vm)
                p.V(lambda e, icp=icp, oqT=oqT: e.tensor_scalar(
                    oqT[:, icp * 1024:(icp + 1) * 1024],
                    oqrs[:, 0:1024], MAGIC, colbuf[:, 11:12],
                    ALU.subtract, ALU.mult), drain=True)
            voq = p.sig("vector", "vv")
            p.wait("tensor", "vv", voq)
            avs = {}
            xvs = {}
            for ts_ in range(4):
                sl4 = ts_ % 4
                tr = (t0 + ts_ * 128) // 128
                p.gwait("sync", "attin%d" % sl4)
                avs[ts_] = p.dma("sync", attin[:, sl4 * 1024:(sl4 + 1) * 1024],
                                 bass.AP(d["attnT_d"], b * H * T + tr * 128,
                                         [[T, 128], [128 * T, 8], [1, 128]]), "ld_a%d" % sl4)
                p.gwait("sync", "xres%d" % sl4)
                xvs[ts_] = p.dma("sync", xres[:, sl4 * 1024:(sl4 + 1) * 1024],
                                 d["x_in"].ap()[b, tr * 128:(tr + 1) * 128, :], "ld_xr%d" % sl4)
            for ts_ in range(4):
                tc_g = (b * T + t0 + ts_ * 128) // 128
                sl = ts_ % 2
                sl4 = ts_ % 4
                tr = (t0 + ts_ * 128) // 128
                av = avs[ts_]
                xv = xvs[ts_]
                p.gwait("tensor", "pdown%d" % sl)
                for dcol in range(2):
                    for ic in range(24):
                        p.TE(lambda e, ic=ic, dcol=dcol, ts_=ts_, sl=sl, oqT=oqT: e.matmul(
                            pdown[:, sl * 1024 + dcol * 512: sl * 1024 + (dcol + 1) * 512],
                            oqT[:, ic * 512 + ts_ * 128: ic * 512 + (ts_ + 1) * 128],
                            wdT[:, ic * H + dcol * 512: ic * H + (dcol + 1) * 512],
                            start=(ic == 0), stop=(ic == 23)))
                p.wait("tensor", "ld_a%d" % sl4, av)
                p.gwait("tensor", "patt%d" % sl)
                for k in range(8):
                    p.TE(lambda e, k=k, sl=sl, sl4=sl4: e.transpose(
                        patt[:, sl * 1024 + k * 128: sl * 1024 + (k + 1) * 128],
                        attin[:, sl4 * 1024 + k * 128: sl4 * 1024 + (k + 1) * 128], ident[:]))
                p.gwait("tensor", "rcol%d" % sl)
                p.TE(lambda e, tr=tr, sl=sl: e.matmul(
                    pcol[:, 0:1], rrow[0:1, tr * 128:(tr + 1) * 128],
                    onesf[0:1, 0:1], start=True, stop=True))
                tv = p.sig("tensor", "tt")
                p.gset("oqT%d" % ssl, "tt", tv)
                p.wait("scalar", "tt", tv)
                p.S(lambda e, sl=sl: e.copy(rcol1[:, sl:sl + 1], pcol[:, 0:1]))
                sv = p.sig("scalar", "sc")
                p.gset("rrow", "sc", sv)
                p.wait("vector", "sc", sv)
                p.wait("vector", "tt", tv)
                p.wait("vector", "ld_xr%d" % sl4, xv)
                p.gwait("vector", "outb%d" % sl)
                p.V(lambda e, sl=sl, sl4=sl4: e.scalar_tensor_tensor(
                    accb[:, sl * 1024:(sl + 1) * 1024], patt[:, sl * 1024:(sl + 1) * 1024],
                    rcol1[:, sl:sl + 1],
                    xres[:, sl4 * 1024:(sl4 + 1) * 1024], ALU.mult, ALU.add), drain=True)
                p.V(lambda e, sl=sl: e.tensor_tensor(
                    outb[:, sl * 1024:(sl + 1) * 1024], accb[:, sl * 1024:(sl + 1) * 1024],
                    pdown[:, sl * 1024:(sl + 1) * 1024], ALU.add))
                vo = p.sig("vector", "vv")
                p.gset("patt%d" % sl, "vv", vo)
                p.gset("pdown%d" % sl, "vv", vo)
                p.gset("attin%d" % sl4, "vv", vo)
                p.gset("xres%d" % sl4, "vv", vo)
                p.gset("rcol%d" % sl, "vv", vo)
                pend_store.append((vo, b, tr, sl))
                if len(pend_store) > 1:
                    q_vo, q_b, q_tr, q_sl = pend_store.pop(0)
                    p.wait("scalar", "vv", q_vo)
                    q_ov = p.dma("scalar", d["out_t"].ap()[q_b, q_tr * 128:(q_tr + 1) * 128, :],
                                 outb[:, q_sl * 1024:(q_sl + 1) * 1024], "ld_o%d" % q_sl)
                    p.gset("outb%d" % q_sl, "ld_o%d" % q_sl, q_ov)

    for q_vo, q_b, q_tr, q_sl in pend_store:
        p.wait("scalar", "vv", q_vo)
        q_ov = p.dma("scalar", d["out_t"].ap()[q_b, q_tr * 128:(q_tr + 1) * 128, :],
                     outb[:, q_sl * 1024:(q_sl + 1) * 1024], "ld_o%d" % q_sl)
        p.gset("outb%d" % q_sl, "ld_o%d" % q_sl, q_ov)
    p.dma_finalize()
    p.barrier()
    stD2.close()
    stD.close()
    stP.close()



def kernel(L0_attn_X, L0_attn_W_Q, L0_attn_W_K, L0_attn_W_V,
           L0_ffn_w_gate, L0_ffn_w_up, L0_ffn_w_down):
    import ml_dtypes
    perm = np.concatenate([np.arange(0, H, 2), np.arange(1, H, 2)])
    wq = np.ascontiguousarray(np.asarray(L0_attn_W_Q, np.float32)[:, perm])
    wk = np.ascontiguousarray(np.asarray(L0_attn_W_K, np.float32)[:, perm])
    wv = np.ascontiguousarray(np.asarray(L0_attn_W_V, np.float32))
    wu = np.ascontiguousarray(np.asarray(L0_ffn_w_up, np.float32))
    wg = np.ascontiguousarray(np.asarray(L0_ffn_w_gate, np.float32))
    wd = np.ascontiguousarray(np.asarray(L0_ffn_w_down, np.float32))
    X = np.asarray(L0_attn_X, np.float32)
    ident = np.eye(128, dtype=ml_dtypes.bfloat16)

    nc = build_nc()
    in_maps = []
    for c in range(8):
        in_maps.append({"x": np.ascontiguousarray(X[2 * c:2 * c + 2]),
                        "wq": wq, "wk": wk, "wv": wv,
                        "wu": wu, "wg": wg, "wd": wd, "ident": ident})
    res = run_bass_kernel_spmd(nc, in_maps, core_ids=list(range(8)))
    out = np.concatenate([r["out"] for r in res.results], axis=0)
    return out.astype(np.float32)


# version 37 (final)
# speedup vs baseline: 1.0286x; 1.0031x over previous
import numpy as np
import concourse.bass as bass
import concourse.mybir as mybir
from concourse.bass_utils import run_bass_kernel_spmd
from contextlib import ExitStack

F32 = mybir.dt.float32
BF16 = mybir.dt.bfloat16
ALU = mybir.AluOpType
ACT = mybir.ActivationFunctionType
AX = mybir.AxisListType.X

MAGIC = 12582912.0
LN_GAMMA = -0.10536051565782628
INV2PI = 0.15915494309189535
C1_2PI = 6.28125
C2_2PI = 0.0019353071795864769
HALF_PI = 1.5707963267948966
LNF = 0.017988946039015984
QMAX = 32767.0
WQMAX = 7.0
EPS_LN = 1e-5

B, T, H, I = 2, 2048, 1024, 3072
NT = B * T

ENGINES = ("sync", "vector", "scalar", "tensor", "gpsimd")
DEBUG_TAPS = set()
PHASE_LIMIT = 7


class P:

    def __init__(self, nc, es):
        self.nc = nc
        self.es = es
        self.q = {e: [] for e in ENGINES}
        self.cnt = {}
        self._sems = {}
        self.guards = {}
        self.dma_streams = {}

    def sem(self, name):
        if name not in self._sems:
            self._sems[name] = self.es.enter_context(self.nc.semaphore(name))
        return self._sems[name]

    def inc(self, name, n=1):
        self.cnt[name] = self.cnt.get(name, 0) + n
        return self.cnt[name]

    def val(self, name):
        return self.cnt.get(name, 0)

    def emit(self, eng, fn, kind="op"):
        self.q[eng].append((fn, kind))

    def wait(self, eng, sem_name, value):
        if value <= 0:
            return
        s = self.sem(sem_name)
        self.emit(eng, lambda e, s=s, v=value: e.wait_ge(s, v), kind="wait")

    def sig(self, eng, sem_name):
        s = self.sem(sem_name)
        v = self.inc(sem_name)
        q = self.q[eng]
        i = len(q) - 1
        while i >= 0 and q[i][1] == "wait":
            i -= 1
        if i >= 0 and q[i][1] == "op":
            fn = q[i][0]
            q[i] = (lambda e, fn=fn, s=s: fn(e).then_inc(s, 1), "pre")
            return v
        if eng == "vector":
            self.emit(eng, lambda e: e.drain(), kind="drain")
        self.emit(eng, lambda e, s=s: e.nop().then_inc(s, 1), kind="pre")
        return v

    def sig_nop(self, eng, sem_name):
        s = self.sem(sem_name)
        v = self.inc(sem_name)
        if eng == "vector":
            self.emit(eng, lambda e: e.drain(), kind="drain")
        self.emit(eng, lambda e, s=s: e.nop().then_inc(s, 1), kind="pre")
        return v

    def dma(self, eng, out, in_, sem_name):
        s = self.sem(sem_name)
        v = self.inc(sem_name, 16)
        self.dma_streams[sem_name] = eng
        self.emit(eng, lambda e, o=out, i=in_, s=s: e.dma_start(out=o, in_=i).then_inc(s, 16),
                  kind="dma")
        return v

    def flush_sig(self, eng):
        return self.sig(eng, "fl_" + eng)

    def dma_finalize(self):
        for name, eng in sorted(self.dma_streams.items()):
            self.wait(eng, name, self.val(name))

    def gwait(self, eng, key):
        g = self.guards.get(key)
        if g is not None:
            self.wait(eng, g[0], g[1])

    def gset(self, key, sem_name, value):
        self.guards[key] = (sem_name, value)

    def barrier(self, engines=ENGINES, skip=()):
        for name, eng in sorted(self.dma_streams.items()):
            if name in skip:
                continue
            self.wait(eng, name, self.val(name))
        s = self.sem("bar")
        tgt = self.inc("bar", len(engines))
        for eng in engines:
            if eng == "vector":
                self.emit(eng, lambda e: e.drain(), kind="drain")
            self.emit(eng, lambda e, s=s: e.nop().then_inc(s, 1), kind="pre")
            self.emit(eng, lambda e, t=tgt, s=s: e.wait_ge(s, t), kind="wait")

    def V(self, fn, drain=False):
        self.emit("vector", fn)
        if drain:
            self.emit("vector", lambda e: e.drain(), kind="drain")

    def S(self, fn, drain=False):
        self.emit("scalar", fn)
        if drain:
            self.emit("scalar", lambda e: e.drain(), kind="drain")

    def TE(self, fn):
        self.emit("tensor", fn)

    def G(self, fn):
        self.emit("gpsimd", fn)


def build_nc():
    nc = bass.Bass("TRN2", target_bir_lowering=False)

    d = {}
    d["x_in"] = nc.dram_tensor("x", [B, T, H], F32, kind="ExternalInput")
    d["wq_in"] = nc.dram_tensor("wq", [H, H], F32, kind="ExternalInput")
    d["wk_in"] = nc.dram_tensor("wk", [H, H], F32, kind="ExternalInput")
    d["wv_in"] = nc.dram_tensor("wv", [H, H], F32, kind="ExternalInput")
    d["wk16"] = nc.dram_tensor("wk16", [H, H], BF16)
    d["wu_in"] = nc.dram_tensor("wu", [I, H], F32, kind="ExternalInput")
    d["wg_in"] = nc.dram_tensor("wg", [I, H], F32, kind="ExternalInput")
    d["wd_in"] = nc.dram_tensor("wd", [H, I], F32, kind="ExternalInput")
    d["ident_in"] = nc.dram_tensor("ident", [128, 128], BF16, kind="ExternalInput")
    d["out_t"] = nc.dram_tensor("out", [B, T, H], F32, kind="ExternalOutput")

    d["tabs"] = nc.dram_tensor("tabs", [4, 512, T], BF16)
    d["attnT_d"] = nc.dram_tensor("attnT_d", [B, H, T], BF16)
    d["sums_d"] = nc.dram_tensor("sums_d", [B, T], F32)
    d["sp_up"] = nc.dram_tensor("sp_up", [I, NT], BF16)
    d["sp_gate"] = nc.dram_tensor("sp_gate", [I, NT], BF16)
    d["sp_g2"] = nc.dram_tensor("sp_g2", [I, NT], BF16)
    d["sp_m"] = nc.dram_tensor("sp_m", [I, NT], BF16)
    d["cc_in"] = [nc.dram_tensor(f"cc_in{i}", [1, 8], F32) for i in range(4)]
    d["cc_out"] = [nc.dram_tensor(f"cc_out{i}", [1, 8], F32) for i in range(4)]
    d["misc_d"] = nc.dram_tensor("misc_d", [8, 256], F32)

    d["taps"] = {}
    for name, shape, dt in [
        ("tap_xnT", [128, 8 * NT], BF16),
        ("tap_xqT", [128, 8 * NT], BF16),
        ("tap_scales", [1, 64], F32),
        ("tap_dtile", [128, 512], BF16),
        ("tap_tabs", [4 * 512, T], BF16),
        ("tap_kT", [128, 8 * T], BF16),
        ("tap_vtok", [128, 16 * H], BF16),
        ("tap_qT", [128, 8 * 512], BF16),
        ("tap_sums", [B, T], F32),
        ("tap_wuT", [128, 8 * I], BF16),
        ("tap_wdT", [128, 24 * H], BF16),
    ]:
        if name in DEBUG_TAPS:
            d["taps"][name] = nc.dram_tensor(name, shape, dt, kind="ExternalOutput")

    with ExitStack() as es0:
        p = P(nc, es0)
        build_program(p, d)
        with nc.Block() as block:
            @block.sync
            def _(e):
                for fn, _k in p.q["sync"]:
                    fn(e)

            @block.vector
            def _(e):
                for fn, _k in p.q["vector"]:
                    fn(e)

            @block.scalar
            def _(e):
                for fn, _k in p.q["scalar"]:
                    fn(e)

            @block.tensor
            def _(e):
                for fn, _k in p.q["tensor"]:
                    fn(e)

            @block.gpsimd
            def _(e):
                for fn, _k in p.q["gpsimd"]:
                    fn(e)
    return nc


def build_program(p, d):
    nc = p.nc
    stP, stXN, stA, stB, stC, stD, stDw, stD2 = (ExitStack() for _ in range(8))

    def sb(st, name, shape, dt, side=None):
        return st.enter_context(nc.sbuf_tensor(name, shape, dt, side=side))

    def psal(st, name, shape, dt=F32):
        return st.enter_context(nc.psum_tensor(name, shape, dt))

    for s in ("tt", "vv", "sc", "gg", "cc"):
        p.sem(s)

    ident = sb(stP, "identity", [128, 128], BF16)
    colbuf = sb(stP, "colbuf", [128, 16], F32)
    rowbuf = sb(stP, "rowbuf", [1, 392], F32)
    mxb = sb(stP, "mxb", [128, 12], F32)
    onesf = sb(stP, "onesf", [1, 128], F32)
    onesb = sb(stP, "onesb", [128, 1], BF16)
    pcol = psal(stP, "pcol", [128, 1], F32)

    idv = p.dma("sync", ident[:], d["ident_in"].ap(), "ld_i")
    cast_v = p.dma("gpsimd", d["wk16"].ap(), d["wk_in"].ap(), "gl_x")
    p.G(lambda e: e.memset(onesf[0:1, :], 1.0))
    p.G(lambda e: e.memset(onesb[:, 0:1], 1.0))
    p.G(lambda e: e.memset(rowbuf[0:1, :], 0.0))
    p.G(lambda e: e.memset(colbuf[:, 12:13], MAGIC))
    gset = p.sig("gpsimd", "gg")

    def bcast_col(src_ap, dst_col):
        p.gwait("tensor", "pcol")
        tv = p.inc("tt")
        p.emit("tensor", lambda e, s=src_ap: e.matmul(pcol[:, 0:1], onesf[0:1, :], s,
                                          start=True, stop=True).then_inc(p.sem("tt"), 1),
               kind="pre")
        p.wait("scalar", "tt", tv)
        p.S(lambda e, dc=dst_col: e.copy(dc, pcol[:, 0:1]))
        sv = p.sig("scalar", "sc")
        p.gset("pcol", "sc", sv)
        return sv

    p.dma_streams["gl_c"] = "gpsimd"
    p.dma_streams["gl_x"] = "gpsimd"
    p.dma_streams["gl_p"] = "gpsimd"
    p.dma_streams["gl_v"] = "gpsimd"

    def collective_max(idx, src_ap, n_vals):
        p.wait("gpsimd", "gg", p.val("gg"))
        p.dma("gpsimd", d["cc_in"][idx].ap()[0:1, n_vals:8],
              rowbuf[0:1, 144:144 + (8 - n_vals)], "gl_c")
        v = p.inc("gl_c", 16)
        p.emit("gpsimd", lambda e, idx=idx, s=src_ap, n=n_vals: e.dma_start(
            out=d["cc_in"][idx].ap()[0:1, 0:n], in_=s).then_inc(p.sem("gl_c"), 16), kind="dma")
        p.wait("gpsimd", "gl_c", v)
        ccv = p.inc("cc")
        p.emit("gpsimd", lambda e, idx=idx: e.collective_compute(
            "AllReduce", ALU.max, replica_groups=[list(range(8))],
            ins=[d["cc_in"][idx].ap()], outs=[d["cc_out"][idx].ap()]).then_inc(p.sem("cc"), 1),
            kind="dma")
        p.wait("gpsimd", "cc", ccv)
        rv = p.inc("gl_c", 16)
        self_sem = p.sem("gl_c")
        p.emit("gpsimd", lambda e, idx=idx, s=self_sem: e.dma_start(
            out=rowbuf[0:1, 160 + 8 * idx:168 + 8 * idx],
            in_=d["cc_out"][idx].ap()).then_inc(s, 16), kind="dma")
        return rv

    def pcollapse(cols_ap, n, dst_off):
        gv = p.inc("gl_p", 16)
        sp_ = p.sem("gl_p")
        p.emit("gpsimd", lambda e, c=cols_ap, s=sp_: e.dma_start(
            out=d["misc_d"].ap()[0:1, 0:128 * n].rearrange("a (p c) -> (a p) c", c=n),
            in_=c).then_inc(s, 16), kind="dma")
        p.wait("gpsimd", "gl_p", gv)
        gv2 = p.inc("gl_p", 16)
        p.emit("gpsimd", lambda e, n=n, s=sp_: e.dma_start(
            out=rowbuf[0:1, 256:256 + 128 * n],
            in_=d["misc_d"].ap()[0:1, 0:128 * n]).then_inc(s, 16), kind="dma")
        p.wait("vector", "gl_p", gv2)
        for j in range(n):
            p.V(lambda e, j=j, n=n, o=dst_off: e.tensor_reduce(
                rowbuf[0:1, o + j:o + j + 1],
                rowbuf[0:1, 256:256 + 128 * n].rearrange("a (c n) -> a c n", n=n)[:, :, j],
                AX, ALU.max), drain=True)
        return p.sig("vector", "vv")

    xnT = sb(stXN, "xnT", [128, 8 * NT], BF16, side="right")

    xbuf = sb(stA, "xbuf", [128, 2 * H], F32)
    xnml = sb(stA, "xnml", [128, 2 * H], BF16)
    lncols_ = sb(stA, "lncols_", [128, 16], F32)
    bn6_ = sb(stA, "bn6_", [128, 24], F32)
    ptr = psal(stA, "ptr", [128, 2 * 1024], BF16)

    tabio = sb(stA, "tabio", [128, T], F32)
    tabc1 = sb(stA, "tabc1", [128, 8], F32)
    tabth = sb(stA, "tabth", [128, T], F32)
    tabk = sb(stA, "tabk", [128, T], F32)
    tabt1 = sb(stA, "tabt1", [128, T], F32)
    tabsin = sb(stA, "tabsin", [128, T], F32)
    tabcos = sb(stA, "tabcos", [128, T], F32)
    tabs1 = sb(stA, "tabs1", [128, T], F32)
    tabs2 = sb(stA, "tabs2", [128, T], F32)
    tabo = sb(stA, "tabo", [128, T], BF16)

    def emit_tab(ic):
        i0 = ic * 128
        p.gwait("gpsimd", "tabgen_v")
        p.gwait("gpsimd", "tabgen_s")
        p.G(lambda e: e.iota(tabio[:], [[1, T]], channel_multiplier=0,
                             allow_small_or_imprecise_dtypes=True))
        p.G(lambda e, i0=i0: e.iota(tabc1[:, 0:1], [[1, 1]], base=i0, channel_multiplier=1,
                                    allow_small_or_imprecise_dtypes=True))
        gi = p.sig("gpsimd", "gg")
        p.wait("scalar", "gg", gi)
        p.S(lambda e: e.activation(tabc1[:, 1:2], tabc1[:, 0:1], ACT.Exp, scale=-LNF))
        ssc = p.sig("scalar", "sc")
        p.wait("vector", "sc", ssc)
        p.wait("vector", "gg", gi)
        p.V(lambda e: e.tensor_scalar(tabc1[:, 4:5], tabc1[:, 0:1], 2.0 / 1433.6,
                                      409.6 / 1433.6, ALU.mult, ALU.add), drain=True)
        v0l = p.sig("vector", "vv")
        p.wait("scalar", "vv", v0l)
        p.S(lambda e: e.activation(tabc1[:, 2:3], tabc1[:, 4:5], ACT.Ln))
        s0l = p.sig("scalar", "sc")
        p.wait("vector", "sc", s0l)
        p.V(lambda e: e.tensor_scalar(tabc1[:, 3:4], tabc1[:, 2:3], 1.0 / 512, None, ALU.mult))
        p.V(lambda e: e.tensor_scalar(tabth[:], tabio[:], tabc1[:, 1:2], None, ALU.mult), drain=True)
        p.V(lambda e: e.tensor_scalar(tabk[:], tabth[:], INV2PI, MAGIC, ALU.mult, ALU.add), drain=True)
        p.V(lambda e: e.tensor_scalar(tabk[:], tabk[:], MAGIC, None, ALU.subtract), drain=True)
        p.V(lambda e: e.scalar_tensor_tensor(tabt1[:], tabk[:], -C1_2PI, tabth[:], ALU.mult, ALU.add), drain=True)
        p.V(lambda e: e.scalar_tensor_tensor(tabt1[:], tabk[:], -C2_2PI, tabt1[:], ALU.mult, ALU.add), drain=True)
        p.V(lambda e: e.tensor_scalar(tabt1[:], tabt1[:], 3.141592, -3.141592, ALU.min, ALU.max))
        v1 = p.sig("vector", "vv")
        p.wait("scalar", "vv", v1)
        p.S(lambda e: e.activation(tabsin[:], tabt1[:], ACT.Sin))
        s1 = p.sig("scalar", "sc")
        p.wait("vector", "sc", s1)
        p.V(lambda e: e.tensor_scalar(tabk[:], tabth[:], INV2PI, 0.25, ALU.mult, ALU.add), drain=True)
        p.V(lambda e: e.tensor_scalar(tabk[:], tabk[:], MAGIC, MAGIC, ALU.add, ALU.subtract), drain=True)
        p.V(lambda e: e.scalar_tensor_tensor(tabt1[:], tabk[:], -C1_2PI, tabth[:], ALU.mult, ALU.add), drain=True)
        p.V(lambda e: e.scalar_tensor_tensor(tabt1[:], tabk[:], -C2_2PI, tabt1[:], ALU.mult, ALU.add), drain=True)
        p.V(lambda e: e.tensor_scalar(tabt1[:], tabt1[:], HALF_PI, None, ALU.add), drain=True)
        p.V(lambda e: e.tensor_scalar(tabt1[:], tabt1[:], 3.141592, -3.141592, ALU.min, ALU.max))
        p.V(lambda e: e.tensor_scalar(tabs1[:], tabio[:], tabc1[:, 3:4], None, ALU.mult))
        v2 = p.sig("vector", "vv")
        p.wait("scalar", "vv", v2)
        p.S(lambda e: e.activation(tabcos[:], tabt1[:], ACT.Sin))
        p.S(lambda e: e.activation(tabs2[:], tabs1[:], ACT.Exp, scale=-1.0), drain=True)
        p.S(lambda e: e.activation(tabs1[:], tabs1[:], ACT.Exp))
        s2 = p.sig("scalar", "sc")
        p.gset("tabgen_s", "sc", s2)
        p.wait("vector", "sc", s2)
        for j, (trig, scl) in enumerate([(tabcos, tabs1), (tabsin, tabs1),
                                         (tabcos, tabs2), (tabsin, tabs2)]):
            p.V(lambda e, a=trig, bb=scl: e.tensor_tensor(tabo[:], a[:], bb[:], ALU.mult))
            vj = p.sig("vector", "vv")
            p.gset("tabgen_v", "vv", vj)
            p.wait("sync", "vv", vj)
            lv = p.dma("sync", d["tabs"].ap()[j, i0:i0 + 128, :], tabo[:], "ld_tb")
            p.wait("vector", "ld_tb", lv)


    p.wait("tensor", "ld_i", idv)
    for i in range(NT // 128):
        if i > 0 and i % 8 == 0:
            emit_tab(i // 8 - 1)
        b, tr = divmod(i, T // 128)
        bf = i % 2
        xt = xbuf[:, bf * H:(bf + 1) * H]
        xn = xnml[:, bf * H:(bf + 1) * H]
        lncols = lncols_[:, bf * 8:(bf + 1) * 8]
        bn6 = bn6_[:, bf * 12:(bf + 1) * 12]
        p.gwait("sync", "xbuf%d" % bf)
        ldv = p.dma("sync", xt, d["x_in"].ap()[b, tr * 128:(tr + 1) * 128, :], "ld_x%d" % bf)
        p.wait("vector", "ld_x%d" % bf, ldv)
        p.V(lambda e, xt=xt, bn6=bn6: e.bn_stats(bn6[:, 0:6], xt[:, 0:512]))
        p.V(lambda e, xt=xt, bn6=bn6: e.bn_stats(bn6[:, 6:12], xt[:, 512:1024]), drain=True)
        p.V(lambda e, lncols=lncols, bn6=bn6: e.bn_aggr(lncols[:, 0:2], bn6[:, 0:12]), drain=True)
        p.V(lambda e, lncols=lncols: e.tensor_scalar(lncols[:, 1:2], lncols[:, 1:2], EPS_LN, None, ALU.add))
        v1 = p.sig("vector", "vv")
        p.wait("scalar", "vv", v1)
        p.S(lambda e, lncols=lncols: e.activation(lncols[:, 2:3], lncols[:, 1:2], ACT.Sqrt))
        s1 = p.sig("scalar", "sc")
        p.wait("vector", "sc", s1)
        p.V(lambda e, lncols=lncols: e.reciprocal(lncols[:, 3:4], lncols[:, 2:3]), drain=True)
        p.V(lambda e, lncols=lncols: e.scalar_tensor_tensor(lncols[:, 4:5], lncols[:, 0:1], -1.0,
                                                            lncols[:, 3:4], ALU.mult, ALU.mult))
        v2 = p.sig("vector", "vv")
        p.wait("scalar", "vv", v2)
        p.gwait("scalar", "xnml%d" % bf)
        p.S(lambda e, xt=xt, xn=xn, lncols=lncols: e.activation(xn, xt, ACT.Identity,
                                                 bias=lncols[:, 4:5], scale=lncols[:, 3:4]))
        sv = p.sig("scalar", "sc")
        p.gset("xbuf%d" % bf, "sc", sv)
        p.wait("tensor", "sc", sv)
        p.gwait("tensor", "ptr%d" % bf)
        for k in range(8):
            p.TE(lambda e, xn=xn, k=k, bf=bf: e.transpose(
                ptr[:, bf * 1024 + k * 128: bf * 1024 + (k + 1) * 128],
                xn[:, k * 128:(k + 1) * 128], ident[:]))
        tv = p.sig("tensor", "tt")
        p.gset("xnml%d" % bf, "tt", tv)
        p.wait("scalar", "tt", tv)
        for k in range(8):
            p.S(lambda e, k=k, bf=bf, i=i: e.copy(
                xnT[:, k * NT + i * 128: k * NT + (i + 1) * 128],
                ptr[:, bf * 1024 + k * 128: bf * 1024 + (k + 1) * 128]))
        ev = p.sig("scalar", "sc")
        p.gset("ptr%d" % bf, "sc", ev)

    emit_tab(3)
    if "tap_tabs" in DEBUG_TAPS:
        p.dma("sync", d["taps"]["tap_tabs"].ap(),
              d["tabs"].ap().rearrange("a b c -> (a b) c"), "ld_tp")
    if "tap_xnT" in DEBUG_TAPS:
        p.dma("sync", d["taps"]["tap_xnT"].ap(), xnT[:], "ld_tp")

    xnT_sc = p.val("sc")
    p.barrier()
    stA.close()
    if PHASE_LIMIT < 2:
        return

    wqkv = sb(stB, "wqkv", [128, 8 * H], BF16)
    kT = sb(stB, "kT", [128, 8 * T], BF16)
    vtok = sb(stB, "vtok", [128, 16 * H], BF16)
    qT = sb(stB, "qT", [128, 8 * 512], BF16)
    expb = sb(stB, "expb", [128, 16 * 512], BF16)
    ptilb = sb(stB, "ptilb", [128, 16 * 512], BF16)
    tb0 = sb(stB, "tb0", [128, 2 * 512], BF16)
    tb1 = sb(stB, "tb1", [128, 2 * 512], BF16)
    evq = sb(stB, "evq", [128, 2 * 512], BF16)
    rotbuf = sb(stB, "rotbuf", [128, 2 * 512], F32)
    atst = sb(stB, "atst", [128, 512], BF16)
    sumsb = sb(stB, "sumsb", [1, 512], F32, side="right")
    d2s = sb(stB, "d2s", [128, 3968], BF16)
    pmm = psal(stB, "pmm", [128, 2 * 1024], F32)
    ppv = psal(stB, "ppv", [128, 2 * 512], F32)
    psums = psal(stB, "psums", [1, 512], F32)

    p.G(lambda e: e.iota(d2s[:], [[1, 3968]], base=-1920, channel_multiplier=-1,
                         allow_small_or_imprecise_dtypes=True))
    gd2 = p.sig("gpsimd", "gg")
    p.wait("scalar", "gg", gd2)
    p.S(lambda e: e.activation(d2s[:], d2s[:], ACT.Abs), drain=True)
    p.S(lambda e: e.activation(d2s[:], d2s[:], ACT.Exp, scale=LN_GAMMA))
    s_d2 = p.sig("scalar", "sc")
    p.wait("vector", "sc", s_d2)

    def load_w(name, wait_val=None):
        p.wait("gpsimd", "tt", p.val("tt") if wait_val is None else wait_val)
        v = None
        sw = p.sem("gl_w")
        p.dma_streams["gl_w"] = "gpsimd"
        for hc in range(8):
            v = p.inc("gl_w", 16)
            p.emit("gpsimd", lambda e, hc=hc, name=name, s=sw: e.dma_start(
                out=wqkv[:, hc * H:(hc + 1) * H],
                in_=d[name].ap()[hc * 128:(hc + 1) * 128, :]).then_inc(s, 16), kind="dma")
        return v

    def xpos_pair(b, tab_idx, src_tok0, n512, dst, dst_chunk_w, slot_tag):
        for c in range(4):
            for tcb in range(n512):
                sl = (c * n512 + tcb) % 2
                tok_off = src_tok0 + tcb * 512
                p.gwait("sync", "tb%d" % sl)
                lv1 = p.dma("sync", tb0[:, sl * 512:(sl + 1) * 512],
                            d["tabs"].ap()[tab_idx, c * 128:(c + 1) * 128,
                                           (tok_off - b * T):(tok_off - b * T) + 512], "ld_t%d" % sl)
                lv2 = p.dma("sync", tb1[:, sl * 512:(sl + 1) * 512],
                            d["tabs"].ap()[tab_idx + 1, c * 128:(c + 1) * 128,
                                           (tok_off - b * T):(tok_off - b * T) + 512], "ld_t%d" % sl)
                p.gwait("tensor", "pmm%d" % sl)
                for half, cc_ in enumerate((c, c + 4)):
                    for hc in range(8):
                        p.TE(lambda e, cc_=cc_, hc=hc, half=half, sl=sl, tok_off=tok_off: e.matmul(
                            pmm[:, sl * 1024 + half * 512: sl * 1024 + (half + 1) * 512],
                            wqkv[:, hc * H + cc_ * 128: hc * H + (cc_ + 1) * 128],
                            xnT[:, hc * NT + tok_off: hc * NT + tok_off + 512],
                            start=(hc == 0), stop=(hc == 7)))
                tv = p.sig("tensor", "tt")
                p.wait("scalar", "tt", tv)
                p.gwait("scalar", "evq")
                for half in range(2):
                    p.S(lambda e, half=half, sl=sl: e.copy(
                        evq[:, half * 512:(half + 1) * 512],
                        pmm[:, sl * 1024 + half * 512: sl * 1024 + (half + 1) * 512]))
                sv = p.sig("scalar", "sc")
                p.gset("pmm%d" % sl, "sc", sv)
                p.wait("vector", "sc", sv)
                p.wait("vector", "ld_t%d" % sl, lv2)
                r0, r1 = rotbuf[:, 0:512], rotbuf[:, 512:1024]
                Qe, Qo = evq[:, 0:512], evq[:, 512:1024]
                cs = tb0[:, sl * 512:(sl + 1) * 512]
                ss = tb1[:, sl * 512:(sl + 1) * 512]
                o_e = c * dst_chunk_w + tcb * 512
                o_o = (c + 4) * dst_chunk_w + tcb * 512
                p.V(lambda e, cs=cs: e.tensor_tensor(r0, Qe, cs, ALU.mult))
                p.V(lambda e, ss=ss: e.tensor_tensor(r1, Qo, ss, ALU.mult), drain=True)
                p.V(lambda e, o=o_e: e.tensor_tensor(dst[:, o:o + 512], r0, r1, ALU.subtract), drain=True)
                p.V(lambda e, cs=cs: e.tensor_tensor(r0, Qo, cs, ALU.mult))
                p.V(lambda e, ss=ss: e.tensor_tensor(r1, Qe, ss, ALU.mult), drain=True)
                p.V(lambda e, o=o_o: e.tensor_tensor(dst[:, o:o + 512], r0, r1, ALU.add), drain=True)
                ev = p.sig("vector", "vv")
                p.gset("evq", "vv", ev)
                p.gset("tb%d" % sl, "vv", ev)

    for b in range(B):
        tok0 = b * T
        if b == 0:
            p.wait("gpsimd", "gl_x", cast_v)
        wl = load_w("wk16", wait_val=None if b == 0 else wq_free)
        p.wait("tensor", "gl_w", wl)
        xpos_pair(b, 2, tok0, 4, kT, T, "k")
        p.wait("tensor", "vv", p.val("vv"))
        p.wait("tensor", "sc", p.val("sc"))
        wl = load_w("wv_in")
        p.wait("tensor", "gl_w", wl)
        for j in range(16):
            for vc in range(2):
                sl = (j * 2 + vc) % 2
                p.gwait("tensor", "pmm%d" % sl)
                for hc in range(8):
                    p.TE(lambda e, j=j, vc=vc, hc=hc, sl=sl, tok0=tok0: e.matmul(
                        pmm[:, sl * 1024: sl * 1024 + 512],
                        xnT[:, hc * NT + tok0 + j * 128: hc * NT + tok0 + (j + 1) * 128],
                        wqkv[:, hc * H + vc * 512: hc * H + (vc + 1) * 512],
                        start=(hc == 0), stop=(hc == 7)))
                tv = p.sig("tensor", "tt")
                p.wait("scalar", "tt", tv)
                p.S(lambda e, j=j, vc=vc, sl=sl: e.copy(
                    vtok[:, j * H + vc * 512: j * H + (vc + 1) * 512],
                    pmm[:, sl * 1024: sl * 1024 + 512]))
                sv = p.sig("scalar", "sc")
                p.gset("pmm%d" % sl, "sc", sv)
        if b == 1:
            p.wait("gpsimd", "vv", vmax)
            vloc = pcollapse(mxb[:, 8:9], 1, 130)
            p.wait("gpsimd", "vv", vloc)
            cm = collective_max(0, rowbuf[0:1, 130:131], 1)
            p.wait("vector", "gl_c", cm)
            p.V(lambda e: e.tensor_scalar(rowbuf[0:1, 1:2], rowbuf[0:1, 160:161], 1.0 / QMAX, None, ALU.mult), drain=True)
            p.V(lambda e: e.reciprocal(rowbuf[0:1, 0:1], rowbuf[0:1, 1:2]), drain=True)
            vsx = p.sig("vector", "vv")
        p.wait("tensor", "sc", p.val("sc"))
        wl = load_w("wq_in")
        p.wait("tensor", "gl_w", wl)

        if b == 0 and "tap_kT" in DEBUG_TAPS:
            p.wait("sync", "vv", p.val("vv"))
            p.dma("sync", d["taps"]["tap_kT"].ap(), kT[:], "ld_tp")
        if b == 0 and "tap_vtok" in DEBUG_TAPS:
            p.wait("sync", "sc", p.val("sc"))
            p.dma("sync", d["taps"]["tap_vtok"].ap(), vtok[:], "ld_tp")

        xpos_pair(b, 0, tok0, 1, qT, 512, "q")
        if b == 0 and "tap_qT" in DEBUG_TAPS:
            p.wait("sync", "vv", p.val("vv"))
            p.dma("sync", d["taps"]["tap_qT"].ap(), qT[:], "ld_tp")
        qdone = p.val("vv")
        for qc in range(4):
            q0 = qc * 512
            p.wait("tensor", "vv", qdone)

            for kc in range(16):
                sl = kc % 2
                k0 = kc * 128
                p.gwait("tensor", "pmm%d" % sl)
                for dc in range(8):
                    p.TE(lambda e, dc=dc, kc=kc, sl=sl: e.matmul(
                        pmm[:, sl * 1024: sl * 1024 + 512],
                        kT[:, dc * T + kc * 128: dc * T + (kc + 1) * 128],
                        qT[:, dc * 512:(dc + 1) * 512],
                        start=(dc == 0), stop=(dc == 7)))
                tv = p.sig("tensor", "tt")
                p.wait("scalar", "tt", tv)
                p.S(lambda e, kc=kc, sl=sl: e.activation(
                    expb[:, kc * 512:(kc + 1) * 512],
                    pmm[:, sl * 1024: sl * 1024 + 512], ACT.Exp))
                sv = p.sig("scalar", "sc")
                p.gset("pmm%d" % sl, "sc", sv)
                p.wait("vector", "sc", sv)
                p.V(lambda e, kc=kc, k0=k0, q0=q0: e.tensor_tensor(
                    ptilb[:, kc * 512:(kc + 1) * 512], expb[:, kc * 512:(kc + 1) * 512],
                    d2s[:, 1920 + q0 - k0: 1920 + q0 - k0 + 512], ALU.mult))
                pv = p.sig("vector", "vv")
            p.wait("tensor", "sc", p.val("sc"))
            p.gwait("tensor", "psums")
            for kc in range(16):
                p.TE(lambda e, kc=kc: e.matmul(psums[0:1, :], onesb[:, 0:1],
                                               expb[:, kc * 512:(kc + 1) * 512],
                                               start=(kc == 0), stop=(kc == 15)))
            tsum = p.sig("tensor", "tt")
            p.wait("scalar", "tt", tsum)
            p.gwait("scalar", "psums")
            p.gwait("scalar", "sumsb")
            p.S(lambda e: e.copy(sumsb[0:1, :], psums[0:1, :]))
            sv = p.sig("scalar", "sc")
            p.gset("psums", "sc", sv)
            p.wait("vector", "sc", sv)
            p.V(lambda e: e.reciprocal(sumsb[0:1, :], sumsb[0:1, :]))
            vsum = p.sig("vector", "vv")
            p.wait("scalar", "vv", vsum)
            sdq = p.dma("scalar", d["sums_d"].ap()[b:b + 1, q0:q0 + 512],
                        sumsb[0:1, :], "sd_s")
            p.gset("sumsb", "sd_s", sdq)

            pt_done = pv
            if qc < 3:
                xpos_pair(b, 0, tok0 + q0 + 512, 1, qT, 512, "q")
                qdone = p.val("vv")
                if qc == 2:
                    wq_free = p.val("tt")
            p.wait("tensor", "vv", pt_done)
            for vc in range(8):
                sl = vc % 2
                p.gwait("tensor", "ppv%d" % sl)
                for kc in range(16):
                    p.TE(lambda e, vc=vc, kc=kc, sl=sl: e.matmul(
                        ppv[:, sl * 512:(sl + 1) * 512],
                        vtok[:, kc * H + vc * 128: kc * H + (vc + 1) * 128],
                        ptilb[:, kc * 512:(kc + 1) * 512],
                        start=(kc == 0), stop=(kc == 15)))
                tv = p.sig("tensor", "tt")
                p.wait("scalar", "tt", tv)
                p.gwait("scalar", "atst")
                p.S(lambda e, sl=sl: e.copy(atst[:, 0:512],
                                            ppv[:, sl * 512:(sl + 1) * 512]))
                sv = p.sig("scalar", "sc")
                p.gset("ppv%d" % sl, "sc", sv)
                p.wait("scalar", "sc", sv)
                dv = p.dma("scalar", d["attnT_d"].ap()[b, vc * 128:(vc + 1) * 128, q0:q0 + 512],
                           atst[:, 0:512], "sd_a0")
                p.gset("atst", "sd_a0", dv)


        if b == 0:
            p.wait("vector", "sc", xnT_sc)
            p.V(lambda e: e.drain())
            for k in range(8):
                p.V(lambda e, k=k: e.tensor_reduce(mxb[:, k:k + 1], xnT[:, k * NT:(k + 1) * NT],
                                                   AX, ALU.max, apply_absolute_value=True))
            p.V(lambda e: e.drain())
            p.V(lambda e: e.tensor_reduce(mxb[:, 8:9], mxb[:, 0:8], AX, ALU.max))
            vmax = p.sig("vector", "vv")

    p.wait("tensor", "vv", vsx)
    p.wait("tensor", "gg", gset)
    bcast_col(rowbuf[0:1, 0:1], colbuf[:, 0:1])
    bcx = bcast_col(rowbuf[0:1, 1:2], colbuf[:, 1:2])
    p.wait("gpsimd", "sc", bcx)
    if "tap_scales" in DEBUG_TAPS and PHASE_LIMIT < 4:
        p.wait("sync", "sc", p.val("sc"))
        p.dma("sync", d["taps"]["tap_scales"].ap()[0:1, 0:64], rowbuf[0:1, 0:64], "ld_tp")

    p.barrier()
    stB.close()
    if PHASE_LIMIT < 3:
        return

    xqT = sb(stC, "xqT", [128, 8 * NT], BF16)
    stPB = ExitStack()
    rtmp = sb(stPB, "rtmp", [128, NT], F32)
    for k in range(8):
        p.V(lambda e, k=k: e.tensor_scalar(rtmp[:], xnT[:, k * NT:(k + 1) * NT],
                                           colbuf[:, 0:1], MAGIC, ALU.mult, ALU.add), drain=True)
        p.V(lambda e, k=k: e.tensor_scalar(xqT[:, k * NT:(k + 1) * NT], rtmp[:],
                                           MAGIC, colbuf[:, 1:2], ALU.subtract, ALU.mult), drain=True)
    vxq = p.sig("vector", "vv")
    if "tap_xqT" in DEBUG_TAPS:
        p.wait("sync", "vv", vxq)
        p.dma("sync", d["taps"]["tap_xqT"].ap(), xqT[:], "ld_tp")

    p.barrier()
    stXN.close()
    stPB.close()
    if PHASE_LIMIT < 4:
        return

    wst = sb(stC, "wst", [128, 2 * H], BF16)
    wnat = sb(stC, "wnat", [128, 2 * H], F32)
    wsc = sb(stC, "wsc", [128, 256], F32)
    wq_b = sb(stC, "wq_b", [128, 2 * H], F32)
    wqb16 = sb(stC, "wqb16", [128, 4 * H], BF16)
    stg = sb(stC, "stg", [128, 4 * 512], BF16)
    mxc = sb(stC, "mxc", [128, 392], F32)
    pwt = psal(stC, "pwt", [128, 2 * 1024], BF16)
    pug = psal(stC, "pug", [128, 2 * 512], F32)

    def deq_chunk(cb):
        src_name = "wu_in" if cb < 24 else "wg_in"
        ic = cb % 24
        bf = cb % 2
        s4 = cb % 4
        nblk = H // 32
        wn = wnat[:, bf * H: bf * H + H]
        ws0 = bf * 128
        wqb = wq_b[:, bf * H: bf * H + H]
        w16 = wqb16[:, s4 * H: s4 * H + H]
        p.gwait("sync", "wnat%d" % bf)
        lv = p.dma("sync", wn, d[src_name].ap()[ic * 128:(ic + 1) * 128, :], "ld_w%d" % bf)
        p.wait("vector", "ld_w%d" % bf, lv)
        p.gwait("vector", "wqb16%d" % s4)
        p.V(lambda e, wn=wn, nblk=nblk, ws0=ws0: e.tensor_reduce(
            wsc[:, ws0:ws0 + nblk], wn.rearrange("p (b j) -> p b j", j=32), AX, ALU.max,
            apply_absolute_value=True), drain=True)
        p.V(lambda e, nblk=nblk, ws0=ws0: e.tensor_scalar(wsc[:, ws0:ws0 + nblk], wsc[:, ws0:ws0 + nblk],
                                                          1.0 / WQMAX, None, ALU.mult), drain=True)
        p.V(lambda e, nblk=nblk, ws0=ws0: e.reciprocal(wsc[:, ws0 + nblk:ws0 + 2 * nblk],
                                                       wsc[:, ws0:ws0 + nblk]), drain=True)
        p.V(lambda e, wn=wn, nblk=nblk, ws0=ws0, wqb=wqb: e.tensor_tensor(
            wqb.rearrange("p (b j) -> p b j", j=32),
            wn.rearrange("p (b j) -> p b j", j=32),
            bass.AP(wsc, ws0 + nblk, [[256, 128], [1, nblk], [0, 32]]),
            ALU.mult), drain=True)
        vwn = p.sig("vector", "vv")
        p.gset("wnat%d" % bf, "vv", vwn)
        p.V(lambda e, wqb=wqb: e.tensor_scalar(wqb, wqb,
                                               MAGIC, MAGIC, ALU.add, ALU.subtract), drain=True)
        p.V(lambda e, nblk=nblk, ws0=ws0, wqb=wqb, w16=w16: e.tensor_tensor(
            w16.rearrange("p (b j) -> p b j", j=32),
            wqb.rearrange("p (b j) -> p b j", j=32),
            bass.AP(wsc, ws0, [[256, 128], [1, nblk], [0, 32]]),
            ALU.mult))
        return p.sig("vector", "vv")

    def deq_transpose(cb, vw):
        bf = cb % 2
        s4 = cb % 4
        w16 = wqb16[:, s4 * H: s4 * H + H]
        p.wait("tensor", "vv", vw)
        p.gwait("tensor", "pwt%d" % bf)
        for k in range(8):
            p.TE(lambda e, k=k, bf=bf, w16=w16: e.transpose(
                pwt[:, bf * 1024 + k * 128: bf * 1024 + k * 128 + 128],
                w16[:, k * 128:(k + 1) * 128], ident[:]))
        tv = p.sig("tensor", "tt")
        p.gset("wqb16%d" % s4, "tt", tv)
        p.wait("scalar", "tt", tv)
        p.gwait("scalar", "wstw%d" % bf)
        for k in range(8):
            p.S(lambda e, k=k, bf=bf: e.copy(
                wst[:, bf * H + k * 128: bf * H + (k + 1) * 128],
                pwt[:, bf * 1024 + k * 128: bf * 1024 + k * 128 + 128]))
        sv = p.sig("scalar", "sc")
        p.gset("pwt%d" % bf, "sc", sv)
        p.gset("wstr%d" % bf, "sc", sv)

    def upgate_mms(cb):
        bf = cb % 2
        ib = cb % 24
        spname = "sp_up" if cb < 24 else "sp_gate"
        p.gwait("tensor", "wstr%d" % bf)
        for tcol in range(8):
            tile_idx = cb * 8 + tcol
            sl = tile_idx % 4
            p.gwait("tensor", "pug%d" % (tile_idx % 2))
            for hc in range(8):
                p.TE(lambda e, bf=bf, tcol=tcol, hc=hc, sl=sl: e.matmul(
                    pug[:, (sl % 2) * 512:((sl % 2) + 1) * 512],
                    wst[:, bf * H + hc * 128: bf * H + (hc + 1) * 128],
                    xqT[:, hc * NT + tcol * 512: hc * NT + (tcol + 1) * 512],
                    start=(hc == 0), stop=(hc == 7)))
            tv = p.sig("tensor", "tt")
            if tcol == 7:
                p.gset("wstw%d" % bf, "tt", tv)
            p.wait("scalar", "tt", tv)
            p.gwait("scalar", "stgv%d" % sl)
            p.gwait("scalar", "stgd%d" % sl)
            p.S(lambda e, sl=sl: e.copy(stg[:, sl * 512:(sl + 1) * 512],
                                        pug[:, (sl % 2) * 512:((sl % 2) + 1) * 512]))
            sv = p.sig("scalar", "sc")
            p.gset("pug%d" % (tile_idx % 2), "sc", sv)
            p.wait("scalar", "sc", sv)
            dv = p.dma("scalar", d[spname].ap()[ib * 128:(ib + 1) * 128,
                                                tcol * 512:(tcol + 1) * 512],
                       stg[:, sl * 512:(sl + 1) * 512], "sd_p%d" % sl)
            p.wait("vector", "sc", sv)
            p.V(lambda e, sl=sl, col=4 + tile_idx: e.tensor_reduce(
                mxc[:, col:col + 1], stg[:, sl * 512:(sl + 1) * 512],
                AX, ALU.max, apply_absolute_value=True))
            mv = p.sig("vector", "vv")
            p.gset("stgv%d" % sl, "vv", mv)
            p.gset("stgd%d" % sl, "sd_p%d" % sl, dv)

    vws = {}
    for cb in range(52):
        if cb < 48:
            vws[cb] = deq_chunk(cb)
        if 3 <= cb < 51:
            deq_transpose(cb - 3, vws[cb - 3])
        if cb >= 4:
            upgate_mms(cb - 4)

    p.V(lambda e: e.drain(), drain=False)
    p.V(lambda e: e.tensor_reduce(mxc[:, 0:1], mxc[:, 4:196], AX, ALU.max), drain=True)
    p.V(lambda e: e.tensor_reduce(mxc[:, 1:2], mxc[:, 196:388], AX, ALU.max), drain=True)
    vmx = p.sig("vector", "vv")
    p.wait("gpsimd", "vv", vmx)
    pcollapse(mxc[:, 0:1], 1, 132)
    vloc = pcollapse(mxc[:, 1:2], 1, 133)
    p.wait("gpsimd", "vv", vloc)
    cm = collective_max(1, rowbuf[0:1, 132:134], 2)
    p.wait("vector", "gl_c", cm)
    p.V(lambda e: e.tensor_scalar(rowbuf[0:1, 3:4], rowbuf[0:1, 168:169], 1.0 / QMAX, None, ALU.mult), drain=True)
    p.V(lambda e: e.reciprocal(rowbuf[0:1, 2:3], rowbuf[0:1, 3:4]), drain=True)
    p.V(lambda e: e.tensor_scalar(rowbuf[0:1, 5:6], rowbuf[0:1, 169:170], 1.0 / QMAX, None, ALU.mult), drain=True)
    p.V(lambda e: e.reciprocal(rowbuf[0:1, 4:5], rowbuf[0:1, 5:6]), drain=True)
    p.V(lambda e: e.tensor_scalar(rowbuf[0:1, 6:7], rowbuf[0:1, 5:6], 1.0 / 65536.0, None, ALU.mult), drain=True)
    p.V(lambda e: e.tensor_scalar(rowbuf[0:1, 7:8], rowbuf[0:1, 5:6], -MAGIC, None, ALU.mult), drain=True)
    vsc2 = p.sig("vector", "vv")
    p.wait("tensor", "vv", vsc2)
    for src, dstc in ((2, 2), (3, 3), (4, 4), (5, 5), (6, 6), (7, 7)):
        bcast_col(rowbuf[0:1, src:src + 1], colbuf[:, dstc:dstc + 1])

    if "tap_scales" in DEBUG_TAPS:
        p.wait("sync", "sc", p.val("sc"))
        p.dma("sync", d["taps"]["tap_scales"].ap()[0:1, 0:64], rowbuf[0:1, 0:64], "ld_tp")

    p.barrier()
    stC.close()
    if PHASE_LIMIT < 5:
        return

    wdT = sb(stD, "wdT", [128, 24 * H], BF16)
    stDs = ExitStack()
    gin = sb(stDs, "gin", [128, 4 * 2048], BF16)
    t1b_ = sb(stDs, "t1b_", [128, 4 * 2048], F32)
    rb__ = sb(stDs, "rb__", [128, 2048], F32)
    sgb_ = sb(stDs, "sgb_", [128, 4 * 2048], F32)
    f3b_ = sb(stDs, "f3b_", [128, 2048], F32)
    g2o = sb(stDs, "g2o", [128, 4 * 2048], BF16)
    mxd = sb(stDs, "mxd", [128, 104], F32)

    n_sweep = 48
    pend_mx = None
    for ti in range(n_sweep):
        ic, th = divmod(ti, 2)
        bf = ti % 4
        g_t = gin[:, bf * 2048:(bf + 1) * 2048]
        o_t = g2o[:, bf * 2048:(bf + 1) * 2048]
        t1b = t1b_[:, bf * 2048:(bf + 1) * 2048]
        rb_ = rb__[:, 0:2048]
        sgb = sgb_[:, bf * 2048:(bf + 1) * 2048]
        f3b = f3b_[:, 0:2048]
        p.gwait("sync", "gin%d" % bf)
        lv = p.dma("sync", g_t, d["sp_gate"].ap()[ic * 128:(ic + 1) * 128,
                                                  th * 2048:(th + 1) * 2048], "ld_g%d" % bf)
        p.wait("scalar", "ld_g%d" % bf, lv)
        p.gwait("scalar", "pdch%d" % bf)
        p.gwait("scalar", "pdcv%d" % bf)
        p.S(lambda e, g_t=g_t, t1b=t1b: e.activation(t1b, g_t, ACT.Identity,
                                                     scale=colbuf[:, 4:5], bias=colbuf[:, 12:13]))
        p.emit("scalar", lambda e: e.drain(), kind="drain")
        p.S(lambda e, sgb=sgb, t1b=t1b: e.activation(sgb, t1b, ACT.Sigmoid,
                                                     scale=colbuf[:, 5:6], bias=colbuf[:, 7:8]))
        s1 = p.sig("scalar", "sc")
        p.gset("gin%d" % bf, "sc", s1)
        p.wait("vector", "sc", s1)
        p.V(lambda e, rb_=rb_, sgb=sgb: e.tensor_scalar(rb_, sgb, 65536.0, MAGIC, ALU.mult, ALU.add))
        if pend_mx is not None:
            po_t, pc_, pbf = pend_mx
            p.emit("vector", lambda e: e.drain(), kind="drain")
            p.V(lambda e, po_t=po_t, pc_=pc_: e.tensor_reduce(
                mxd[:, pc_:pc_ + 1], po_t, AX, ALU.max, apply_absolute_value=True))
            vmxt = p.sig("vector", "vv")
            p.gset("g2g%d" % pbf, "vv", vmxt)
        else:
            p.emit("vector", lambda e: e.drain(), kind="drain")
        p.V(lambda e, f3b=f3b, rb_=rb_: e.tensor_scalar(f3b, rb_, MAGIC, 65535.0, ALU.subtract, ALU.min), drain=True)
        p.gset("pdcv%d" % bf, "vv", p.val("vv") + 1)
        p.gwait("vector", "g2o%d" % bf)
        p.gwait("vector", "g2g%d" % bf)
        p.V(lambda e, o_t=o_t, t1b=t1b, f3b=f3b: e.scalar_tensor_tensor(
            o_t, t1b, MAGIC, f3b, ALU.subtract, ALU.mult))
        v2 = p.sig("vector", "vv")
        p.gset("pdch%d" % bf, "vv", v2)
        pend_mx = (o_t, 4 + ti, bf)
        p.gwait("tensor", "pcol")
        p.wait("tensor", "vv", v2)
        p.TE(lambda e: e.matmul(pcol[0:1, 0:1], onesf[0:1, 0:1], onesf[0:1, 0:1],
                                start=True, stop=True))
        p.wait("gpsimd", "vv", v2)
        dv = p.dma("gpsimd", d["sp_g2"].ap()[ic * 128:(ic + 1) * 128,
                                             th * 2048:(th + 1) * 2048], o_t, "sd_g%d" % bf)
        p.gset("g2o%d" % bf, "sd_g%d" % bf, dv)

    po_t, pc_, pbf = pend_mx
    p.V(lambda e: e.drain(), drain=False)
    p.V(lambda e, po_t=po_t, pc_=pc_: e.tensor_reduce(
        mxd[:, pc_:pc_ + 1], po_t, AX, ALU.max, apply_absolute_value=True), drain=True)
    p.V(lambda e: e.tensor_reduce(mxd[:, 0:1], mxd[:, 4:52], AX, ALU.max), drain=True)
    vmx = p.sig("vector", "vv")
    if PHASE_LIMIT < 4.4:
        return
    p.wait("gpsimd", "vv", vmx)
    vloc = pcollapse(mxd[:, 0:1], 1, 136)
    p.wait("gpsimd", "vv", vloc)
    cm = collective_max(2, rowbuf[0:1, 136:137], 1)
    if PHASE_LIMIT < 4.7:
        return
    wdnat = sb(stDw, "wdnat", [128, 2 * I], F32)
    wdsc = sb(stDw, "wdsc", [128, 256], F32)
    wdq = sb(stDw, "wdq", [128, I], F32)
    wdq16 = sb(stDw, "wdq16", [128, I], BF16)
    pwd = psal(stDw, "pwd", [128, 2 * 1024], BF16)
    for hc in range(8):
        bf = hc % 2
        wn = wdnat[:, bf * I:(bf + 1) * I]
        p.gwait("sync", "wdnat%d" % bf)
        lv = p.dma("sync", wn, d["wd_in"].ap()[hc * 128:(hc + 1) * 128, :], "ld_w%d" % bf)
        p.wait("vector", "ld_w%d" % bf, lv)
        p.gwait("vector", "wdq16")
        p.V(lambda e, wn=wn: e.tensor_reduce(
            wdsc[:, 0:96], wn.rearrange("p (b j) -> p b j", j=32), AX, ALU.max,
            apply_absolute_value=True), drain=True)
        p.V(lambda e: e.tensor_scalar(wdsc[:, 0:96], wdsc[:, 0:96], 1.0 / WQMAX, None, ALU.mult), drain=True)
        p.V(lambda e: e.reciprocal(wdsc[:, 96:192], wdsc[:, 0:96]), drain=True)
        p.V(lambda e, wn=wn: e.tensor_tensor(
            wdq[:].rearrange("p (b j) -> p b j", j=32),
            wn.rearrange("p (b j) -> p b j", j=32),
            bass.AP(wdsc, 96, [[256, 128], [1, 96], [0, 32]]),
            ALU.mult), drain=True)
        vwn = p.sig("vector", "vv")
        p.gset("wdnat%d" % bf, "vv", vwn)
        p.V(lambda e: e.tensor_scalar(wdq[:], wdq[:], MAGIC, MAGIC, ALU.add, ALU.subtract), drain=True)
        p.V(lambda e: e.tensor_tensor(
            wdq16[:].rearrange("p (b j) -> p b j", j=32),
            wdq[:].rearrange("p (b j) -> p b j", j=32),
            bass.AP(wdsc, 0, [[256, 128], [1, 96], [0, 32]]),
            ALU.mult))
        vw = p.sig("vector", "vv")
        p.wait("tensor", "vv", vw)
        for k in range(24):
            sl2 = k % 8
            if sl2 == 0:
                p.gwait("tensor", "pwd%d" % ((k // 8) % 2))
            p.TE(lambda e, k=k, sl2=sl2, g=(k // 8) % 2: e.transpose(
                pwd[:, g * 1024 + sl2 * 128: g * 1024 + sl2 * 128 + 128],
                wdq16[:, k * 128:(k + 1) * 128], ident[:]))
            if sl2 == 7 or k == 23:
                tv = p.sig("tensor", "tt")
                p.wait("scalar", "tt", tv)
                kk0 = (k // 8) * 8
                for kk in range(kk0, k + 1):
                    p.S(lambda e, kk=kk, hc=hc, g=(k // 8) % 2: e.copy(
                        wdT[:, kk * H + hc * 128: kk * H + (hc + 1) * 128],
                        pwd[:, g * 1024 + (kk % 8) * 128: g * 1024 + (kk % 8) * 128 + 128]))
                sv = p.sig("scalar", "sc")
                p.gset("pwd%d" % ((k // 8) % 2), "sc", sv)
        p.gset("wdq16", "sc", p.val("sc"))

    if "tap_wdT" in DEBUG_TAPS:
        p.wait("sync", "sc", p.val("sc"))
        p.dma("sync", d["taps"]["tap_wdT"].ap(), wdT[:], "ld_tp")

    p.barrier()
    stDw.close()
    if PHASE_LIMIT < 6:
        return

    p.wait("vector", "gl_c", cm)
    p.V(lambda e: e.tensor_tensor(rowbuf[0:1, 176:177], rowbuf[0:1, 176:177],
                                  rowbuf[0:1, 6:7], ALU.mult), drain=True)
    p.V(lambda e: e.tensor_scalar(rowbuf[0:1, 9:10], rowbuf[0:1, 176:177], 1.0 / QMAX, None, ALU.mult), drain=True)
    p.V(lambda e: e.reciprocal(rowbuf[0:1, 8:9], rowbuf[0:1, 9:10]), drain=True)
    p.V(lambda e: e.tensor_tensor(rowbuf[0:1, 13:14], rowbuf[0:1, 8:9],
                                  rowbuf[0:1, 6:7], ALU.mult), drain=True)
    vsc3 = p.sig("vector", "vv")
    p.wait("tensor", "vv", vsc3)
    bcast_col(rowbuf[0:1, 13:14], colbuf[:, 8:9])
    bc3 = bcast_col(rowbuf[0:1, 9:10], colbuf[:, 9:10])
    p.wait("vector", "sc", bc3)

    stD2a = ExitStack()
    uin = sb(stD2a, "uin", [128, 4 * 2048], BF16)
    pend_mx = None
    for ti in range(n_sweep):
        ic, th = divmod(ti, 2)
        bf = ti % 4
        g_t = gin[:, bf * 2048:(bf + 1) * 2048]
        u_t = uin[:, bf * 2048:(bf + 1) * 2048]
        o_t = g2o[:, bf * 2048:(bf + 1) * 2048]
        t1b = t1b_[:, bf * 2048:(bf + 1) * 2048]
        rb_ = rb__[:, 0:2048]
        sgb = sgb_[:, bf * 2048:(bf + 1) * 2048]
        f3b = f3b_[:, 0:2048]
        p.gwait("sync", "gin%d" % bf)
        lv1 = p.dma("sync", g_t, d["sp_g2"].ap()[ic * 128:(ic + 1) * 128,
                                                 th * 2048:(th + 1) * 2048], "ld_g%d" % bf)
        p.gwait("sync", "uin%d" % bf)
        lv2 = p.dma("sync", u_t, d["sp_up"].ap()[ic * 128:(ic + 1) * 128,
                                                 th * 2048:(th + 1) * 2048], "ld_u%d" % bf)
        p.wait("scalar", "ld_g%d" % bf, lv1)
        p.wait("scalar", "ld_u%d" % bf, lv2)
        p.gwait("scalar", "pdch%d" % bf)
        p.gwait("scalar", "pdcv%d" % bf)
        p.S(lambda e, g_t=g_t, t1b=t1b: e.activation(t1b, g_t, ACT.Identity,
                                                     scale=colbuf[:, 8:9], bias=colbuf[:, 12:13]))
        p.emit("scalar", lambda e: e.drain(), kind="drain")
        p.S(lambda e, u_t=u_t, sgb=sgb: e.activation(sgb, u_t, ACT.Identity,
                                                     scale=colbuf[:, 2:3], bias=colbuf[:, 12:13]))
        s1 = p.sig("scalar", "sc")
        p.gset("gin%d" % bf, "sc", s1)
        p.gset("uin%d" % bf, "sc", s1)
        p.wait("vector", "sc", s1)
        p.V(lambda e, rb_=rb_, t1b=t1b: e.tensor_scalar(rb_, t1b, MAGIC, None, ALU.subtract))
        if pend_mx is not None:
            po_t, pc_, pbf = pend_mx
            p.emit("vector", lambda e: e.drain(), kind="drain")
            p.V(lambda e, po_t=po_t, pc_=pc_: e.tensor_reduce(
                mxd[:, pc_:pc_ + 1], po_t, AX, ALU.max, apply_absolute_value=True))
            vmxt = p.sig("vector", "vv")
            p.gset("g2g%d" % pbf, "vv", vmxt)
        else:
            p.emit("vector", lambda e: e.drain(), kind="drain")
        p.V(lambda e, f3b=f3b, sgb=sgb: e.tensor_scalar(f3b, sgb, MAGIC, None, ALU.subtract), drain=True)
        p.gset("pdcv%d" % bf, "vv", p.val("vv") + 1)
        p.gwait("vector", "g2o%d" % bf)
        p.gwait("vector", "g2g%d" % bf)
        p.V(lambda e, o_t=o_t, rb_=rb_, f3b=f3b: e.tensor_tensor(o_t, rb_, f3b, ALU.mult))
        v2 = p.sig("vector", "vv")
        p.gset("pdch%d" % bf, "vv", v2)
        pend_mx = (o_t, 52 + ti, bf)
        p.gwait("tensor", "pcol")
        p.wait("tensor", "vv", v2)
        p.TE(lambda e: e.matmul(pcol[0:1, 0:1], onesf[0:1, 0:1], onesf[0:1, 0:1],
                                start=True, stop=True))
        p.wait("gpsimd", "vv", v2)
        dv = p.dma("gpsimd", d["sp_m"].ap()[ic * 128:(ic + 1) * 128,
                                            th * 2048:(th + 1) * 2048], o_t, "sd_g%d" % bf)
        p.gset("g2o%d" % bf, "sd_g%d" % bf, dv)

    po_t, pc_, pbf = pend_mx
    p.V(lambda e: e.drain(), drain=False)
    p.V(lambda e, po_t=po_t, pc_=pc_: e.tensor_reduce(
        mxd[:, pc_:pc_ + 1], po_t, AX, ALU.max, apply_absolute_value=True), drain=True)
    p.V(lambda e: e.tensor_reduce(mxd[:, 1:2], mxd[:, 52:100], AX, ALU.max), drain=True)
    vmx = p.sig("vector", "vv")
    p.wait("gpsimd", "vv", vmx)
    vloc = pcollapse(mxd[:, 1:2], 1, 137)
    p.wait("gpsimd", "vv", vloc)
    cm = collective_max(3, rowbuf[0:1, 137:138], 1)
    p.wait("vector", "gl_c", cm)
    p.V(lambda e: e.tensor_scalar(rowbuf[0:1, 20:21], rowbuf[0:1, 184:185], 1.0 / QMAX, None, ALU.mult), drain=True)
    p.V(lambda e: e.tensor_tensor(rowbuf[0:1, 21:22], rowbuf[0:1, 20:21], rowbuf[0:1, 9:10], ALU.mult), drain=True)
    p.V(lambda e: e.tensor_tensor(rowbuf[0:1, 11:12], rowbuf[0:1, 21:22], rowbuf[0:1, 3:4], ALU.mult), drain=True)
    p.V(lambda e: e.tensor_tensor(rowbuf[0:1, 22:23], rowbuf[0:1, 9:10], rowbuf[0:1, 3:4], ALU.mult), drain=True)
    p.V(lambda e: e.reciprocal(rowbuf[0:1, 23:24], rowbuf[0:1, 11:12]), drain=True)
    p.V(lambda e: e.tensor_tensor(rowbuf[0:1, 10:11], rowbuf[0:1, 22:23], rowbuf[0:1, 23:24], ALU.mult), drain=True)
    vsc4 = p.sig("vector", "vv")
    p.wait("tensor", "vv", vsc4)
    bcast_col(rowbuf[0:1, 10:11], colbuf[:, 10:11])
    bc4 = bcast_col(rowbuf[0:1, 11:12], colbuf[:, 11:12])
    p.wait("vector", "sc", bc4)
    p.barrier()
    stD2a.close()
    stDs.close()
    if PHASE_LIMIT < 7:
        p.barrier()
        return

    oqT_ = sb(stD2, "oqT_", [128, 2 * 24 * 512], BF16)
    oqrs = sb(stD2, "oqrs", [128, 1024], F32)
    minb = sb(stD2, "minb", [128, 4 * 1024], BF16)
    I4 = I * 4
    attin = sb(stD2, "attin", [128, 4 * 1024], BF16)
    xres = sb(stD2, "xres", [128, 4 * 1024], F32)
    accb = sb(stD2, "accb", [128, 2 * 1024], F32)
    outb = sb(stD2, "outb", [128, 2 * 1024], F32)
    rrow = sb(stD2, "rrow", [1, T], F32, side="right")
    rcol1 = sb(stD2, "rcol1", [128, 2], F32)
    pdown = psal(stD2, "pdown", [128, 2 * 1024], F32)
    patt = psal(stD2, "patt", [128, 2 * 1024], BF16)

    pend_store = []
    for b in range(B):
        p.gwait("sync", "rrow")
        rv = p.dma("sync", rrow[0:1, :], d["sums_d"].ap()[b:b + 1, :], "ld_rr")
        p.wait("tensor", "ld_rr", rv)
        for st_ in range(4):
            t0 = st_ * 512
            ssl = st_ % 2
            oqT = oqT_[:, ssl * 24 * 512:(ssl + 1) * 24 * 512]
            p.gwait("vector", "oqT%d" % ssl)
            for icp in range(12):
                bf = (st_ * 12 + icp) % 4
                mt = minb[:, bf * 1024: bf * 1024 + 1024]
                p.gwait("gpsimd", "minb%d" % bf)
                lv = p.dma("gpsimd", mt,
                           bass.AP(d["sp_m"], (2 * icp * 128) * NT + b * T + t0,
                                   [[NT, 128], [128 * NT, 2], [1, 512]]), "ld_m%d" % bf)
                p.wait("vector", "ld_m%d" % bf, lv)
                p.V(lambda e, mt=mt: e.tensor_scalar(oqrs[:, 0:1024], mt, colbuf[:, 10:11],
                                                     MAGIC, ALU.mult, ALU.add), drain=True)
                vm = p.sig("vector", "vv")
                p.gset("minb%d" % bf, "vv", vm)
                p.V(lambda e, icp=icp, oqT=oqT: e.tensor_scalar(
                    oqT[:, icp * 1024:(icp + 1) * 1024],
                    oqrs[:, 0:1024], MAGIC, colbuf[:, 11:12],
                    ALU.subtract, ALU.mult), drain=True)
            voq = p.sig("vector", "vv")
            p.wait("tensor", "vv", voq)
            avs = {}
            xvs = {}
            for ts_ in range(4):
                sl4 = ts_ % 4
                tr = (t0 + ts_ * 128) // 128
                p.gwait("sync", "attin%d" % sl4)
                avs[ts_] = p.dma("sync", attin[:, sl4 * 1024:(sl4 + 1) * 1024],
                                 bass.AP(d["attnT_d"], b * H * T + tr * 128,
                                         [[T, 128], [128 * T, 8], [1, 128]]), "ld_a%d" % sl4)
                p.gwait("sync", "xres%d" % sl4)
                xvs[ts_] = p.dma("sync", xres[:, sl4 * 1024:(sl4 + 1) * 1024],
                                 d["x_in"].ap()[b, tr * 128:(tr + 1) * 128, :], "ld_xr%d" % sl4)
            for ts_ in range(4):
                tc_g = (b * T + t0 + ts_ * 128) // 128
                sl = ts_ % 2
                sl4 = ts_ % 4
                tr = (t0 + ts_ * 128) // 128
                av = avs[ts_]
                xv = xvs[ts_]
                p.gwait("tensor", "pdown%d" % sl)
                for dcol in range(2):
                    for ic in range(24):
                        p.TE(lambda e, ic=ic, dcol=dcol, ts_=ts_, sl=sl, oqT=oqT: e.matmul(
                            pdown[:, sl * 1024 + dcol * 512: sl * 1024 + (dcol + 1) * 512],
                            oqT[:, ic * 512 + ts_ * 128: ic * 512 + (ts_ + 1) * 128],
                            wdT[:, ic * H + dcol * 512: ic * H + (dcol + 1) * 512],
                            start=(ic == 0), stop=(ic == 23)))
                p.wait("tensor", "ld_a%d" % sl4, av)
                p.gwait("tensor", "patt%d" % sl)
                for k in range(8):
                    p.TE(lambda e, k=k, sl=sl, sl4=sl4: e.transpose(
                        patt[:, sl * 1024 + k * 128: sl * 1024 + (k + 1) * 128],
                        attin[:, sl4 * 1024 + k * 128: sl4 * 1024 + (k + 1) * 128], ident[:]))
                p.gwait("tensor", "rcol%d" % sl)
                p.TE(lambda e, tr=tr, sl=sl: e.matmul(
                    pcol[:, 0:1], rrow[0:1, tr * 128:(tr + 1) * 128],
                    onesf[0:1, 0:1], start=True, stop=True))
                tv = p.sig("tensor", "tt")
                p.gset("oqT%d" % ssl, "tt", tv)
                p.wait("scalar", "tt", tv)
                p.S(lambda e, sl=sl: e.copy(rcol1[:, sl:sl + 1], pcol[:, 0:1]))
                sv = p.sig("scalar", "sc")
                p.gset("rrow", "sc", sv)
                p.wait("vector", "sc", sv)
                p.wait("vector", "tt", tv)
                p.wait("vector", "ld_xr%d" % sl4, xv)
                p.gwait("vector", "outb%d" % sl)
                p.V(lambda e, sl=sl, sl4=sl4: e.scalar_tensor_tensor(
                    accb[:, sl * 1024:(sl + 1) * 1024], patt[:, sl * 1024:(sl + 1) * 1024],
                    rcol1[:, sl:sl + 1],
                    xres[:, sl4 * 1024:(sl4 + 1) * 1024], ALU.mult, ALU.add), drain=True)
                p.V(lambda e, sl=sl: e.tensor_tensor(
                    outb[:, sl * 1024:(sl + 1) * 1024], accb[:, sl * 1024:(sl + 1) * 1024],
                    pdown[:, sl * 1024:(sl + 1) * 1024], ALU.add))
                vo = p.sig("vector", "vv")
                p.gset("patt%d" % sl, "vv", vo)
                p.gset("pdown%d" % sl, "vv", vo)
                p.gset("attin%d" % sl4, "vv", vo)
                p.gset("xres%d" % sl4, "vv", vo)
                p.gset("rcol%d" % sl, "vv", vo)
                pend_store.append((vo, b, tr, sl))
                if len(pend_store) > 1:
                    q_vo, q_b, q_tr, q_sl = pend_store.pop(0)
                    p.wait("scalar", "vv", q_vo)
                    q_ov = p.dma("scalar", d["out_t"].ap()[q_b, q_tr * 128:(q_tr + 1) * 128, :],
                                 outb[:, q_sl * 1024:(q_sl + 1) * 1024], "ld_o%d" % q_sl)
                    p.gset("outb%d" % q_sl, "ld_o%d" % q_sl, q_ov)

    for q_vo, q_b, q_tr, q_sl in pend_store:
        p.wait("scalar", "vv", q_vo)
        q_ov = p.dma("scalar", d["out_t"].ap()[q_b, q_tr * 128:(q_tr + 1) * 128, :],
                     outb[:, q_sl * 1024:(q_sl + 1) * 1024], "ld_o%d" % q_sl)
        p.gset("outb%d" % q_sl, "ld_o%d" % q_sl, q_ov)
    p.dma_finalize()
    p.barrier()
    stD2.close()
    stD.close()
    stP.close()



def kernel(L0_attn_X, L0_attn_W_Q, L0_attn_W_K, L0_attn_W_V,
           L0_ffn_w_gate, L0_ffn_w_up, L0_ffn_w_down):
    import ml_dtypes
    perm = np.concatenate([np.arange(0, H, 2), np.arange(1, H, 2)])
    wq = np.ascontiguousarray(np.asarray(L0_attn_W_Q, np.float32)[:, perm])
    wk = np.ascontiguousarray(np.asarray(L0_attn_W_K, np.float32)[:, perm])
    wv = np.ascontiguousarray(np.asarray(L0_attn_W_V, np.float32))
    wu = np.ascontiguousarray(np.asarray(L0_ffn_w_up, np.float32))
    wg = np.ascontiguousarray(np.asarray(L0_ffn_w_gate, np.float32))
    wd = np.ascontiguousarray(np.asarray(L0_ffn_w_down, np.float32))
    X = np.asarray(L0_attn_X, np.float32)
    ident = np.eye(128, dtype=ml_dtypes.bfloat16)

    nc = build_nc()
    in_maps = []
    for c in range(8):
        in_maps.append({"x": np.ascontiguousarray(X[2 * c:2 * c + 2]),
                        "wq": wq, "wk": wk, "wv": wv,
                        "wu": wu, "wg": wg, "wd": wd, "ident": ident})
    res = run_bass_kernel_spmd(nc, in_maps, core_ids=list(range(8)))
    out = np.concatenate([r["out"] for r in res.results], axis=0)
    return out.astype(np.float32)
